# version 16 (final)
# baseline (speedup 1.0000x reference)
import numpy as np
import ml_dtypes
import concourse.bass as bass
import concourse.mybir as mybir
from concourse.bass_utils import run_bass_kernel_spmd

F32 = mybir.dt.float32
BF16 = mybir.dt.bfloat16
I32 = mybir.dt.int32
AF = mybir.ActivationFunctionType
ALU = mybir.AluOpType

B, S, D, H, HD, HID = 2, 8192, 1024, 16, 64, 4096
NCORES = 8
NPAIR = 8
QB = 512
NM = 4
TRUNC = 160.0

SLOPES = [2.0 ** (-0.5 * (h + 1)) for h in range(H)]
KB = [min(S // 128, int((TRUNC / SLOPES[2 * p + 1]) // 128) + 1) for p in range(NPAIR)]
TMP = [[min(16 * m + 16, KB[p]) for p in range(NPAIR)] for m in range(NM)]
NSLOT = sum(sum(t) for t in TMP)
KV_TILE = 2 * 128 * 128


SLOTS = []
for _m in range(NM):
    for _p in range(NPAIR):
        for _t in range(TMP[_m][_p]):
            SLOTS.append((len(SLOTS), _m, _p, _t, TMP[_m][_p], _t >= 16 * _m))
_c = 0
MSK_CNT = []
for _s in SLOTS:
    _c += 1 if _s[5] else 0
    MSK_CNT.append(_c)
MSK_MAX_OF_MP = {}
for _s in SLOTS:
    if _s[5]:
        MSK_MAX_OF_MP[(_s[1], _s[2])] = MSK_CNT[_s[0]]


def _host_tables(j):
    kvidx = np.zeros((128, NSLOT), dtype=np.int32)
    midx = np.zeros((128, NSLOT), dtype=np.int32)
    biasseq = np.empty((128, 2 * NSLOT), dtype=np.float32)
    rows = np.arange(128, dtype=np.float64)
    ri = np.arange(128, dtype=np.int32)
    for (s, m, p, t, T, masked) in SLOTS:
        dstart = 16 * m + 4 * j
        live = min(KB[p], dstart + 4)
        if t < live:
            kt = t
            kvidx[:, s] = (p * 64 + kt) * 128 + ri
            for hs, h in enumerate((2 * p, 2 * p + 1)):
                biasseq[:, 2 * s + hs] = (-SLOPES[h] * (128 * kt + rows)).astype(np.float32)
            pat = (kt - dstart) if kt >= dstart else 4
        else:
            kvidx[:, s] = ri
            biasseq[:, 2 * s] = -1e30
            biasseq[:, 2 * s + 1] = -1e30
            pat = 4
        midx[:, s] = pat * 128 + ri
    return kvidx, midx, biasseq


NMASKED = MSK_CNT[-1]


def _host_maskseq(j):
    pats = _host_masks().astype(np.float32)
    seq = np.empty((NMASKED, 128, QB), dtype=np.float32)
    for (s, m, p, t, T, masked) in SLOTS:
        if not masked:
            continue
        dstart = 16 * m + 4 * j
        live = min(KB[p], dstart + 4)
        pat = ((t - dstart) if t >= dstart else 4) if t < live else 4
        seq[MSK_CNT[s] - 1] = pats[pat * 128:(pat + 1) * 128, :]
    return np.ascontiguousarray(seq.astype(ml_dtypes.bfloat16))


def _host_masks():
    pats = np.zeros((5 * 128, QB), dtype=np.float32)
    cols = np.arange(QB)[None, :]
    r = np.arange(128)[:, None]
    for d in range(4):
        pats[d * 128:(d + 1) * 128] = (cols >= 128 * d + r)
    pats[4 * 128:] = 1.0
    return np.ascontiguousarray(pats.astype(ml_dtypes.bfloat16))


from concourse.bass_core_rust import IndirectOffsetOnAxis


class SemC:
    REGISTRY = []

    def __init__(self, es, nc, name):
        self.h = es.enter_context(nc.semaphore(name))
        self.n = 0
        SemC.REGISTRY.append(self)

    def inc(self, inst, k=1):
        inst.then_inc(self.h, k)
        self.n += k
        return self.n


def build_nc(debug=False):
    from contextlib import ExitStack
    nc = bass.Bass("TRN2", target_bir_lowering=False)
    es = ExitStack()

    x_full = nc.dram_tensor("x_full", [S, D], F32, kind="ExternalInput")
    x_own = nc.dram_tensor("x_own", [NM * QB, D], F32, kind="ExternalInput")
    wqkvT = nc.dram_tensor("wqkvT", [D, 3 * D], F32, kind="ExternalInput")
    woT = nc.dram_tensor("woT", [D, D], F32, kind="ExternalInput")
    w1blk = nc.dram_tensor("w1blk", [32, D, 128], F32, kind="ExternalInput")
    w2blk = nc.dram_tensor("w2blk", [32, D, 128], F32, kind="ExternalInput")
    h1blk = nc.dram_tensor("h1blk", [32, HID, 128], F32, kind="ExternalInput")
    h2blk = nc.dram_tensor("h2blk", [32, HID, 128], F32, kind="ExternalInput")
    w3blk = nc.dram_tensor("w3blk", [8, HID, 128], F32, kind="ExternalInput")
    s1m = nc.dram_tensor("s1m", [128, 32], F32, kind="ExternalInput")
    s2m = nc.dram_tensor("s2m", [128, 32], F32, kind="ExternalInput")
    h1sm = nc.dram_tensor("h1sm", [128, 32], F32, kind="ExternalInput")
    h2sm = nc.dram_tensor("h2sm", [128, 32], F32, kind="ExternalInput")
    w3sm = nc.dram_tensor("w3sm", [128, 8], F32, kind="ExternalInput")
    biasseq_in = nc.dram_tensor("biasseq", [128, 2 * NSLOT], F32, kind="ExternalInput")
    kvidx_in = nc.dram_tensor("kvidx", [128, NSLOT], I32, kind="ExternalInput")
    midx_in = nc.dram_tensor("midx", [128, NSLOT], I32, kind="ExternalInput")
    maskseq_in = nc.dram_tensor("maskseq", [NMASKED, 128, QB], BF16, kind="ExternalInput")
    ident_in = nc.dram_tensor("ident", [128, 128], F32, kind="ExternalInput")
    out = nc.dram_tensor("out", [NM * QB, D], F32, kind="ExternalOutput")

    kv_scr = nc.dram_tensor("kv_scr", [NPAIR * 64 * 128, 256], BF16)
    flush_scr = nc.dram_tensor("flush_scr", [128, 2 * 256 + 2 * QB], BF16)
    x2_scr = nc.dram_tensor("x2_scr", [NM * QB, D], F32)
    b_scr = nc.dram_tensor("b_scr", [HID, NM * QB], BF16)

    dbg = {}
    if debug:
        dbg["qt"] = nc.dram_tensor("dbg_qt", [128, 8 * 2048], BF16, kind="ExternalOutput")
        dbg["xnt"] = nc.dram_tensor("dbg_xnt", [128, 8 * 2048], BF16, kind="ExternalOutput")
        dbg["kv0"] = nc.dram_tensor("dbg_kv0", [128, 256], BF16, kind="ExternalOutput")
        dbg["u"] = nc.dram_tensor("dbg_u", [NM, 128, 8 * QB], BF16, kind="ExternalOutput")
        dbg["x2"] = nc.dram_tensor("dbg_x2", [NM, 128, 4 * 1024], F32, kind="ExternalOutput")
        dbg["a"] = nc.dram_tensor("dbg_a", [128, 32 * 2048], BF16, kind="ExternalOutput")
        dbg["dbc"] = nc.dram_tensor("dbg_dbc", [128, QB], F32, kind="ExternalOutput")
        dbg["upr"] = nc.dram_tensor("dbg_upr", [128, QB], F32, kind="ExternalOutput")
        dbg["pt"] = nc.dram_tensor("dbg_pt", [3, 128, 2 * QB], BF16, kind="ExternalOutput")
        dbg["mk"] = nc.dram_tensor("dbg_mk", [3, 128, QB], BF16, kind="ExternalOutput")
        dbg["tbl"] = nc.dram_tensor("dbg_tbl", [128, 16], I32, kind="ExternalOutput")
        dbg["sps"] = nc.dram_tensor("dbg_sps", [128, 1024], F32, kind="ExternalOutput")
        dbg["qt2"] = nc.dram_tensor("dbg_qt2", [128, 8 * 2048], BF16, kind="ExternalOutput")
        dbg["xblk0"] = nc.dram_tensor("dbg_xblk0", [128, 4 * 1024], F32, kind="ExternalOutput")
        dbg["xntok0"] = nc.dram_tensor("dbg_xntok0", [128, 4 * 1024], BF16, kind="ExternalOutput")
        dbg["bsq"] = nc.dram_tensor("dbg_bsq", [128, 8], F32, kind="ExternalOutput")

    def sbp(name, shape, dt):
        return es.enter_context(nc.sbuf_tensor(name, shape, dt))

    ones64 = sbp("ones64", [128, 64], F32)
    ones64b = sbp("ones64b", [128, 64], BF16)
    ident_sb = sbp("ident_sb", [128, 128], F32)
    s1_sb = sbp("s1_sb", [128, 32], F32)
    s2_sb = sbp("s2_sb", [128, 32], F32)
    h1s_sb = sbp("h1s_sb", [128, 32], F32)
    h2s_sb = sbp("h2s_sb", [128, 32], F32)
    w3s_sb = sbp("w3s_sb", [128, 8], F32)
    st_scr = sbp("st_scr", [128, 1024], F32)
    st_scr2 = sbp("st_scr2", [128, 1024], F32)
    ident_b = sbp("ident_b", [128, 128], BF16)
    st_small = sbp("st_small", [128, 64], F32)
    xn2T_sb = sbp("xn2T_sb", [128, 8 * 2048], BF16)
    STATS = {
        "ssum": st_small[:, 0:8], "ssq": st_small[:, 8:16],
        "mean": st_small[:, 16:24], "var": st_small[:, 24:32],
        "m2": st_small[:, 32:40], "rstd": st_small[:, 40:48],
        "nb": st_small[:, 48:56], "lnv": st_small[:, 56:64],
    }

    SemC.REGISTRY = []
    sS = SemC(es, nc, "sS")
    sA = SemC(es, nc, "sA")
    sG = SemC(es, nc, "sG")
    mPE = SemC(es, nc, "mPE")
    mPE2 = SemC(es, nc, "mPE2")
    mACT = SemC(es, nc, "mACT")
    mDVE = SemC(es, nc, "mDVE")
    mDVE2 = SemC(es, nc, "mDVE2")
    KVL = SemC(es, nc, "KVL")
    MKL = SemC(es, nc, "MKL")
    SMM = SemC(es, nc, "SMM")
    EXD = SemC(es, nc, "EXD")
    MSK = SemC(es, nc, "MSK")
    AVD = SemC(es, nc, "AVD")
    UEV = SemC(es, nc, "UEV")
    WOM = SemC(es, nc, "WOM")
    X2D = SemC(es, nc, "X2D")
    FLS = SemC(es, nc, "FLS")
    FMM = SemC(es, nc, "FMM")
    TS1 = SemC(es, nc, "TS1")
    ADN = SemC(es, nc, "ADN")
    BLD = SemC(es, nc, "BLD")
    CMM = SemC(es, nc, "CMM")
    CGL = SemC(es, nc, "CGL")
    DM2 = SemC(es, nc, "DM2")
    YEV = SemC(es, nc, "YEV")
    TRD = SemC(es, nc, "TRD")
    OAD = SemC(es, nc, "OAD")
    X2L = SemC(es, nc, "X2L")
    ODM = SemC(es, nc, "ODM")

    def emit_ln(blk, nt, xblk, xn_tok, ready_sem, ready_val,
                apply_sem=None, apply_val=0, apply_sem2=None, apply_val2=0,
                warm=0, post=0):
        def _act1(act):
            act.wait_ge(ready_sem.h, ready_val)
            for _ in range(warm):
                act.activation(st_scr[:, :], st_scr[:, :], AF.Square)
            last = None
            for t in range(nt):
                last = act.activation(st_scr[:, :], xblk[:, t * 1024:(t + 1) * 1024],
                                      AF.Square, accum_out=STATS["ssq"][:, t:t + 1])
            mACT.inc(last)
        blk.scalar(_act1)
        act_sq = mACT.n

        def _dve1(vec):
            vec.wait_ge(ready_sem.h, ready_val)
            for _ in range(warm):
                vec.tensor_scalar_mul(st_scr2[:, :], st_scr2[:, :], 1.0)
            for t in range(nt):
                vec.tensor_scalar(st_scr2[:, :], xblk[:, t * 1024:(t + 1) * 1024],
                                  1.0, 0.0, ALU.mult, ALU.add,
                                  accum_out=STATS["ssum"][:, t:t + 1])
            vec.wait_ge(mACT.h, act_sq)
            mDVE.inc(vec.tensor_scalar_mul(STATS["mean"][:, 0:nt], STATS["ssum"][:, 0:nt], 1.0 / D))
            vec.wait_ge(mDVE.h, mDVE.n)
            vec.tensor_scalar_mul(STATS["var"][:, 0:nt], STATS["ssq"][:, 0:nt], 1.0 / D)
            mDVE.inc(vec.tensor_mul(STATS["m2"][:, 0:nt], STATS["mean"][:, 0:nt], STATS["mean"][:, 0:nt]))
            vec.wait_ge(mDVE.h, mDVE.n)
            mDVE.inc(vec.tensor_sub(STATS["var"][:, 0:nt], STATS["var"][:, 0:nt], STATS["m2"][:, 0:nt]))
            vec.wait_ge(mDVE.h, mDVE.n)
            mDVE.inc(vec.tensor_scalar_add(STATS["var"][:, 0:nt], STATS["var"][:, 0:nt], 1e-5))
        blk.vector(_dve1)
        dve_var = mDVE.n

        def _act2(act):
            act.wait_ge(mDVE.h, dve_var)
            mACT.inc(act.activation(STATS["lnv"][:, 0:nt], STATS["var"][:, 0:nt], AF.Ln))
            act.wait_ge(mACT.h, mACT.n)
            mACT.inc(act.activation(STATS["rstd"][:, 0:nt], STATS["lnv"][:, 0:nt],
                                    AF.Exp, scale=-0.5))
        blk.scalar(_act2)
        act_rstd = mACT.n

        def _dve2(vec):
            vec.wait_ge(mACT.h, act_rstd)
            mDVE.inc(vec.tensor_mul(STATS["nb"][:, 0:nt], STATS["mean"][:, 0:nt], STATS["rstd"][:, 0:nt]))
            vec.wait_ge(mDVE.h, mDVE.n)
            mDVE.inc(vec.tensor_scalar_mul(STATS["nb"][:, 0:nt], STATS["nb"][:, 0:nt], -1.0))
        blk.vector(_dve2)
        dve_nb = mDVE.n

        def _act3(act):
            act.wait_ge(mDVE.h, dve_nb)
            if apply_sem is not None:
                act.wait_ge(apply_sem.h, apply_val)
            if apply_sem2 is not None:
                act.wait_ge(apply_sem2.h, apply_val2)
            last = None
            for t in range(nt):
                last = act.activation(xn_tok[:, t * 1024:(t + 1) * 1024],
                                      xblk[:, t * 1024:(t + 1) * 1024], AF.Identity,
                                      bias=STATS["nb"][:, t:t + 1],
                                      scale=STATS["rstd"][:, t:t + 1])
            for t in range(post):
                last = act.activation(st_scr[:, 0:1024],
                                      xn_tok[:, t * 1024:(t + 1) * 1024], AF.Square)
            mACT.inc(last)
        blk.scalar(_act3)
        return mACT.n

    def emit_ln2(blk, xa, xb, xna, xnb, ra_sem, ra_val, rb_val,
                 apply_sem=None, apply_vala=0, apply_valb=0, warm=0):
        def _act1(act):
            act.wait_ge(ra_sem.h, ra_val)
            for _ in range(warm):
                act.activation(st_scr[:, :], st_scr[:, :], AF.Square)
            last = None
            for t in range(4):
                last = act.activation(st_scr[:, :], xa[:, t * 1024:(t + 1) * 1024],
                                      AF.Square, accum_out=STATS["ssq"][:, t:t + 1])
            act.wait_ge(ra_sem.h, rb_val)
            for t in range(4):
                last = act.activation(st_scr[:, :], xb[:, t * 1024:(t + 1) * 1024],
                                      AF.Square, accum_out=STATS["ssq"][:, 4 + t:5 + t])
            mACT.inc(last)
        blk.scalar(_act1)
        act_sq = mACT.n

        def _dve1(vec):
            vec.wait_ge(ra_sem.h, ra_val)
            for _ in range(warm):
                vec.tensor_scalar_mul(st_scr2[:, :], st_scr2[:, :], 1.0)
            for t in range(4):
                vec.tensor_scalar(st_scr2[:, :], xa[:, t * 1024:(t + 1) * 1024],
                                  1.0, 0.0, ALU.mult, ALU.add,
                                  accum_out=STATS["ssum"][:, t:t + 1])
            vec.wait_ge(ra_sem.h, rb_val)
            for t in range(4):
                vec.tensor_scalar(st_scr2[:, :], xb[:, t * 1024:(t + 1) * 1024],
                                  1.0, 0.0, ALU.mult, ALU.add,
                                  accum_out=STATS["ssum"][:, 4 + t:5 + t])
            vec.wait_ge(mACT.h, act_sq)
            mDVE.inc(vec.tensor_scalar_mul(STATS["mean"][:, 0:8], STATS["ssum"][:, 0:8], 1.0 / D))
            vec.wait_ge(mDVE.h, mDVE.n)
            vec.tensor_scalar_mul(STATS["var"][:, 0:8], STATS["ssq"][:, 0:8], 1.0 / D)
            mDVE.inc(vec.tensor_mul(STATS["m2"][:, 0:8], STATS["mean"][:, 0:8], STATS["mean"][:, 0:8]))
            vec.wait_ge(mDVE.h, mDVE.n)
            mDVE.inc(vec.tensor_sub(STATS["var"][:, 0:8], STATS["var"][:, 0:8], STATS["m2"][:, 0:8]))
            vec.wait_ge(mDVE.h, mDVE.n)
            mDVE.inc(vec.tensor_scalar_add(STATS["var"][:, 0:8], STATS["var"][:, 0:8], 1e-5))
        blk.vector(_dve1)
        dve_var = mDVE.n

        def _act2(act):
            act.wait_ge(mDVE.h, dve_var)
            mACT.inc(act.activation(STATS["lnv"][:, 0:8], STATS["var"][:, 0:8], AF.Ln))
            act.wait_ge(mACT.h, mACT.n)
            mACT.inc(act.activation(STATS["rstd"][:, 0:8], STATS["lnv"][:, 0:8],
                                    AF.Exp, scale=-0.5))
        blk.scalar(_act2)
        act_rstd = mACT.n

        def _dve2(vec):
            vec.wait_ge(mACT.h, act_rstd)
            mDVE.inc(vec.tensor_mul(STATS["nb"][:, 0:8], STATS["mean"][:, 0:8], STATS["rstd"][:, 0:8]))
            vec.wait_ge(mDVE.h, mDVE.n)
            mDVE.inc(vec.tensor_scalar_mul(STATS["nb"][:, 0:8], STATS["nb"][:, 0:8], -1.0))
        blk.vector(_dve2)
        dve_nb = mDVE.n

        def _act3(act):
            act.wait_ge(mDVE.h, dve_nb)
            if apply_sem is not None:
                act.wait_ge(apply_sem.h, apply_vala)
            last = None
            for t in range(4):
                last = act.activation(xna[:, t * 1024:(t + 1) * 1024],
                                      xa[:, t * 1024:(t + 1) * 1024], AF.Identity,
                                      bias=STATS["nb"][:, t:t + 1],
                                      scale=STATS["rstd"][:, t:t + 1])
            mACT.inc(last)
        blk.scalar(_act3)
        a_done = mACT.n

        def _act4(act):
            if apply_sem is not None:
                act.wait_ge(apply_sem.h, apply_valb)
            last = None
            for t in range(4):
                last = act.activation(xnb[:, t * 1024:(t + 1) * 1024],
                                      xb[:, t * 1024:(t + 1) * 1024], AF.Identity,
                                      bias=STATS["nb"][:, 4 + t:5 + t],
                                      scale=STATS["rstd"][:, 4 + t:5 + t])
            mACT.inc(last)
        blk.scalar(_act4)
        return a_done, mACT.n

    with nc.Block("p0") as blk:
        def _g(g):
            for sc in SemC.REGISTRY:
                g.sem_clear(sc.h)
            g.memset(ones64[:, :], 1.0)
            g.memset(ones64b[:, :], 1.0)
            for dst, src in ((ident_sb, ident_in), (s1_sb, s1m), (s2_sb, s2m),
                             (h1s_sb, h1sm), (h2s_sb, h2sm), (w3s_sb, w3sm)):
                g.dma_start(out=dst[:, :], in_=src.ap()).then_inc(sG.h, 16)
                sG.n += 16
            g.wait_ge(sG.h, sG.n)
        blk.gpsimd(_g)

        p0_g = sG.n

        def _cast(act):
            act.wait_ge(sG.h, p0_g)
            act.activation(ident_b[:, :], ident_sb[:, :], AF.Copy)
        blk.scalar(_cast)

    qts = ExitStack()
    qt_sb = qts.enter_context(nc.sbuf_tensor("qt_sb", [128, 8 * 2048], BF16))

    p1 = ExitStack()
    xblk = (p1.enter_context(nc.sbuf_tensor("xblkA", [128, 4 * 1024], F32)),
            p1.enter_context(nc.sbuf_tensor("xblkB", [128, 4 * 1024], F32)))
    xn_tok = (p1.enter_context(nc.sbuf_tensor("xn_tokA", [128, 4 * 1024], BF16)),
              p1.enter_context(nc.sbuf_tensor("xn_tokB", [128, 4 * 1024], BF16)))
    wqkvT_sb = p1.enter_context(nc.sbuf_tensor("wqkvT_sb", [128, 8 * 3072], BF16))

    sub1a = ExitStack()
    xn_ownT = sub1a.enter_context(nc.sbuf_tensor("xn_ownT", [128, 8 * 2048], BF16))

    with nc.Block("p1a") as blk, \
            nc.psum_tensor("tpA", [128, 1024], F32) as tpA, \
            nc.psum_tensor("tpB", [128, 1024], F32) as tpB:
        def _g(g):
            g.dma_start(
                out=bass.AP(wqkvT_sb, 0, [[8 * 3072, 128], [3072, 8], [1, 3072]]),
                in_=bass.AP(wqkvT, 0, [[3072, 128], [128 * 3072, 8], [1, 3072]])
            ).then_inc(sG.h, 16)
            sG.n += 16
        blk.gpsimd(_g)

        tps = (tpA, tpB)
        ld_hist = []
        apply_hist = []
        tpe_base = mPE.n
        tdve_base = mDVE2.n

        def _ld(sy, g_):
            if g_ >= 2:
                sy.wait_ge(mACT.h, apply_hist[g_ - 2])
            sy.dma_start(
                out=bass.AP(xblk[g_ % 2], 0, [[4 * 1024, 128], [1024, 4], [1, 1024]]),
                in_=bass.AP(x_own, g_ * 512 * D, [[1024, 128], [128 * 1024, 4], [1, 1024]])
            ).then_inc(sS.h, 16)
            sS.n += 16

        blk.sync(lambda sy: _ld(sy, 0))
        ld_hist.append(sS.n)
        blk.sync(lambda sy: _ld(sy, 1))
        ld_hist.append(sS.n)

        for pr in range(2):
            ga_, gb_ = 2 * pr, 2 * pr + 1
            ap_pair = emit_ln2(
                blk, xblk[ga_ % 2], xblk[gb_ % 2], xn_tok[ga_ % 2], xn_tok[gb_ % 2],
                sS, ld_hist[ga_], ld_hist[gb_],
                apply_sem=(mPE if ga_ >= 2 else None),
                apply_vala=(tpe_base + 4 * (ga_ - 2) + 4 if ga_ >= 2 else 0),
                apply_valb=(tpe_base + 4 * (gb_ - 2) + 4 if gb_ >= 2 else 0),
                warm=4)
            apply_hist.append(ap_pair[0])
            apply_hist.append(ap_pair[1])
            for g_ in (ga_, gb_):
                apply_at = ap_pair[g_ % 2]
                if debug and g_ == 0:
                    def _gdump(g, ld0=ld_hist[0], ap0=apply_at):
                        g.wait_ge(sS.h, ld0)
                        g.dma_start(out=dbg["xblk0"].ap(), in_=xblk[0][:, :]).then_inc(sG.h, 16)
                        g.wait_ge(mACT.h, ap0)
                        g.dma_start(out=dbg["xntok0"].ap(), in_=xn_tok[0][:, :]).then_inc(sG.h, 16)
                        sG.n += 32
                    blk.gpsimd(_gdump)

                def _tpe(t, g_=g_, apply_at=apply_at):
                    t.wait_ge(mACT.h, apply_at)
                    for tt in range(4):
                        idx = g_ * 4 + tt
                        if idx >= 2:
                            t.wait_ge(mDVE2.h, tdve_base + idx - 1)
                        ps = tps[idx % 2]
                        last = None
                        for fc in range(8):
                            last = t.matmul(ps[:, fc * 128:(fc + 1) * 128],
                                            xn_tok[g_ % 2][:, tt * 1024 + fc * 128:
                                                           tt * 1024 + fc * 128 + 128],
                                            ident_b[:, :], start=True, stop=True)
                        mPE.inc(last)
                blk.tensor(_tpe)
                assert mPE.n == tpe_base + 4 * g_ + 4

                def _tev(v, g_=g_):
                    for tt in range(4):
                        idx = g_ * 4 + tt
                        v.wait_ge(mPE.h, tpe_base + idx + 1)
                        ps = tps[idx % 2]
                        last = None
                        for fc in range(8):
                            last = v.tensor_copy(
                                xn_ownT[:, fc * 2048 + g_ * 512 + tt * 128:
                                        fc * 2048 + g_ * 512 + tt * 128 + 128],
                                ps[:, fc * 128:(fc + 1) * 128])
                        mDVE2.inc(last)
                blk.vector(_tev)
                assert mDVE2.n == tdve_base + 4 * g_ + 4
                if g_ + 2 < 4:
                    blk.sync(lambda sy, g_=g_: _ld(sy, g_ + 2))
                    ld_hist.append(sS.n)

    with nc.Block("p1q") as blk, \
            nc.psum_tensor("qps0", [128, 2048], F32) as qps0, \
            nc.psum_tensor("qps1", [128, 2048], F32) as qps1:
        qps = (qps0, qps1)
        pe_base = mPE.n
        dve_base = mDVE.n
        g_need = sG.n
        xnt_need = mDVE2.n

        def _pe(t):
            t.wait_ge(mDVE2.h, xnt_need)
            t.wait_ge(sG.h, g_need)
            for p in range(NPAIR):
                if p >= 2:
                    t.wait_ge(mDVE.h, dve_base + p - 1)
                ps = qps[p % 2]
                last = None
                for ic in range(8):
                    lhsT = wqkvT_sb[:, ic * 3072 + p * 128: ic * 3072 + p * 128 + 128]
                    for qn in range(4):
                        last = t.matmul(ps[:, qn * 512:(qn + 1) * 512], lhsT,
                                        xn_ownT[:, ic * 2048 + qn * 512:
                                                ic * 2048 + (qn + 1) * 512],
                                        start=(ic == 0), stop=(ic == 7))
                mPE.inc(last)
        blk.tensor(_pe)

        def _dv(v):
            for p in range(NPAIR):
                v.wait_ge(mPE.h, pe_base + p + 1)
                mDVE.inc(v.tensor_copy(qt_sb[:, p * 2048:(p + 1) * 2048], qps[p % 2][:, :]))
        blk.vector(_dv)

        if debug:
            dv_need = mDVE.n

            def _dbg(sy):
                sy.wait_ge(mDVE.h, dv_need)
                sy.dma_start(out=dbg["qt"].ap(), in_=qt_sb[:, :]).then_inc(sS.h, 16)
                sy.dma_start(out=dbg["xnt"].ap(), in_=xn_ownT[:, :]).then_inc(sS.h, 16)
                sS.n += 32
                sy.wait_ge(sS.h, sS.n)
            blk.sync(_dbg)

    sub1a.close()

    sub1b = ExitStack()
    xnT = (sub1b.enter_context(nc.sbuf_tensor("xnT_A", [128, 8 * 1024], BF16)),
           sub1b.enter_context(nc.sbuf_tensor("xnT_B", [128, 8 * 1024], BF16)))
    kv_ev = (sub1b.enter_context(nc.sbuf_tensor("kv_evA", [128, 1024], BF16)),
             sub1b.enter_context(nc.sbuf_tensor("kv_evB", [128, 1024], BF16)))
    KEPT = [[p for p in range(NPAIR) if KB[p] > 8 * bb] for bb in range(8)]

    with nc.Block("p1b") as blk, \
            nc.psum_tensor("ktps", [128, 1024], F32) as ktps, \
            nc.psum_tensor("vtps0", [128, 1024], F32) as vtps0, \
            nc.psum_tensor("vtps1", [128, 1024], F32) as vtps1, \
            nc.psum_tensor("trp0", [128, 512], F32) as trp0, \
            nc.psum_tensor("trp1", [128, 512], F32) as trp1:
        vtps = (vtps0, vtps1)
        trps = (trp0, trp1)
        ld_hist = []
        apply_hist = []
        kev_vals = []
        vev_vals = []
        trgrp_evac = []
        trgrp_pe = []
        trpe_done = []
        trev_done = []
        _n = mDVE2.n
        _p = mPE2.n
        for _bb in range(8):
            for _h in range(2):
                for _ in range(8):
                    _n += 1
                    trgrp_evac.append(_n)
                    _p += 1
                    trgrp_pe.append(_p)
                trev_done.append(_n)
                trpe_done.append(_p)
            for _ in KEPT[_bb]:
                _n += 1
                kev_vals.append(_n)
                _p += 1
            for _ in range(8):
                _n += 1
                vev_vals.append(_n)
                _p += 1
        evac_n = 0
        gk_count = [0]
        gv_count = [0]
        evac_wr = {}
        _sg = sG.n
        _ei = 0
        for _bb in range(8):
            for _ in KEPT[_bb]:
                _sg += 16
                evac_wr[_ei] = _sg
                _ei += 1
            for _ in range(8):
                _sg += 16 * len(KEPT[_bb])
                evac_wr[_ei] = _sg
                _ei += 1

        def _ld(sy, gg):
            if gg >= 2:
                sy.wait_ge(mACT.h, apply_hist[gg - 2])
            sy.dma_start(
                out=bass.AP(xblk[gg % 2], 0, [[4 * 1024, 128], [1024, 4], [1, 1024]]),
                in_=bass.AP(x_full, gg * 512 * D, [[1024, 128], [128 * 1024, 4], [1, 1024]])
            ).then_inc(sS.h, 16)
            sS.n += 16

        blk.sync(lambda sy: _ld(sy, 0))
        ld_hist.append(sS.n)
        blk.sync(lambda sy: _ld(sy, 1))
        ld_hist.append(sS.n)

        for bb in range(8):
            ga, gb = 2 * bb, 2 * bb + 1
            ap_pair = emit_ln2(
                blk, xblk[ga % 2], xblk[gb % 2], xn_tok[ga % 2], xn_tok[gb % 2],
                sS, ld_hist[ga], ld_hist[gb],
                apply_sem=(mPE2 if ga >= 2 else None),
                apply_vala=(trpe_done[ga - 2] if ga >= 2 else 0),
                apply_valb=(trpe_done[gb - 2] if gb >= 2 else 0))
            apply_hist.append(ap_pair[0])
            apply_hist.append(ap_pair[1])
            for half in range(2):
                gg = 2 * bb + half
                apply_at = ap_pair[half]

                def _trpe(t, gg=gg, apply_at=apply_at):
                    t.wait_ge(mACT.h, apply_at)
                    for grp in range(8):
                        g = gg * 8 + grp
                        if g >= 2:
                            t.wait_ge(mDVE2.h, trgrp_evac[g - 2])
                        tt, fch = grp // 2, (grp % 2) * 4
                        ps = trps[g % 2]
                        last = None
                        for k in range(4):
                            fc = fch + k
                            last = t.matmul(ps[:, k * 128:(k + 1) * 128],
                                            xn_tok[gg % 2][:, tt * 1024 + fc * 128:
                                                           tt * 1024 + fc * 128 + 128],
                                            ident_b[:, :], start=True, stop=True)
                        mPE2.inc(last)
                        assert mPE2.n == trgrp_pe[g]
                blk.tensor(_trpe)

                def _trev(v, gg=gg, bb=bb, half=half):
                    for grp in range(8):
                        g = gg * 8 + grp
                        v.wait_ge(mPE2.h, trgrp_pe[g])
                        tt, fch = grp // 2, (grp % 2) * 4
                        ps = trps[g % 2]
                        last = None
                        for k in range(4):
                            fc = fch + k
                            last = v.tensor_copy(
                                xnT[bb % 2][:, fc * 1024 + half * 512 + tt * 128:
                                            fc * 1024 + half * 512 + tt * 128 + 128],
                                ps[:, k * 128:(k + 1) * 128])
                        mDVE2.inc(last)
                        assert mDVE2.n == trgrp_evac[g]
                blk.vector(_trev)

                if gg + 2 < 16:
                    blk.sync(lambda sy, gg=gg: _ld(sy, gg + 2))
                    ld_hist.append(sS.n)

            kept = KEPT[bb]
            pmin, npk = kept[0], len(kept)
            tr_need = trev_done[2 * bb + 1]

            pe2_base = mPE2.n
            dve2k_base = mDVE2.n

            nk_prev = gk_count[0]

            def _pek(t, bb=bb, kept=kept, tr_need=tr_need, nk_prev=nk_prev):
                t.wait_ge(mDVE2.h, tr_need)
                for idx, p in enumerate(kept):
                    gk = nk_prev + idx
                    if gk >= 1:
                        t.wait_ge(mDVE2.h, kev_vals[gk - 1])
                    last = None
                    for ic in range(8):
                        lhsT = wqkvT_sb[:, ic * 3072 + 1024 + p * 128:
                                        ic * 3072 + 1024 + p * 128 + 128]
                        for qn in range(2):
                            last = t.matmul(ktps[:, qn * 512:(qn + 1) * 512], lhsT,
                                            xnT[bb % 2][:, ic * 1024 + qn * 512:
                                                        ic * 1024 + (qn + 1) * 512],
                                            start=(ic == 0), stop=(ic == 7))
                    mPE2.inc(last)
            blk.tensor(_pek)

            ev0 = evac_n

            def _dvk(v, kept=kept, pe2_base=pe2_base, ev0=ev0):
                for idx, p in enumerate(kept):
                    v.wait_ge(mPE2.h, pe2_base + idx + 1)
                    ei = ev0 + idx
                    if ei >= 2:
                        v.wait_ge(sG.h, evac_wr[ei - 2])
                    mDVE2.inc(v.tensor_copy(kv_ev[ei % 2][:, :], ktps[:, :]))
            blk.vector(_dvk)

            def _gwk(g, kept=kept, bb=bb, dve2k_base=dve2k_base, ev0=ev0):
                for idx, p in enumerate(kept):
                    g.wait_ge(mDVE2.h, dve2k_base + idx + 1)
                    ei = ev0 + idx
                    g.dma_start(
                        out=bass.AP(kv_scr, (p * 64 + 8 * bb) * 128 * 256,
                                    [[256, 128], [128 * 256, 8], [1, 128]]),
                        in_=bass.AP(kv_ev[ei % 2], 0, [[1024, 128], [128, 8], [1, 128]])
                    ).then_inc(sG.h, 16)
                    sG.n += 16
                    assert evac_wr[ei] == sG.n
            blk.gpsimd(_gwk)
            evac_n += npk
            gk_count[0] += npk

            pe2v_base = mPE2.n
            dve2v_base = mDVE2.n

            nv_prev = gv_count[0]

            def _pev(t, bb=bb, pmin=pmin, npk=npk, nv_prev=nv_prev):
                for tt in range(8):
                    gv = nv_prev + tt
                    if gv >= 2:
                        t.wait_ge(mDVE2.h, vev_vals[gv - 2])
                    ps = vtps[tt % 2]
                    last = None
                    for ic in range(8):
                        lhsT = xnT[bb % 2][:, ic * 1024 + tt * 128: ic * 1024 + tt * 128 + 128]
                        rem = npk * 128
                        o = 0
                        while rem > 0:
                            n = min(512, rem)
                            last = t.matmul(
                                ps[:, pmin * 128 + o: pmin * 128 + o + n], lhsT,
                                wqkvT_sb[:, ic * 3072 + 2048 + pmin * 128 + o:
                                         ic * 3072 + 2048 + pmin * 128 + o + n],
                                start=(ic == 0), stop=(ic == 7))
                            o += n
                            rem -= n
                    mPE2.inc(last)
            blk.tensor(_pev)

            ev0 = evac_n

            def _dvv(v, pe2v_base=pe2v_base, ev0=ev0, pmin=pmin, npk=npk):
                for tt in range(8):
                    v.wait_ge(mPE2.h, pe2v_base + tt + 1)
                    ei = ev0 + tt
                    if ei >= 2:
                        v.wait_ge(sG.h, evac_wr[ei - 2])
                    mDVE2.inc(v.tensor_copy(
                        kv_ev[ei % 2][:, pmin * 128:(pmin + npk) * 128],
                        vtps[tt % 2][:, pmin * 128:(pmin + npk) * 128]))
            blk.vector(_dvv)

            def _gwv(g, kept=kept, bb=bb, dve2v_base=dve2v_base, ev0=ev0):
                for tt in range(8):
                    g.wait_ge(mDVE2.h, dve2v_base + tt + 1)
                    kt = 8 * bb + tt
                    ei = ev0 + tt
                    for p in kept:
                        g.dma_start(
                            out=bass.AP(kv_scr, (p * 64 + kt) * 128 * 256 + 128,
                                        [[256, 128], [1, 128]]),
                            in_=bass.AP(kv_ev[ei % 2], p * 128, [[1024, 128], [1, 128]])
                        ).then_inc(sG.h, 16)
                        sG.n += 16
                    assert evac_wr[ei] == sG.n
            blk.gpsimd(_gwv)
            evac_n += 8
            gv_count[0] += 8

        blk.gpsimd(lambda g: g.wait_ge(sG.h, sG.n))
        if debug:
            def _dbgkv(sy):
                sy.wait_ge(sG.h, sG.n)
                sy.dma_start(out=kv_ev[0][:, 0:256],
                             in_=bass.AP(kv_scr, 0, [[256, 128], [1, 256]])
                             ).then_inc(sS.h, 16)
                sS.n += 16
                sy.wait_ge(sS.h, sS.n)
                sy.dma_start(out=dbg["kv0"].ap(), in_=kv_ev[0][:, 0:256]).then_inc(sS.h, 16)
                sS.n += 16
                sy.wait_ge(sS.h, sS.n)
            blk.sync(_dbgkv)

    sub1b.close()
    p1.close()

    attn2 = ExitStack()
    biasseq_sb = attn2.enter_context(nc.sbuf_tensor("biasseq_sb", [128, 2 * NSLOT], F32))
    kvidx_sb = attn2.enter_context(nc.sbuf_tensor("kvidx_sb", [128, NSLOT], I32))
    woT_sb = attn2.enter_context(nc.sbuf_tensor("woT_sb", [128, 8 * 1024], BF16))
    kvbuf = tuple(attn2.enter_context(nc.sbuf_tensor(f"kvbuf{i}", [128, 256], BF16))
                  for i in range(8))
    pts = tuple(attn2.enter_context(nc.sbuf_tensor(f"pt{i}", [128, 1024], BF16))
                for i in range(8))
    mkbuf = tuple(attn2.enter_context(nc.sbuf_tensor(f"mk{i}", [128, QB], BF16))
                  for i in range(8))
    dbc = attn2.enter_context(nc.sbuf_tensor("dbc", [128, QB], F32))
    u_sb = attn2.enter_context(nc.sbuf_tensor("u_sb", [128, 8 * QB], BF16))
    xo_buf = (attn2.enter_context(nc.sbuf_tensor("xo0", [128, 1024], F32)),
              attn2.enter_context(nc.sbuf_tensor("xo1", [128, 1024], F32)))
    x2m = attn2.enter_context(nc.sbuf_tensor("x2m", [128, 4 * 1024], F32))
    xn2_tokm = attn2.enter_context(nc.sbuf_tensor("xn2_tokm", [128, 4 * 1024], BF16))

    with nc.Block("p2pre") as blk:
        def _g(g):
            for dst, src_ in ((biasseq_sb, biasseq_in), (kvidx_sb, kvidx_in)):
                g.dma_start(out=dst[:, :], in_=src_.ap()).then_inc(sG.h, 16)
                sG.n += 16
            g.dma_start(
                out=bass.AP(woT_sb, 0, [[8 * 1024, 128], [1024, 8], [1, 1024]]),
                in_=bass.AP(woT, 0, [[1024, 128], [128 * 1024, 8], [1, 1024]])
            ).then_inc(sG.h, 16)
            sG.n += 16
            g.wait_ge(sG.h, sG.n)
        blk.gpsimd(_g)

    xn2_tr_done = (0, 0)
    fls_target = {}

    if debug:
        with nc.Block("p2dbgtbl") as blk:
            def _gd(g):
                g.dma_start(out=dbg["qt2"].ap(), in_=qt_sb[:, :]).then_inc(sG.h, 16)
                sG.n += 16
                g.dma_start(out=dbg["tbl"].ap()[:, 0:8], in_=kvidx_sb[:, 0:8]).then_inc(sG.h, 16)
                g.dma_start(out=dbg["tbl"].ap()[:, 8:16], in_=kvidx_sb[:, 0:8]).then_inc(sG.h, 16)
                g.dma_start(out=dbg["bsq"].ap(), in_=biasseq_sb[:, 0:8]).then_inc(sG.h, 16)
                g.dma_start(out=dbg["sps"].ap()[:, 0:8], in_=biasseq_sb[:, 0:8]).then_inc(sG.h, 16)
                sG.n += 64
                g.wait_ge(sG.h, sG.n)
            blk.gpsimd(_gd)

    for m in range(NM):
        with nc.Block(f"p2m{m}") as blk, \
                nc.psum_tensor(f"sps{m}a", [128, 1024], F32) as sps0, \
                nc.psum_tensor(f"sps{m}b", [128, 1024], F32) as sps1, \
                nc.psum_tensor(f"ups{m}a", [128, QB], F32) as u_ps0, \
                nc.psum_tensor(f"ups{m}b", [128, QB], F32) as u_ps1, \
                nc.psum_tensor(f"dps{m}a", [128, QB], F32) as d_ps0, \
                nc.psum_tensor(f"dps{m}b", [128, QB], F32) as d_ps1:
            s_ps = (sps0, sps1)
            u_pst = (u_ps0, u_ps1)
            d_pst = (d_ps0, d_ps1)

            for p in range(NPAIR):
                pslots = [sl for sl in SLOTS if sl[1] == m and sl[2] == p]
                pair_idx = m * NPAIR + p

                def _gkv(g, pslots=pslots, p=p,
                         dkv=None if not (debug and m == 0 and p == 0) else (sS.n + 16)):
                    for (s, _m, _p, t, T_, masked) in pslots:
                        if s >= 8:
                            g.wait_ge(AVD.h, s - 7)
                        if dkv is not None and s == 2:
                            g.wait_ge(sS.h, dkv)
                        g.indirect_dma_start(
                            kvbuf[s % 8][:, :], None,
                            bass.AP(kv_scr, 0, [[256, 128], [1, 256]]),
                            IndirectOffsetOnAxis(kvidx_sb[:, s:s + 1], 0)
                        ).then_inc(KVL.h, 16)
                        KVL.n += 16
                        if p == 0 and t == 1:
                            g.dma_start(out=bass.AP(flush_scr, 0, [[1536, 128], [1, 256]]),
                                        in_=kvbuf[(s - 1) % 8][:, :]).then_inc(FLS.h, 16)
                            g.dma_start(out=bass.AP(flush_scr, 256, [[1536, 128], [1, 256]]),
                                        in_=kvbuf[s % 8][:, :]).then_inc(FLS.h, 16)
                            FLS.n += 32
                blk.gpsimd(_gkv)

                def _gmsk(sy, pslots=pslots):
                    for (s, _m, _p, t, T_, masked) in pslots:
                        if masked:
                            c = MSK_CNT[s]
                            if c >= 9:
                                sy.wait_ge(MSK.h, c - 8)
                            sy.dma_start(out=mkbuf[c % 8][:, :],
                                         in_=maskseq_in.ap()[c - 1, :, :]
                                         ).then_inc(MKL.h, 16)
                            MKL.n += 16
                blk.sync(_gmsk)
                if p == 0:
                    fls_target[m] = FLS.n

                def _pe(te, pslots=pslots, p=p, m=m, pair_idx=pair_idx):
                    qtp = qt_sb[:, p * 2048 + m * QB: p * 2048 + m * QB + QB]

                    u_ps = u_pst[pair_idx % 2]
                    d_ps = d_pst[pair_idx % 2]

                    def emit_avd(sl):
                        (s, _m, _p, t, T_, masked) = sl
                        if t == 0 and pair_idx >= 2:
                            te.wait_ge(UEV.h, pair_idx - 1)
                        if masked:
                            te.wait_ge(MSK.h, MSK_CNT[s])
                        else:
                            te.wait_ge(EXD.h, s + 1)
                        kb = kvbuf[s % 8]
                        pt = pts[s % 8]
                        st_, sp_ = (t == 0), (t == T_ - 1)
                        te.matmul(u_ps[0:64, :], kb[:, 128:192], pt[:, 0:QB],
                                  start=st_, stop=sp_, tile_position=(0, 0))
                        te.matmul(u_ps[64:128, :], kb[:, 192:256], pt[:, QB:2 * QB],
                                  start=st_, stop=sp_, tile_position=(0, 64))
                        te.matmul(d_ps[0:64, :], ones64b[:, :], pt[:, 0:QB],
                                  start=st_, stop=sp_, tile_position=(0, 0))
                        AVD.inc(te.matmul(d_ps[64:128, :], ones64b[:, :], pt[:, QB:2 * QB],
                                          start=st_, stop=sp_, tile_position=(0, 64)))

                    for idx, sl in enumerate(pslots):
                        (s, _m, _p, t, T_, masked) = sl
                        te.wait_ge(KVL.h, 16 * (s + 1))
                        if idx <= 1 and p == 0:
                            te.wait_ge(FLS.h, fls_target[m])
                        if s >= 2:
                            te.wait_ge(EXD.h, s - 1)
                        kb = kvbuf[s % 8]
                        te.matmul(s_ps[s % 2][:, 0:QB], kb[0:64, 0:128], qtp[0:64, :],
                                  start=True, stop=True, tile_position=(0, 0))
                        SMM.inc(te.matmul(s_ps[s % 2][:, QB:2 * QB], kb[64:128, 0:128],
                                          qtp[64:128, :],
                                          start=True, stop=True, tile_position=(64, 0)))
                        if idx >= 1:
                            emit_avd(pslots[idx - 1])
                    emit_avd(pslots[-1])
                blk.tensor(_pe)

                dump0_val = (sS.n + 32) if (debug and m == 0 and p == 0) else None

                def _act(act, pslots=pslots, p=p, dump0_val=dump0_val):
                    for i_, (s, _m, _p, t, T_, masked) in enumerate(pslots):
                        act.wait_ge(SMM.h, s + 1 + (1 if (i_ == 0 and p == 0) else 0))
                        if dump0_val is not None and s == 2:
                            act.wait_ge(sS.h, dump0_val)
                        if s >= 8:
                            act.wait_ge(AVD.h, s - 7)
                        pt = pts[s % 8]
                        act.activation(pt[:, 0:QB], s_ps[s % 2][:, 0:QB], AF.Exp,
                                       bias=biasseq_sb[:, 2 * s:2 * s + 1], scale=0.125)
                        EXD.inc(act.activation(pt[:, QB:2 * QB], s_ps[s % 2][:, QB:2 * QB],
                                               AF.Exp,
                                               bias=biasseq_sb[:, 2 * s + 1:2 * s + 2],
                                               scale=0.125))
                blk.scalar(_act)

                def _dve(v, pslots=pslots, p=p, pair_idx=pair_idx, m=m):
                    for (s, _m, _p, t, T_, masked) in pslots:
                        if masked:
                            c = MSK_CNT[s]
                            v.wait_ge(EXD.h, s + 1)
                            v.wait_ge(MKL.h, 16 * min(c + 1, MSK_MAX_OF_MP[(_m, _p)]))
                            pt = pts[s % 8]
                            v.tensor_mul(pt[:, 0:QB], pt[:, 0:QB], mkbuf[c % 8][:, :])
                            MSK.inc(v.tensor_mul(pt[:, QB:2 * QB], pt[:, QB:2 * QB],
                                                 mkbuf[c % 8][:, :]))
                    if debug and m == 0 and p == 1:
                        v.wait_ge(sS.h, sS.n)
                    v.wait_ge(AVD.h, pslots[-1][0] + 1)
                    mDVE.inc(v.tensor_copy(dbc[:, :], d_pst[pair_idx % 2][:, :]))
                    v.wait_ge(mDVE.h, mDVE.n)
                    if debug and m == 0 and p == 0:
                        mDVE.inc(v.tensor_copy(st_scr2[:, 0:QB], u_pst[pair_idx % 2][:, :]))
                        v.wait_ge(mDVE.h, mDVE.n)
                    mDVE.inc(v.reciprocal(dbc[:, :], dbc[:, :]))
                    v.wait_ge(mDVE.h, mDVE.n)
                    UEV.inc(v.tensor_mul(u_sb[:, p * QB:(p + 1) * QB],
                                         u_pst[pair_idx % 2][:, :], dbc[:, :]))
                blk.vector(_dve)

                if debug and m == 0 and p == 0:
                    def _dumpkv(sy):
                        sy.wait_ge(KVL.h, 16)
                        sy.dma_start(out=dbg["kv0"].ap(),
                                     in_=kvbuf[0][:, :]).then_inc(sS.h, 16)
                        sS.n += 16
                    blk.sync(_dumpkv)

                    def _dump0(sy):
                        sy.wait_ge(MSK.h, 1)
                        sy.dma_start(out=dbg["pt"].ap()[0, :, :],
                                     in_=pts[0][:, :]).then_inc(sS.h, 16)
                        sS.n += 16
                        assert sS.n == dump0_val
                    blk.sync(_dump0)

                if debug and m == 0 and p == 0 and False:
                    def _dumpp(sy):
                        for si in range(3):
                            sy.wait_ge(MSK.h, si + 1)
                            sy.dma_start(out=dbg["pt"].ap()[si, :, :],
                                         in_=pts[si % 2][:, :]).then_inc(sS.h, 16)
                            sy.dma_start(out=dbg["mk"].ap()[si, :, :],
                                         in_=mkbuf[(si + 1) % 2][:, :]).then_inc(sS.h, 16)
                            sS.n += 32
                    blk.sync(_dumpp)
                    ptdump_val = sS.n


                if debug and m == 0 and p == 0 and False:
                    uev_dump = UEV.n
                    dve_dump = mDVE.n

                    def _dump(sy):
                        sy.wait_ge(UEV.h, uev_dump)
                        sy.wait_ge(mDVE.h, dve_dump)
                        sy.dma_start(out=dbg["dbc"].ap(), in_=dbc[:, :]).then_inc(sS.h, 16)
                        sy.dma_start(out=dbg["upr"].ap(), in_=st_scr2[:, 0:QB]).then_inc(sS.h, 16)
                        sS.n += 32
                    blk.sync(_dump)

        with nc.Block(f"p2wo{m}") as blk, \
                nc.psum_tensor(f"atps0_{m}", [128, 1024], F32) as atps0, \
                nc.psum_tensor(f"atps1_{m}", [128, 1024], F32) as atps1:
            atps = (atps0, atps1)
            uev_need = UEV.n
            x2d_base = X2D.n
            wom_base = WOM.n
            sS_base = sS.n
            NTT = QB // 128

            def _syxo(sy, m=m):
                for tt in range(NTT):
                    if tt >= 2:
                        sy.wait_ge(X2D.h, x2d_base + tt - 1)
                    sy.dma_start(out=xo_buf[tt % 2][:, :],
                                 in_=bass.AP(x_own, (m * QB + tt * 128) * D,
                                             [[1024, 128], [1, 1024]])
                                 ).then_inc(sS.h, 16)
                    sS.n += 16
            blk.sync(_syxo)

            def _pewo(te, m=m):
                te.wait_ge(UEV.h, uev_need)
                for tt in range(NTT):
                    if tt >= 2:
                        te.wait_ge(X2D.h, x2d_base + tt - 1)
                    ps = atps[tt % 2]
                    last = None
                    for p in range(NPAIR):
                        lhsT = u_sb[:, p * QB + tt * 128: p * QB + tt * 128 + 128]
                        for on in range(2):
                            last = te.matmul(ps[:, on * 512:(on + 1) * 512], lhsT,
                                             woT_sb[:, p * 1024 + on * 512:
                                                    p * 1024 + (on + 1) * 512],
                                             start=(p == 0), stop=(p == NPAIR - 1))
                    WOM.inc(last)
            blk.tensor(_pewo)

            def _dvx2(v):
                for tt in range(NTT):
                    v.wait_ge(WOM.h, wom_base + tt + 1)
                    v.wait_ge(sS.h, sS_base + 16 * (tt + 1))
                    X2D.inc(v.tensor_add(x2m[:, tt * 1024:(tt + 1) * 1024],
                                         atps[tt % 2][:, :], xo_buf[tt % 2][:, :]))
            blk.vector(_dvx2)

            def _syx2(sy, m=m):
                for tt in range(NTT):
                    sy.wait_ge(X2D.h, x2d_base + tt + 1)
                    sy.dma_start(out=bass.AP(x2_scr, (m * QB + tt * 128) * D,
                                             [[1024, 128], [1, 1024]]),
                                 in_=x2m[:, tt * 1024:(tt + 1) * 1024]
                                 ).then_inc(sS.h, 16)
                    sS.n += 16
            blk.sync(_syx2)

            apply_at = emit_ln(blk, NTT, x2m, xn2_tokm, X2D, x2d_base + NTT)

            trw_base = mPE2.n
            trd_base = mDVE2.n

            def _trxpe(t, m=m, apply_at=apply_at):
                t.wait_ge(mACT.h, apply_at)
                for tt in range(NTT):
                    if tt >= 2:
                        t.wait_ge(mDVE2.h, trd_base + tt - 1)
                    ps = atps[tt % 2]
                    last = None
                    for fc in range(8):
                        last = t.matmul(ps[:, fc * 128:(fc + 1) * 128],
                                        xn2_tokm[:, tt * 1024 + fc * 128:
                                                 tt * 1024 + fc * 128 + 128],
                                        ident_b[:, :], start=True, stop=True)
                    mPE2.inc(last)
            blk.tensor(_trxpe)

            def _trxev(v, m=m):
                for tt in range(NTT):
                    v.wait_ge(mPE2.h, trw_base + tt + 1)
                    ps = atps[tt % 2]
                    last = None
                    for fc in range(8):
                        last = v.tensor_copy(
                            xn2T_sb[:, fc * 2048 + m * QB + tt * 128:
                                    fc * 2048 + m * QB + tt * 128 + 128],
                            ps[:, fc * 128:(fc + 1) * 128])
                    mDVE2.inc(last)
            blk.vector(_trxev)

            if debug:
                def _dbgm(sy, m=m):
                    sy.dma_start(out=dbg["u"].ap()[m, :, :], in_=u_sb[:, :]).then_inc(sS.h, 16)
                    sy.dma_start(out=dbg["x2"].ap()[m, :, :], in_=x2m[:, :]).then_inc(sS.h, 16)
                    sS.n += 32
                    sy.wait_ge(sS.h, sS.n)
                blk.sync(_dbgm)

    attn2.close()
    qts.close()


    ffnA = ExitStack()
    a_sb = ffnA.enter_context(nc.sbuf_tensor("a_sb", [128, 32 * 2048], BF16))

    with nc.Block("p3a") as blk, \
            nc.sbuf_tensor("w1b0", [128, 1024], BF16) as w1b0, \
            nc.sbuf_tensor("w1b1", [128, 1024], BF16) as w1b1, \
            nc.sbuf_tensor("w2b0", [128, 1024], BF16) as w2b0, \
            nc.sbuf_tensor("w2b1", [128, 1024], BF16) as w2b1, \
            nc.sbuf_tensor("t1_sb", [128, 2048], BF16) as t1_sb, \
            nc.psum_tensor("aps1", [128, 2048], F32) as aps1, \
            nc.psum_tensor("aps2", [128, 2048], F32) as aps2:
        w1b = (w1b0, w1b1)
        w2b = (w2b0, w2b1)
        g_base = sG.n
        fmm_base = FMM.n
        ts1_base = TS1.n
        adn_base = ADN.n

        def _g(g):
            for oc in range(32):
                if oc >= 2:
                    g.wait_ge(FMM.h, fmm_base + 2 * (oc - 2) + 2)
                for blk_in, dst in ((w1blk, w1b[oc % 2]), (w2blk, w2b[oc % 2])):
                    g.dma_start(
                        out=bass.AP(dst, 0, [[1024, 128], [128, 8], [1, 128]]),
                        in_=bass.AP(blk_in, oc * D * 128,
                                    [[128, 128], [128 * 128, 8], [1, 128]])
                    ).then_inc(sG.h, 16)
                    sG.n += 16
        blk.gpsimd(_g)

        def _pe(te):
            for oc in range(32):
                te.wait_ge(sG.h, g_base + 32 * (oc + 1))
                if oc >= 1:
                    te.wait_ge(TS1.h, ts1_base + oc)
                    te.wait_ge(ADN.h, adn_base + oc)
                last = None
                for ic in range(8):
                    lhsT = w1b[oc % 2][:, ic * 128:(ic + 1) * 128]
                    for tb in range(4):
                        last = te.matmul(aps1[:, tb * 512:(tb + 1) * 512], lhsT,
                                         xn2T_sb[:, ic * 2048 + tb * 512:
                                                 ic * 2048 + (tb + 1) * 512],
                                         start=(ic == 0), stop=(ic == 7))
                FMM.inc(last)
                last = None
                for ic in range(8):
                    lhsT = w2b[oc % 2][:, ic * 128:(ic + 1) * 128]
                    for tb in range(4):
                        last = te.matmul(aps2[:, tb * 512:(tb + 1) * 512], lhsT,
                                         xn2T_sb[:, ic * 2048 + tb * 512:
                                                 ic * 2048 + (tb + 1) * 512],
                                         start=(ic == 0), stop=(ic == 7))
                FMM.inc(last)
        blk.tensor(_pe)

        def _act(act):
            for oc in range(32):
                act.wait_ge(FMM.h, fmm_base + 2 * oc + 1)
                if oc >= 1:
                    act.wait_ge(ADN.h, adn_base + oc)
                TS1.inc(act.activation(t1_sb[:, :], aps1[:, :], AF.Silu,
                                       scale=s1_sb[:, oc:oc + 1]))
        blk.scalar(_act)

        def _dv(v):
            for oc in range(32):
                v.wait_ge(FMM.h, fmm_base + 2 * oc + 2)
                v.wait_ge(TS1.h, ts1_base + oc + 1)
                ADN.inc(v.scalar_tensor_tensor(a_sb[:, oc * 2048:(oc + 1) * 2048],
                                               aps2[:, :], s2_sb[:, oc:oc + 1],
                                               t1_sb[:, :], ALU.mult, ALU.mult))
        blk.vector(_dv)

        if debug:
            adn_need = ADN.n

            def _dbga(sy):
                sy.wait_ge(ADN.h, adn_need)
                sy.dma_start(out=dbg["a"].ap(), in_=a_sb[:, :]).then_inc(sS.h, 16)
                sS.n += 16
                sy.wait_ge(sS.h, sS.n)
            blk.sync(_dbga)

    with nc.Block("p3b") as blk, \
            nc.sbuf_tensor("h1b0", [128, 4096], BF16) as h1b0, \
            nc.sbuf_tensor("h1b1", [128, 4096], BF16) as h1b1, \
            nc.sbuf_tensor("bev0", [128, 2048], BF16) as bev0, \
            nc.sbuf_tensor("bev1", [128, 2048], BF16) as bev1, \
            nc.psum_tensor("bps0", [128, 2048], F32) as bps0, \
            nc.psum_tensor("bps1", [128, 2048], F32) as bps1:
        h1b = (h1b0, h1b1)
        bev = (bev0, bev1)
        bps = (bps0, bps1)
        g_base = sG.n
        fmm_base = FMM.n
        ts1_base = TS1.n
        s_base = sS.n

        def _g(g):
            for oc in range(32):
                if oc >= 2:
                    g.wait_ge(FMM.h, fmm_base + oc - 1)
                g.dma_start(
                    out=bass.AP(h1b[oc % 2], 0, [[4096, 128], [128, 32], [1, 128]]),
                    in_=bass.AP(h1blk, oc * HID * 128,
                                [[128, 128], [128 * 128, 32], [1, 128]])
                ).then_inc(sG.h, 16)
                sG.n += 16
        blk.gpsimd(_g)

        def _pe(te):
            for oc in range(32):
                te.wait_ge(sG.h, g_base + 16 * (oc + 1))
                if oc >= 2:
                    te.wait_ge(TS1.h, ts1_base + oc - 1)
                last = None
                for ic in range(32):
                    lhsT = h1b[oc % 2][:, ic * 128:(ic + 1) * 128]
                    for tb in range(4):
                        last = te.matmul(bps[oc % 2][:, tb * 512:(tb + 1) * 512], lhsT,
                                         a_sb[:, ic * 2048 + tb * 512:
                                              ic * 2048 + (tb + 1) * 512],
                                         start=(ic == 0), stop=(ic == 31))
                FMM.inc(last)
        blk.tensor(_pe)

        def _act(act):
            for oc in range(32):
                act.wait_ge(FMM.h, fmm_base + oc + 1)
                if oc >= 2:
                    act.wait_ge(sS.h, s_base + 16 * (oc - 1))
                TS1.inc(act.activation(bev[oc % 2][:, :], bps[oc % 2][:, :], AF.Gelu,
                                       scale=h1s_sb[:, oc:oc + 1]))
        blk.scalar(_act)

        def _sy(sy):
            for oc in range(32):
                sy.wait_ge(TS1.h, ts1_base + oc + 1)
                sy.dma_start(out=bass.AP(b_scr, oc * 128 * 2048, [[2048, 128], [1, 2048]]),
                             in_=bev[oc % 2][:, :]).then_inc(sS.h, 16)
                sS.n += 16
        blk.sync(_sy)
        bscr_q = [s_base + 16 * 8 * (q + 1) for q in range(4)]

    ffnA.close()


    p4 = ExitStack()
    c_sb = p4.enter_context(nc.sbuf_tensor("c_sb", [128, 32 * 1024], BF16))

    for th in range(2):
        with nc.Block(f"p4c{th}") as blk, \
                nc.sbuf_tensor(f"bb_all{th}", [128, 32 * 1024], BF16) as bb_all, \
                nc.sbuf_tensor(f"h2b0{th}", [128, 4096], BF16) as h2b0, \
                nc.sbuf_tensor(f"h2b1{th}", [128, 4096], BF16) as h2b1, \
                nc.psum_tensor(f"cps0{th}", [128, 1024], F32) as cps0, \
                nc.psum_tensor(f"cps1{th}", [128, 1024], F32) as cps1:
            h2b = (h2b0, h2b1)
            cps = (cps0, cps1)
            g_base = sG.n
            bld_base = BLD.n
            cmm_base = CMM.n
            cgl_base = CGL.n

            def _syb(sy, th=th):
                for q in range(4):
                    sy.wait_ge(sS.h, bscr_q[q])
                    sy.dma_start(
                        out=bass.AP(bb_all, q * 8 * 1024,
                                    [[32 * 1024, 128], [1024, 8], [1, 1024]]),
                        in_=bass.AP(b_scr, th * 1024 + q * 8 * 128 * 2048,
                                    [[2048, 128], [128 * 2048, 8], [1, 1024]])
                    ).then_inc(BLD.h, 16)
                    BLD.n += 16
            blk.sync(_syb)

            def _g(g, th=th):
                for oc in range(32):
                    if oc >= 2:
                        g.wait_ge(CMM.h, cmm_base + oc - 1)
                    g.dma_start(
                        out=bass.AP(h2b[oc % 2], 0, [[4096, 128], [128, 32], [1, 128]]),
                        in_=bass.AP(h2blk, oc * HID * 128,
                                    [[128, 128], [128 * 128, 32], [1, 128]])
                    ).then_inc(sG.h, 16)
                    sG.n += 16
            blk.gpsimd(_g)

            def _pe(te, th=th):
                for oc in range(32):
                    te.wait_ge(sG.h, g_base + 16 * (oc + 1))
                    if oc >= 2:
                        te.wait_ge(CGL.h, cgl_base + oc - 1)
                    if oc == 1:
                        te.wait_ge(BLD.h, bld_base + 64)
                    last = None
                    for ic in range(32):
                        if oc == 0 and ic % 8 == 0:
                            te.wait_ge(BLD.h, bld_base + 16 * (ic // 8 + 1))
                        lhsT = h2b[oc % 2][:, ic * 128:(ic + 1) * 128]
                        for tb in range(2):
                            last = te.matmul(cps[oc % 2][:, tb * 512:(tb + 1) * 512],
                                             lhsT,
                                             bb_all[:, ic * 1024 + tb * 512:
                                                    ic * 1024 + (tb + 1) * 512],
                                             start=(ic == 0), stop=(ic == 31))
                    CMM.inc(last)
            blk.tensor(_pe)

            def _act(act, th=th):
                for oc in range(32):
                    act.wait_ge(CMM.h, cmm_base + oc + 1)
                    CGL.inc(act.activation(c_sb[:, oc * 1024:(oc + 1) * 1024],
                                           cps[oc % 2][:, :], AF.Gelu,
                                           scale=h2s_sb[:, oc:oc + 1]))
            blk.scalar(_act)

        with nc.Block(f"p4d{th}") as blk, \
                nc.sbuf_tensor(f"w3b0{th}", [128, 4096], BF16) as w3b0, \
                nc.sbuf_tensor(f"w3b1{th}", [128, 4096], BF16) as w3b1, \
                nc.sbuf_tensor(f"yt{th}", [128, 1024], F32) as yt_sb, \
                nc.sbuf_tensor(f"x2th{th}", [128, 8 * 1024], F32) as x2th, \
                nc.sbuf_tensor(f"ot0{th}", [128, 128], F32) as ot0, \
                nc.sbuf_tensor(f"ot1{th}", [128, 128], F32) as ot1, \
                nc.psum_tensor(f"d4ps{th}", [128, 1024], F32) as dps0, \
                nc.psum_tensor(f"trp0{th}", [128, 128], F32) as trp0, \
                nc.psum_tensor(f"trp1{th}", [128, 128], F32) as trp1:
            w3b = (w3b0, w3b1)
            ot = (ot0, ot1)
            trp = (trp0, trp1)
            g_base = sG.n
            dm_base = DM2.n
            yev_base = YEV.n
            trd_base = TRD.n
            oad_base = OAD.n
            x2l_base = X2L.n
            odm_base = ODM.n

            def _g(g, th=th):
                for oc in range(8):
                    if oc >= 2:
                        g.wait_ge(DM2.h, dm_base + oc - 1)
                    g.dma_start(
                        out=bass.AP(w3b[oc % 2], 0, [[4096, 128], [128, 32], [1, 128]]),
                        in_=bass.AP(w3blk, oc * HID * 128,
                                    [[128, 128], [128 * 128, 32], [1, 128]])
                    ).then_inc(sG.h, 16)
                    sG.n += 16
            blk.gpsimd(_g)

            def _pe(te, th=th):
                for oc in range(8):
                    te.wait_ge(sG.h, g_base + 16 * (oc + 1))
                    if oc >= 1:
                        te.wait_ge(YEV.h, yev_base + oc)
                    last = None
                    for ic in range(32):
                        lhsT = w3b[oc % 2][:, ic * 128:(ic + 1) * 128]
                        for tb in range(2):
                            last = te.matmul(dps0[:, tb * 512:(tb + 1) * 512], lhsT,
                                             c_sb[:, ic * 1024 + tb * 512:
                                                  ic * 1024 + (tb + 1) * 512],
                                             start=(ic == 0), stop=(ic == 31))
                    DM2.inc(last)
                    te.wait_ge(YEV.h, yev_base + oc + 1)
                    for tt in range(8):
                        k = oc * 8 + tt
                        if k >= 2:
                            te.wait_ge(OAD.h, oad_base + k - 1)
                        TRD.inc(te.transpose(trp[k % 2][:, :],
                                             yt_sb[:, tt * 128:(tt + 1) * 128],
                                             ident_sb[:, :]))
            blk.tensor(_pe)

            def _dv(v, th=th):
                v.wait_ge(X2L.h, x2l_base + 16)
                for oc in range(8):
                    v.wait_ge(DM2.h, dm_base + oc + 1)
                    YEV.inc(v.tensor_scalar_mul(yt_sb[:, :], dps0[:, :],
                                                w3s_sb[:, oc:oc + 1]))
                    for tt in range(8):
                        k = oc * 8 + tt
                        v.wait_ge(TRD.h, trd_base + k + 1)
                        if k >= 2:
                            v.wait_ge(ODM.h, odm_base + 16 * (k - 1))
                        OAD.inc(v.tensor_add(ot[k % 2][:, :], trp[k % 2][:, :],
                                             x2th[:, tt * 1024 + oc * 128:
                                                  tt * 1024 + oc * 128 + 128]))
            blk.vector(_dv)

            def _syio(sy, th=th):
                sy.dma_start(
                    out=bass.AP(x2th, 0, [[8 * 1024, 128], [1024, 8], [1, 1024]]),
                    in_=bass.AP(x2_scr, th * 1024 * D,
                                [[1024, 128], [128 * 1024, 8], [1, 1024]])
                ).then_inc(X2L.h, 16)
                X2L.n += 16
                for k in range(64):
                    oc, tt = divmod(k, 8)
                    sy.wait_ge(OAD.h, oad_base + k + 1)
                    sy.dma_start(out=bass.AP(out, (th * 1024 + tt * 128) * D + oc * 128,
                                             [[1024, 128], [1, 128]]),
                                 in_=ot[k % 2][:, :]).then_inc(ODM.h, 16)
                    ODM.n += 16
            blk.sync(_syio)

    with nc.Block("pfin") as blk:
        blk.sync(lambda sy: sy.wait_ge(ODM.h, ODM.n))

    p4.close()
    es.close()
    return nc


_NC_CACHE = {}


def _get_nc(debug=False):
    key = bool(debug)
    if key not in _NC_CACHE:
        _NC_CACHE[key] = build_nc(debug=debug)
    return _NC_CACHE[key]


def _prep_inputs(inputs):
    x = np.asarray(inputs["L0_attn_x"], dtype=np.float32)
    wqkv = np.asarray(inputs["L0_attn_W_qkv"], dtype=np.float32)
    wo = np.asarray(inputs["L0_attn_W_o"], dtype=np.float32)
    w1 = np.asarray(inputs["L0_ffn_w1_w"], dtype=np.float32)
    s1 = np.asarray(inputs["L0_ffn_w1_s"], dtype=np.float32)
    w2 = np.asarray(inputs["L0_ffn_w2_w"], dtype=np.float32)
    s2 = np.asarray(inputs["L0_ffn_w2_s"], dtype=np.float32)
    h1 = np.asarray(inputs["L0_ffn_h1_w"], dtype=np.float32)
    h1s = np.asarray(inputs["L0_ffn_h1_s"], dtype=np.float32)
    h2 = np.asarray(inputs["L0_ffn_h2_w"], dtype=np.float32)
    h2s = np.asarray(inputs["L0_ffn_h2_s"], dtype=np.float32)
    w3 = np.asarray(inputs["L0_ffn_w3_w"], dtype=np.float32)
    w3s = np.asarray(inputs["L0_ffn_w3_s"], dtype=np.float32)

    def cont(a):
        return np.ascontiguousarray(a)

    shared = {
        "wqkvT": cont(wqkv.T),
        "woT": cont(wo.T),
        "w1blk": cont(w1.T.reshape(D, 32, 128).transpose(1, 0, 2)),
        "w2blk": cont(w2.T.reshape(D, 32, 128).transpose(1, 0, 2)),
        "h1blk": cont(h1.T.reshape(HID, 32, 128).transpose(1, 0, 2)),
        "h2blk": cont(h2.T.reshape(HID, 32, 128).transpose(1, 0, 2)),
        "w3blk": cont(w3.T.reshape(HID, 8, 128).transpose(1, 0, 2)),
        "s1m": cont(s1.reshape(32, 128).T),
        "s2m": cont(s2.reshape(32, 128).T),
        "h1sm": cont(h1s.reshape(32, 128).T),
        "h2sm": cont(h2s.reshape(32, 128).T),
        "w3sm": cont(w3s.reshape(8, 128).T),
        "ident": np.eye(128, dtype=np.float32),
    }
    in_maps = []
    for c in range(NCORES):
        b, j = c // 4, c % 4
        kvidx, midx, biasseq = _host_tables(j)
        m = dict(shared)
        m["x_full"] = cont(x[b])
        m["x_own"] = cont(np.concatenate(
            [x[b, QB * (4 * mm + j):QB * (4 * mm + j) + QB] for mm in range(NM)], axis=0))
        m["biasseq"] = cont(biasseq)
        m["kvidx"] = kvidx
        m["midx"] = midx
        m["maskseq"] = _host_maskseq(j)
        in_maps.append(m)
    return in_maps


def kernel(**inputs):
    nc = _get_nc(debug=False)
    in_maps = _prep_inputs(inputs)
    res = run_bass_kernel_spmd(nc, in_maps, core_ids=list(range(NCORES)))
    final = np.empty((B, S, D), dtype=np.float32)
    for c in range(NCORES):
        b, j = c // 4, c % 4
        o = res.results[c]["out"]
        for mm in range(NM):
            final[b, QB * (4 * mm + j):QB * (4 * mm + j) + QB] = o[QB * mm:QB * (mm + 1)]
    return final


# version 18
# speedup vs baseline: 1.0171x; 1.0171x over previous
import numpy as np
import ml_dtypes
import concourse.bass as bass
import concourse.mybir as mybir
from concourse.bass_utils import run_bass_kernel_spmd

F32 = mybir.dt.float32
BF16 = mybir.dt.bfloat16
I32 = mybir.dt.int32
AF = mybir.ActivationFunctionType
ALU = mybir.AluOpType

B, S, D, H, HD, HID = 2, 8192, 1024, 16, 64, 4096
NCORES = 8
NPAIR = 8
QB = 512
NM = 4
TRUNC = 160.0

SLOPES = [2.0 ** (-0.5 * (h + 1)) for h in range(H)]
KB = [min(S // 128, int((TRUNC / SLOPES[2 * p + 1]) // 128) + 1) for p in range(NPAIR)]
TMP = [[min(16 * m + 16, KB[p]) for p in range(NPAIR)] for m in range(NM)]
NSLOT = sum(sum(t) for t in TMP)
KV_TILE = 2 * 128 * 128


SLOTS = []
for _m in range(NM):
    for _p in range(NPAIR):
        for _t in range(TMP[_m][_p]):
            SLOTS.append((len(SLOTS), _m, _p, _t, TMP[_m][_p], _t >= 16 * _m))
_c = 0
MSK_CNT = []
for _s in SLOTS:
    _c += 1 if _s[5] else 0
    MSK_CNT.append(_c)
MSK_MAX_OF_MP = {}
for _s in SLOTS:
    if _s[5]:
        MSK_MAX_OF_MP[(_s[1], _s[2])] = MSK_CNT[_s[0]]


def _host_tables(j):
    kvidx = np.zeros((128, NSLOT), dtype=np.int32)
    midx = np.zeros((128, NSLOT), dtype=np.int32)
    biasseq = np.empty((128, 2 * NSLOT), dtype=np.float32)
    rows = np.arange(128, dtype=np.float64)
    ri = np.arange(128, dtype=np.int32)
    for (s, m, p, t, T, masked) in SLOTS:
        dstart = 16 * m + 4 * j
        live = min(KB[p], dstart + 4)
        if t < live:
            kt = t
            kvidx[:, s] = (p * 64 + kt) * 128 + ri
            for hs, h in enumerate((2 * p, 2 * p + 1)):
                biasseq[:, 2 * s + hs] = (-SLOPES[h] * (128 * kt + rows)).astype(np.float32)
            pat = (kt - dstart) if kt >= dstart else 4
        else:
            kvidx[:, s] = ri
            biasseq[:, 2 * s] = -1e30
            biasseq[:, 2 * s + 1] = -1e30
            pat = 4
        midx[:, s] = pat * 128 + ri
    return kvidx, midx, biasseq


NMASKED = MSK_CNT[-1]


def _host_maskseq(j):
    pats = _host_masks().astype(np.float32)
    seq = np.empty((NMASKED, 128, QB), dtype=np.float32)
    for (s, m, p, t, T, masked) in SLOTS:
        if not masked:
            continue
        dstart = 16 * m + 4 * j
        live = min(KB[p], dstart + 4)
        pat = ((t - dstart) if t >= dstart else 4) if t < live else 4
        seq[MSK_CNT[s] - 1] = pats[pat * 128:(pat + 1) * 128, :]
    return np.ascontiguousarray(seq.astype(ml_dtypes.bfloat16))


def _host_masks():
    pats = np.zeros((5 * 128, QB), dtype=np.float32)
    cols = np.arange(QB)[None, :]
    r = np.arange(128)[:, None]
    for d in range(4):
        pats[d * 128:(d + 1) * 128] = (cols >= 128 * d + r)
    pats[4 * 128:] = 1.0
    return np.ascontiguousarray(pats.astype(ml_dtypes.bfloat16))


from concourse.bass_core_rust import IndirectOffsetOnAxis


class SemC:
    REGISTRY = []

    def __init__(self, es, nc, name):
        self.h = es.enter_context(nc.semaphore(name))
        self.n = 0
        SemC.REGISTRY.append(self)

    def inc(self, inst, k=1):
        inst.then_inc(self.h, k)
        self.n += k
        return self.n


def build_nc(debug=False):
    from contextlib import ExitStack
    nc = bass.Bass("TRN2", target_bir_lowering=False)
    es = ExitStack()

    x_full = nc.dram_tensor("x_full", [S, D], F32, kind="ExternalInput")
    x_own = nc.dram_tensor("x_own", [NM * QB, D], F32, kind="ExternalInput")
    wqkvT = nc.dram_tensor("wqkvT", [D, 3 * D], F32, kind="ExternalInput")
    woT = nc.dram_tensor("woT", [D, D], F32, kind="ExternalInput")
    w1blk = nc.dram_tensor("w1blk", [32, D, 128], F32, kind="ExternalInput")
    w2blk = nc.dram_tensor("w2blk", [32, D, 128], F32, kind="ExternalInput")
    h1blk = nc.dram_tensor("h1blk", [32, HID, 128], F32, kind="ExternalInput")
    h2blk = nc.dram_tensor("h2blk", [32, HID, 128], F32, kind="ExternalInput")
    w3blk = nc.dram_tensor("w3blk", [8, HID, 128], F32, kind="ExternalInput")
    s1m = nc.dram_tensor("s1m", [128, 32], F32, kind="ExternalInput")
    s2m = nc.dram_tensor("s2m", [128, 32], F32, kind="ExternalInput")
    h1sm = nc.dram_tensor("h1sm", [128, 32], F32, kind="ExternalInput")
    h2sm = nc.dram_tensor("h2sm", [128, 32], F32, kind="ExternalInput")
    w3sm = nc.dram_tensor("w3sm", [128, 8], F32, kind="ExternalInput")
    biasseq_in = nc.dram_tensor("biasseq", [128, 2 * NSLOT], F32, kind="ExternalInput")
    kvidx_in = nc.dram_tensor("kvidx", [128, NSLOT], I32, kind="ExternalInput")
    midx_in = nc.dram_tensor("midx", [128, NSLOT], I32, kind="ExternalInput")
    maskseq_in = nc.dram_tensor("maskseq", [NMASKED, 128, QB], BF16, kind="ExternalInput")
    ident_in = nc.dram_tensor("ident", [128, 128], F32, kind="ExternalInput")
    out = nc.dram_tensor("out", [NM * QB, D], F32, kind="ExternalOutput")

    kv_scr = nc.dram_tensor("kv_scr", [NPAIR * 64 * 128, 256], BF16)
    flush_scr = nc.dram_tensor("flush_scr", [128, 2 * 256 + 2 * QB], BF16)
    x2_scr = nc.dram_tensor("x2_scr", [NM * QB, D], F32)
    b_scr = nc.dram_tensor("b_scr", [HID, NM * QB], BF16)

    dbg = {}
    if debug:
        dbg["qt"] = nc.dram_tensor("dbg_qt", [128, 8 * 2048], BF16, kind="ExternalOutput")
        dbg["xnt"] = nc.dram_tensor("dbg_xnt", [128, 8 * 2048], BF16, kind="ExternalOutput")
        dbg["kv0"] = nc.dram_tensor("dbg_kv0", [128, 256], BF16, kind="ExternalOutput")
        dbg["u"] = nc.dram_tensor("dbg_u", [NM, 128, 8 * QB], BF16, kind="ExternalOutput")
        dbg["x2"] = nc.dram_tensor("dbg_x2", [NM, 128, 4 * 1024], F32, kind="ExternalOutput")
        dbg["a"] = nc.dram_tensor("dbg_a", [128, 32 * 2048], BF16, kind="ExternalOutput")
        dbg["dbc"] = nc.dram_tensor("dbg_dbc", [128, QB], F32, kind="ExternalOutput")
        dbg["upr"] = nc.dram_tensor("dbg_upr", [128, QB], F32, kind="ExternalOutput")
        dbg["pt"] = nc.dram_tensor("dbg_pt", [3, 128, 2 * QB], BF16, kind="ExternalOutput")
        dbg["mk"] = nc.dram_tensor("dbg_mk", [3, 128, QB], BF16, kind="ExternalOutput")
        dbg["tbl"] = nc.dram_tensor("dbg_tbl", [128, 16], I32, kind="ExternalOutput")
        dbg["sps"] = nc.dram_tensor("dbg_sps", [128, 1024], F32, kind="ExternalOutput")
        dbg["qt2"] = nc.dram_tensor("dbg_qt2", [128, 8 * 2048], BF16, kind="ExternalOutput")
        dbg["xblk0"] = nc.dram_tensor("dbg_xblk0", [128, 4 * 1024], F32, kind="ExternalOutput")
        dbg["xntok0"] = nc.dram_tensor("dbg_xntok0", [128, 4 * 1024], BF16, kind="ExternalOutput")
        dbg["bsq"] = nc.dram_tensor("dbg_bsq", [128, 8], F32, kind="ExternalOutput")

    def sbp(name, shape, dt):
        return es.enter_context(nc.sbuf_tensor(name, shape, dt))

    ones64 = sbp("ones64", [128, 64], F32)
    ones64b = sbp("ones64b", [128, 64], BF16)
    ident_sb = sbp("ident_sb", [128, 128], F32)
    s1_sb = sbp("s1_sb", [128, 32], F32)
    s2_sb = sbp("s2_sb", [128, 32], F32)
    h1s_sb = sbp("h1s_sb", [128, 32], F32)
    h2s_sb = sbp("h2s_sb", [128, 32], F32)
    w3s_sb = sbp("w3s_sb", [128, 8], F32)
    st_scr = sbp("st_scr", [128, 1024], F32)
    st_scr2 = sbp("st_scr2", [128, 1024], F32)
    ident_b = sbp("ident_b", [128, 128], BF16)
    st_small = sbp("st_small", [128, 64], F32)
    xn2T_sb = sbp("xn2T_sb", [128, 8 * 2048], BF16)
    STATS = {
        "ssum": st_small[:, 0:8], "ssq": st_small[:, 8:16],
        "mean": st_small[:, 16:24], "var": st_small[:, 24:32],
        "m2": st_small[:, 32:40], "rstd": st_small[:, 40:48],
        "nb": st_small[:, 48:56], "lnv": st_small[:, 56:64],
    }

    SemC.REGISTRY = []
    sS = SemC(es, nc, "sS")
    sA = SemC(es, nc, "sA")
    sG = SemC(es, nc, "sG")
    mPE = SemC(es, nc, "mPE")
    mPE2 = SemC(es, nc, "mPE2")
    mACT = SemC(es, nc, "mACT")
    mDVE = SemC(es, nc, "mDVE")
    mDVE2 = SemC(es, nc, "mDVE2")
    KVL = SemC(es, nc, "KVL")
    MKL = SemC(es, nc, "MKL")
    SMM = SemC(es, nc, "SMM")
    EXD = SemC(es, nc, "EXD")
    MSK = SemC(es, nc, "MSK")
    AVD = SemC(es, nc, "AVD")
    UEV = SemC(es, nc, "UEV")
    WOM = SemC(es, nc, "WOM")
    X2D = SemC(es, nc, "X2D")
    FLS = SemC(es, nc, "FLS")
    FMM = SemC(es, nc, "FMM")
    TS1 = SemC(es, nc, "TS1")
    ADN = SemC(es, nc, "ADN")
    BLD = SemC(es, nc, "BLD")
    CMM = SemC(es, nc, "CMM")
    CGL = SemC(es, nc, "CGL")
    DM2 = SemC(es, nc, "DM2")
    YEV = SemC(es, nc, "YEV")
    TRD = SemC(es, nc, "TRD")
    OAD = SemC(es, nc, "OAD")
    X2L = SemC(es, nc, "X2L")
    ODM = SemC(es, nc, "ODM")

    def emit_ln(blk, nt, xblk, xn_tok, ready_sem, ready_val,
                apply_sem=None, apply_val=0, apply_sem2=None, apply_val2=0,
                warm=0, post=0):
        def _act1(act):
            act.wait_ge(ready_sem.h, ready_val)
            for _ in range(warm):
                act.activation(st_scr[:, :], st_scr[:, :], AF.Square)
            last = None
            for t in range(nt):
                last = act.activation(st_scr[:, :], xblk[:, t * 1024:(t + 1) * 1024],
                                      AF.Square, accum_out=STATS["ssq"][:, t:t + 1])
            mACT.inc(last)
        blk.scalar(_act1)
        act_sq = mACT.n

        def _dve1(vec):
            vec.wait_ge(ready_sem.h, ready_val)
            for _ in range(warm):
                vec.tensor_scalar_mul(st_scr2[:, :], st_scr2[:, :], 1.0)
            for t in range(nt):
                vec.tensor_scalar(st_scr2[:, :], xblk[:, t * 1024:(t + 1) * 1024],
                                  1.0, 0.0, ALU.mult, ALU.add,
                                  accum_out=STATS["ssum"][:, t:t + 1])
            vec.wait_ge(mACT.h, act_sq)
            mDVE.inc(vec.tensor_scalar_mul(STATS["mean"][:, 0:nt], STATS["ssum"][:, 0:nt], 1.0 / D))
            vec.wait_ge(mDVE.h, mDVE.n)
            vec.tensor_scalar_mul(STATS["var"][:, 0:nt], STATS["ssq"][:, 0:nt], 1.0 / D)
            mDVE.inc(vec.tensor_mul(STATS["m2"][:, 0:nt], STATS["mean"][:, 0:nt], STATS["mean"][:, 0:nt]))
            vec.wait_ge(mDVE.h, mDVE.n)
            mDVE.inc(vec.tensor_sub(STATS["var"][:, 0:nt], STATS["var"][:, 0:nt], STATS["m2"][:, 0:nt]))
            vec.wait_ge(mDVE.h, mDVE.n)
            mDVE.inc(vec.tensor_scalar_add(STATS["var"][:, 0:nt], STATS["var"][:, 0:nt], 1e-5))
        blk.vector(_dve1)
        dve_var = mDVE.n

        def _act2(act):
            act.wait_ge(mDVE.h, dve_var)
            mACT.inc(act.activation(STATS["lnv"][:, 0:nt], STATS["var"][:, 0:nt], AF.Ln))
            act.wait_ge(mACT.h, mACT.n)
            mACT.inc(act.activation(STATS["rstd"][:, 0:nt], STATS["lnv"][:, 0:nt],
                                    AF.Exp, scale=-0.5))
        blk.scalar(_act2)
        act_rstd = mACT.n

        def _dve2(vec):
            vec.wait_ge(mACT.h, act_rstd)
            mDVE.inc(vec.tensor_mul(STATS["nb"][:, 0:nt], STATS["mean"][:, 0:nt], STATS["rstd"][:, 0:nt]))
            vec.wait_ge(mDVE.h, mDVE.n)
            mDVE.inc(vec.tensor_scalar_mul(STATS["nb"][:, 0:nt], STATS["nb"][:, 0:nt], -1.0))
        blk.vector(_dve2)
        dve_nb = mDVE.n

        def _act3(act):
            act.wait_ge(mDVE.h, dve_nb)
            if apply_sem is not None:
                act.wait_ge(apply_sem.h, apply_val)
            if apply_sem2 is not None:
                act.wait_ge(apply_sem2.h, apply_val2)
            last = None
            for t in range(nt):
                last = act.activation(xn_tok[:, t * 1024:(t + 1) * 1024],
                                      xblk[:, t * 1024:(t + 1) * 1024], AF.Identity,
                                      bias=STATS["nb"][:, t:t + 1],
                                      scale=STATS["rstd"][:, t:t + 1])
            for t in range(post):
                last = act.activation(st_scr[:, 0:1024],
                                      xn_tok[:, t * 1024:(t + 1) * 1024], AF.Square)
            mACT.inc(last)
        blk.scalar(_act3)
        return mACT.n

    def emit_ln2(blk, xa, xb, xna, xnb, ra_sem, ra_val, rb_val,
                 apply_sem=None, apply_vala=0, apply_valb=0, warm=0):
        def _act1(act):
            act.wait_ge(ra_sem.h, ra_val)
            for _ in range(warm):
                act.activation(st_scr[:, :], st_scr[:, :], AF.Square)
            last = None
            for t in range(4):
                last = act.activation(st_scr[:, :], xa[:, t * 1024:(t + 1) * 1024],
                                      AF.Square, accum_out=STATS["ssq"][:, t:t + 1])
            act.wait_ge(ra_sem.h, rb_val)
            for t in range(4):
                last = act.activation(st_scr[:, :], xb[:, t * 1024:(t + 1) * 1024],
                                      AF.Square, accum_out=STATS["ssq"][:, 4 + t:5 + t])
            mACT.inc(last)
        blk.scalar(_act1)
        act_sq = mACT.n

        def _dve1(vec):
            vec.wait_ge(ra_sem.h, ra_val)
            for _ in range(warm):
                vec.tensor_scalar_mul(st_scr2[:, :], st_scr2[:, :], 1.0)
            for t in range(4):
                vec.tensor_scalar(st_scr2[:, :], xa[:, t * 1024:(t + 1) * 1024],
                                  1.0, 0.0, ALU.mult, ALU.add,
                                  accum_out=STATS["ssum"][:, t:t + 1])
            vec.wait_ge(ra_sem.h, rb_val)
            for t in range(4):
                vec.tensor_scalar(st_scr2[:, :], xb[:, t * 1024:(t + 1) * 1024],
                                  1.0, 0.0, ALU.mult, ALU.add,
                                  accum_out=STATS["ssum"][:, 4 + t:5 + t])
            vec.wait_ge(mACT.h, act_sq)
            mDVE.inc(vec.tensor_scalar_mul(STATS["mean"][:, 0:8], STATS["ssum"][:, 0:8], 1.0 / D))
            vec.wait_ge(mDVE.h, mDVE.n)
            vec.tensor_scalar_mul(STATS["var"][:, 0:8], STATS["ssq"][:, 0:8], 1.0 / D)
            mDVE.inc(vec.tensor_mul(STATS["m2"][:, 0:8], STATS["mean"][:, 0:8], STATS["mean"][:, 0:8]))
            vec.wait_ge(mDVE.h, mDVE.n)
            mDVE.inc(vec.tensor_sub(STATS["var"][:, 0:8], STATS["var"][:, 0:8], STATS["m2"][:, 0:8]))
            vec.wait_ge(mDVE.h, mDVE.n)
            mDVE.inc(vec.tensor_scalar_add(STATS["var"][:, 0:8], STATS["var"][:, 0:8], 1e-5))
        blk.vector(_dve1)
        dve_var = mDVE.n

        def _act2(act):
            act.wait_ge(mDVE.h, dve_var)
            mACT.inc(act.activation(STATS["lnv"][:, 0:8], STATS["var"][:, 0:8], AF.Ln))
            act.wait_ge(mACT.h, mACT.n)
            mACT.inc(act.activation(STATS["rstd"][:, 0:8], STATS["lnv"][:, 0:8],
                                    AF.Exp, scale=-0.5))
        blk.scalar(_act2)
        act_rstd = mACT.n

        def _dve2(vec):
            vec.wait_ge(mACT.h, act_rstd)
            mDVE.inc(vec.tensor_mul(STATS["nb"][:, 0:8], STATS["mean"][:, 0:8], STATS["rstd"][:, 0:8]))
            vec.wait_ge(mDVE.h, mDVE.n)
            mDVE.inc(vec.tensor_scalar_mul(STATS["nb"][:, 0:8], STATS["nb"][:, 0:8], -1.0))
        blk.vector(_dve2)
        dve_nb = mDVE.n

        def _act3(act):
            act.wait_ge(mDVE.h, dve_nb)
            if apply_sem is not None:
                act.wait_ge(apply_sem.h, apply_vala)
            last = None
            for t in range(4):
                last = act.activation(xna[:, t * 1024:(t + 1) * 1024],
                                      xa[:, t * 1024:(t + 1) * 1024], AF.Identity,
                                      bias=STATS["nb"][:, t:t + 1],
                                      scale=STATS["rstd"][:, t:t + 1])
            mACT.inc(last)
        blk.scalar(_act3)
        a_done = mACT.n

        def _act4(act):
            if apply_sem is not None:
                act.wait_ge(apply_sem.h, apply_valb)
            last = None
            for t in range(4):
                last = act.activation(xnb[:, t * 1024:(t + 1) * 1024],
                                      xb[:, t * 1024:(t + 1) * 1024], AF.Identity,
                                      bias=STATS["nb"][:, 4 + t:5 + t],
                                      scale=STATS["rstd"][:, 4 + t:5 + t])
            mACT.inc(last)
        blk.scalar(_act4)
        return a_done, mACT.n

    with nc.Block("p0") as blk:
        def _g(g):
            for sc in SemC.REGISTRY:
                g.sem_clear(sc.h)
            g.memset(ones64[:, :], 1.0)
            g.memset(ones64b[:, :], 1.0)
            for dst, src in ((ident_sb, ident_in), (s1_sb, s1m), (s2_sb, s2m),
                             (h1s_sb, h1sm), (h2s_sb, h2sm), (w3s_sb, w3sm)):
                g.dma_start(out=dst[:, :], in_=src.ap()).then_inc(sG.h, 16)
                sG.n += 16
            g.wait_ge(sG.h, sG.n)
        blk.gpsimd(_g)

        p0_g = sG.n

        def _cast(act):
            act.wait_ge(sG.h, p0_g)
            act.activation(ident_b[:, :], ident_sb[:, :], AF.Copy)
        blk.scalar(_cast)

    qts = ExitStack()
    qt_sb = qts.enter_context(nc.sbuf_tensor("qt_sb", [128, 8 * 2048], BF16))

    p1 = ExitStack()
    xblk = (p1.enter_context(nc.sbuf_tensor("xblkA", [128, 4 * 1024], F32)),
            p1.enter_context(nc.sbuf_tensor("xblkB", [128, 4 * 1024], F32)))
    xn_tok = (p1.enter_context(nc.sbuf_tensor("xn_tokA", [128, 4 * 1024], BF16)),
              p1.enter_context(nc.sbuf_tensor("xn_tokB", [128, 4 * 1024], BF16)))
    wqkvT_sb = p1.enter_context(nc.sbuf_tensor("wqkvT_sb", [128, 8 * 3072], BF16))

    sub1a = ExitStack()
    xn_ownT = sub1a.enter_context(nc.sbuf_tensor("xn_ownT", [128, 8 * 2048], BF16))

    with nc.Block("p1a") as blk, \
            nc.psum_tensor("tpA", [128, 1024], F32) as tpA, \
            nc.psum_tensor("tpB", [128, 1024], F32) as tpB:
        def _g(g):
            g.dma_start(
                out=bass.AP(wqkvT_sb, 0, [[8 * 3072, 128], [3072, 8], [1, 3072]]),
                in_=bass.AP(wqkvT, 0, [[3072, 128], [128 * 3072, 8], [1, 3072]])
            ).then_inc(sG.h, 16)
            sG.n += 16
        blk.gpsimd(_g)

        tps = (tpA, tpB)
        ld_hist = []
        apply_hist = []
        tpe_base = mPE.n
        tdve_base = mDVE2.n

        def _ld(sy, g_):
            if g_ >= 2:
                sy.wait_ge(mACT.h, apply_hist[g_ - 2])
            sy.dma_start(
                out=bass.AP(xblk[g_ % 2], 0, [[4 * 1024, 128], [1024, 4], [1, 1024]]),
                in_=bass.AP(x_own, g_ * 512 * D, [[1024, 128], [128 * 1024, 4], [1, 1024]])
            ).then_inc(sS.h, 16)
            sS.n += 16

        blk.sync(lambda sy: _ld(sy, 0))
        ld_hist.append(sS.n)
        blk.sync(lambda sy: _ld(sy, 1))
        ld_hist.append(sS.n)

        for pr in range(2):
            ga_, gb_ = 2 * pr, 2 * pr + 1
            ap_pair = emit_ln2(
                blk, xblk[ga_ % 2], xblk[gb_ % 2], xn_tok[ga_ % 2], xn_tok[gb_ % 2],
                sS, ld_hist[ga_], ld_hist[gb_],
                apply_sem=(mPE if ga_ >= 2 else None),
                apply_vala=(tpe_base + 4 * (ga_ - 2) + 4 if ga_ >= 2 else 0),
                apply_valb=(tpe_base + 4 * (gb_ - 2) + 4 if gb_ >= 2 else 0),
                warm=4)
            apply_hist.append(ap_pair[0])
            apply_hist.append(ap_pair[1])
            for g_ in (ga_, gb_):
                apply_at = ap_pair[g_ % 2]
                if debug and g_ == 0:
                    def _gdump(g, ld0=ld_hist[0], ap0=apply_at):
                        g.wait_ge(sS.h, ld0)
                        g.dma_start(out=dbg["xblk0"].ap(), in_=xblk[0][:, :]).then_inc(sG.h, 16)
                        g.wait_ge(mACT.h, ap0)
                        g.dma_start(out=dbg["xntok0"].ap(), in_=xn_tok[0][:, :]).then_inc(sG.h, 16)
                        sG.n += 32
                    blk.gpsimd(_gdump)

                def _tpe(t, g_=g_, apply_at=apply_at):
                    t.wait_ge(mACT.h, apply_at)
                    for tt in range(4):
                        idx = g_ * 4 + tt
                        if idx >= 2:
                            t.wait_ge(mDVE2.h, tdve_base + idx - 1)
                        ps = tps[idx % 2]
                        last = None
                        for fc in range(8):
                            last = t.matmul(ps[:, fc * 128:(fc + 1) * 128],
                                            xn_tok[g_ % 2][:, tt * 1024 + fc * 128:
                                                           tt * 1024 + fc * 128 + 128],
                                            ident_b[:, :], start=True, stop=True)
                        mPE.inc(last)
                blk.tensor(_tpe)
                assert mPE.n == tpe_base + 4 * g_ + 4

                def _tev(v, g_=g_):
                    for tt in range(4):
                        idx = g_ * 4 + tt
                        v.wait_ge(mPE.h, tpe_base + idx + 1)
                        ps = tps[idx % 2]
                        last = None
                        for fc in range(8):
                            last = v.tensor_copy(
                                xn_ownT[:, fc * 2048 + g_ * 512 + tt * 128:
                                        fc * 2048 + g_ * 512 + tt * 128 + 128],
                                ps[:, fc * 128:(fc + 1) * 128])
                        mDVE2.inc(last)
                blk.vector(_tev)
                assert mDVE2.n == tdve_base + 4 * g_ + 4
                if g_ + 2 < 4:
                    blk.sync(lambda sy, g_=g_: _ld(sy, g_ + 2))
                    ld_hist.append(sS.n)

    with nc.Block("p1q") as blk, \
            nc.psum_tensor("qps0", [128, 2048], F32) as qps0, \
            nc.psum_tensor("qps1", [128, 2048], F32) as qps1:
        qps = (qps0, qps1)
        pe_base = mPE.n
        dve_base = mDVE.n
        g_need = sG.n
        xnt_need = mDVE2.n

        def _pe(t):
            t.wait_ge(mDVE2.h, xnt_need)
            t.wait_ge(sG.h, g_need)
            for p in range(NPAIR):
                if p >= 2:
                    t.wait_ge(mDVE.h, dve_base + p - 1)
                ps = qps[p % 2]
                last = None
                for ic in range(8):
                    lhsT = wqkvT_sb[:, ic * 3072 + p * 128: ic * 3072 + p * 128 + 128]
                    for qn in range(4):
                        last = t.matmul(ps[:, qn * 512:(qn + 1) * 512], lhsT,
                                        xn_ownT[:, ic * 2048 + qn * 512:
                                                ic * 2048 + (qn + 1) * 512],
                                        start=(ic == 0), stop=(ic == 7))
                mPE.inc(last)
        blk.tensor(_pe)

        def _dv(v):
            for p in range(NPAIR):
                v.wait_ge(mPE.h, pe_base + p + 1)
                mDVE.inc(v.tensor_copy(qt_sb[:, p * 2048:(p + 1) * 2048], qps[p % 2][:, :]))
        blk.vector(_dv)

        if debug:
            dv_need = mDVE.n

            def _dbg(sy):
                sy.wait_ge(mDVE.h, dv_need)
                sy.dma_start(out=dbg["qt"].ap(), in_=qt_sb[:, :]).then_inc(sS.h, 16)
                sy.dma_start(out=dbg["xnt"].ap(), in_=xn_ownT[:, :]).then_inc(sS.h, 16)
                sS.n += 32
                sy.wait_ge(sS.h, sS.n)
            blk.sync(_dbg)

    sub1a.close()

    sub1b = ExitStack()
    xnT = (sub1b.enter_context(nc.sbuf_tensor("xnT_A", [128, 8 * 1024], BF16)),
           sub1b.enter_context(nc.sbuf_tensor("xnT_B", [128, 8 * 1024], BF16)))
    kv_ev = (sub1b.enter_context(nc.sbuf_tensor("kv_evA", [128, 1024], BF16)),
             sub1b.enter_context(nc.sbuf_tensor("kv_evB", [128, 1024], BF16)))
    KEPT = [[p for p in range(NPAIR) if KB[p] > 8 * bb] for bb in range(8)]

    with nc.Block("p1b") as blk, \
            nc.psum_tensor("ktps", [128, 1024], F32) as ktps, \
            nc.psum_tensor("vtps0", [128, 1024], F32) as vtps0, \
            nc.psum_tensor("vtps1", [128, 1024], F32) as vtps1, \
            nc.psum_tensor("trp0", [128, 512], F32) as trp0, \
            nc.psum_tensor("trp1", [128, 512], F32) as trp1:
        vtps = (vtps0, vtps1)
        trps = (trp0, trp1)
        ld_hist = []
        apply_hist = []
        kev_vals = []
        vev_vals = []
        trgrp_evac = []
        trgrp_pe = []
        trpe_done = []
        trev_done = []
        _n = mDVE2.n
        _p = mPE2.n
        for _bb in range(8):
            for _h in range(2):
                for _ in range(8):
                    _n += 1
                    trgrp_evac.append(_n)
                    _p += 1
                    trgrp_pe.append(_p)
                trev_done.append(_n)
                trpe_done.append(_p)
            for _ in KEPT[_bb]:
                _n += 1
                kev_vals.append(_n)
                _p += 1
            for _ in range(8):
                _n += 1
                vev_vals.append(_n)
                _p += 1
        evac_n = 0
        gk_count = [0]
        gv_count = [0]
        evac_wr = {}
        _sg = sG.n
        _ei = 0
        for _bb in range(8):
            for _ in KEPT[_bb]:
                _sg += 16
                evac_wr[_ei] = _sg
                _ei += 1
            for _ in range(8):
                _sg += 16 * len(KEPT[_bb])
                evac_wr[_ei] = _sg
                _ei += 1

        def _ld(sy, gg):
            if gg >= 2:
                sy.wait_ge(mACT.h, apply_hist[gg - 2])
            sy.dma_start(
                out=bass.AP(xblk[gg % 2], 0, [[4 * 1024, 128], [1024, 4], [1, 1024]]),
                in_=bass.AP(x_full, gg * 512 * D, [[1024, 128], [128 * 1024, 4], [1, 1024]])
            ).then_inc(sS.h, 16)
            sS.n += 16

        blk.sync(lambda sy: _ld(sy, 0))
        ld_hist.append(sS.n)
        blk.sync(lambda sy: _ld(sy, 1))
        ld_hist.append(sS.n)

        for bb in range(8):
            ga, gb = 2 * bb, 2 * bb + 1
            ap_pair = emit_ln2(
                blk, xblk[ga % 2], xblk[gb % 2], xn_tok[ga % 2], xn_tok[gb % 2],
                sS, ld_hist[ga], ld_hist[gb],
                apply_sem=(mPE2 if ga >= 2 else None),
                apply_vala=(trpe_done[ga - 2] if ga >= 2 else 0),
                apply_valb=(trpe_done[gb - 2] if gb >= 2 else 0))
            apply_hist.append(ap_pair[0])
            apply_hist.append(ap_pair[1])
            for half in range(2):
                gg = 2 * bb + half
                apply_at = ap_pair[half]

                def _trpe(t, gg=gg, apply_at=apply_at):
                    t.wait_ge(mACT.h, apply_at)
                    for grp in range(8):
                        g = gg * 8 + grp
                        if g >= 2:
                            t.wait_ge(mDVE2.h, trgrp_evac[g - 2])
                        tt, fch = grp // 2, (grp % 2) * 4
                        ps = trps[g % 2]
                        last = None
                        for k in range(4):
                            fc = fch + k
                            last = t.matmul(ps[:, k * 128:(k + 1) * 128],
                                            xn_tok[gg % 2][:, tt * 1024 + fc * 128:
                                                           tt * 1024 + fc * 128 + 128],
                                            ident_b[:, :], start=True, stop=True)
                        mPE2.inc(last)
                        assert mPE2.n == trgrp_pe[g]
                blk.tensor(_trpe)

                def _trev(v, gg=gg, bb=bb, half=half):
                    for grp in range(8):
                        g = gg * 8 + grp
                        v.wait_ge(mPE2.h, trgrp_pe[g])
                        tt, fch = grp // 2, (grp % 2) * 4
                        ps = trps[g % 2]
                        last = None
                        for k in range(4):
                            fc = fch + k
                            last = v.tensor_copy(
                                xnT[bb % 2][:, fc * 1024 + half * 512 + tt * 128:
                                            fc * 1024 + half * 512 + tt * 128 + 128],
                                ps[:, k * 128:(k + 1) * 128])
                        mDVE2.inc(last)
                        assert mDVE2.n == trgrp_evac[g]
                blk.vector(_trev)

                if gg + 2 < 16:
                    blk.sync(lambda sy, gg=gg: _ld(sy, gg + 2))
                    ld_hist.append(sS.n)

            kept = KEPT[bb]
            pmin, npk = kept[0], len(kept)
            tr_need = trev_done[2 * bb + 1]

            pe2_base = mPE2.n
            dve2k_base = mDVE2.n

            nk_prev = gk_count[0]

            def _pek(t, bb=bb, kept=kept, tr_need=tr_need, nk_prev=nk_prev):
                t.wait_ge(mDVE2.h, tr_need)
                for idx, p in enumerate(kept):
                    gk = nk_prev + idx
                    if gk >= 1:
                        t.wait_ge(mDVE2.h, kev_vals[gk - 1])
                    last = None
                    for ic in range(8):
                        lhsT = wqkvT_sb[:, ic * 3072 + 1024 + p * 128:
                                        ic * 3072 + 1024 + p * 128 + 128]
                        for qn in range(2):
                            last = t.matmul(ktps[:, qn * 512:(qn + 1) * 512], lhsT,
                                            xnT[bb % 2][:, ic * 1024 + qn * 512:
                                                        ic * 1024 + (qn + 1) * 512],
                                            start=(ic == 0), stop=(ic == 7))
                    mPE2.inc(last)
            blk.tensor(_pek)

            ev0 = evac_n

            def _dvk(v, kept=kept, pe2_base=pe2_base, ev0=ev0):
                for idx, p in enumerate(kept):
                    v.wait_ge(mPE2.h, pe2_base + idx + 1)
                    ei = ev0 + idx
                    if ei >= 2:
                        v.wait_ge(sG.h, evac_wr[ei - 2])
                    mDVE2.inc(v.tensor_copy(kv_ev[ei % 2][:, :], ktps[:, :]))
            blk.vector(_dvk)

            def _gwk(g, kept=kept, bb=bb, dve2k_base=dve2k_base, ev0=ev0):
                for idx, p in enumerate(kept):
                    g.wait_ge(mDVE2.h, dve2k_base + idx + 1)
                    ei = ev0 + idx
                    g.dma_start(
                        out=bass.AP(kv_scr, (p * 64 + 8 * bb) * 128 * 256,
                                    [[256, 128], [128 * 256, 8], [1, 128]]),
                        in_=bass.AP(kv_ev[ei % 2], 0, [[1024, 128], [128, 8], [1, 128]])
                    ).then_inc(sG.h, 16)
                    sG.n += 16
                    assert evac_wr[ei] == sG.n
            blk.gpsimd(_gwk)
            evac_n += npk
            gk_count[0] += npk

            pe2v_base = mPE2.n
            dve2v_base = mDVE2.n

            nv_prev = gv_count[0]

            def _pev(t, bb=bb, pmin=pmin, npk=npk, nv_prev=nv_prev):
                for tt in range(8):
                    gv = nv_prev + tt
                    if gv >= 2:
                        t.wait_ge(mDVE2.h, vev_vals[gv - 2])
                    ps = vtps[tt % 2]
                    last = None
                    for ic in range(8):
                        lhsT = xnT[bb % 2][:, ic * 1024 + tt * 128: ic * 1024 + tt * 128 + 128]
                        rem = npk * 128
                        o = 0
                        while rem > 0:
                            n = min(512, rem)
                            last = t.matmul(
                                ps[:, pmin * 128 + o: pmin * 128 + o + n], lhsT,
                                wqkvT_sb[:, ic * 3072 + 2048 + pmin * 128 + o:
                                         ic * 3072 + 2048 + pmin * 128 + o + n],
                                start=(ic == 0), stop=(ic == 7))
                            o += n
                            rem -= n
                    mPE2.inc(last)
            blk.tensor(_pev)

            ev0 = evac_n

            def _dvv(v, pe2v_base=pe2v_base, ev0=ev0, pmin=pmin, npk=npk):
                for tt in range(8):
                    v.wait_ge(mPE2.h, pe2v_base + tt + 1)
                    ei = ev0 + tt
                    if ei >= 2:
                        v.wait_ge(sG.h, evac_wr[ei - 2])
                    mDVE2.inc(v.tensor_copy(
                        kv_ev[ei % 2][:, pmin * 128:(pmin + npk) * 128],
                        vtps[tt % 2][:, pmin * 128:(pmin + npk) * 128]))
            blk.vector(_dvv)

            def _gwv(g, kept=kept, bb=bb, dve2v_base=dve2v_base, ev0=ev0):
                for tt in range(8):
                    g.wait_ge(mDVE2.h, dve2v_base + tt + 1)
                    kt = 8 * bb + tt
                    ei = ev0 + tt
                    for p in kept:
                        g.dma_start(
                            out=bass.AP(kv_scr, (p * 64 + kt) * 128 * 256 + 128,
                                        [[256, 128], [1, 128]]),
                            in_=bass.AP(kv_ev[ei % 2], p * 128, [[1024, 128], [1, 128]])
                        ).then_inc(sG.h, 16)
                        sG.n += 16
                    assert evac_wr[ei] == sG.n
            blk.gpsimd(_gwv)
            evac_n += 8
            gv_count[0] += 8

        blk.gpsimd(lambda g: g.wait_ge(sG.h, sG.n))
        if debug:
            def _dbgkv(sy):
                sy.wait_ge(sG.h, sG.n)
                sy.dma_start(out=kv_ev[0][:, 0:256],
                             in_=bass.AP(kv_scr, 0, [[256, 128], [1, 256]])
                             ).then_inc(sS.h, 16)
                sS.n += 16
                sy.wait_ge(sS.h, sS.n)
                sy.dma_start(out=dbg["kv0"].ap(), in_=kv_ev[0][:, 0:256]).then_inc(sS.h, 16)
                sS.n += 16
                sy.wait_ge(sS.h, sS.n)
            blk.sync(_dbgkv)

    sub1b.close()
    p1.close()

    attn2 = ExitStack()
    biasseq_sb = attn2.enter_context(nc.sbuf_tensor("biasseq_sb", [128, 2 * NSLOT], F32))
    kvidx_sb = attn2.enter_context(nc.sbuf_tensor("kvidx_sb", [128, NSLOT], I32))
    woT_sb = attn2.enter_context(nc.sbuf_tensor("woT_sb", [128, 8 * 1024], BF16))
    kvbuf = tuple(attn2.enter_context(nc.sbuf_tensor(f"kvbuf{i}", [128, 256], BF16))
                  for i in range(8))
    pts = tuple(attn2.enter_context(nc.sbuf_tensor(f"pt{i}", [128, 1024], BF16))
                for i in range(8))
    mkbuf = tuple(attn2.enter_context(nc.sbuf_tensor(f"mk{i}", [128, QB], BF16))
                  for i in range(8))
    dbc = attn2.enter_context(nc.sbuf_tensor("dbc", [128, QB], F32))
    u_sb = attn2.enter_context(nc.sbuf_tensor("u_sb", [128, 8 * QB], BF16))
    xo_buf = (attn2.enter_context(nc.sbuf_tensor("xo0", [128, 1024], F32)),
              attn2.enter_context(nc.sbuf_tensor("xo1", [128, 1024], F32)))
    x2m = attn2.enter_context(nc.sbuf_tensor("x2m", [128, 4 * 1024], F32))
    xn2_tokm = attn2.enter_context(nc.sbuf_tensor("xn2_tokm", [128, 4 * 1024], BF16))

    with nc.Block("p2pre") as blk:
        def _g(g):
            for dst, src_ in ((biasseq_sb, biasseq_in), (kvidx_sb, kvidx_in)):
                g.dma_start(out=dst[:, :], in_=src_.ap()).then_inc(sG.h, 16)
                sG.n += 16
            g.dma_start(
                out=bass.AP(woT_sb, 0, [[8 * 1024, 128], [1024, 8], [1, 1024]]),
                in_=bass.AP(woT, 0, [[1024, 128], [128 * 1024, 8], [1, 1024]])
            ).then_inc(sG.h, 16)
            sG.n += 16
            g.wait_ge(sG.h, sG.n)
        blk.gpsimd(_g)

    xn2_tr_done = (0, 0)
    fls_target = {}

    if debug:
        with nc.Block("p2dbgtbl") as blk:
            def _gd(g):
                g.dma_start(out=dbg["qt2"].ap(), in_=qt_sb[:, :]).then_inc(sG.h, 16)
                sG.n += 16
                g.dma_start(out=dbg["tbl"].ap()[:, 0:8], in_=kvidx_sb[:, 0:8]).then_inc(sG.h, 16)
                g.dma_start(out=dbg["tbl"].ap()[:, 8:16], in_=kvidx_sb[:, 0:8]).then_inc(sG.h, 16)
                g.dma_start(out=dbg["bsq"].ap(), in_=biasseq_sb[:, 0:8]).then_inc(sG.h, 16)
                g.dma_start(out=dbg["sps"].ap()[:, 0:8], in_=biasseq_sb[:, 0:8]).then_inc(sG.h, 16)
                sG.n += 64
                g.wait_ge(sG.h, sG.n)
            blk.gpsimd(_gd)

    for m in range(NM):
        with nc.Block(f"p2m{m}") as blk, \
                nc.psum_tensor(f"sps{m}a", [128, 1024], F32) as sps0, \
                nc.psum_tensor(f"sps{m}b", [128, 1024], F32) as sps1, \
                nc.psum_tensor(f"ups{m}a", [128, QB], F32) as u_ps0, \
                nc.psum_tensor(f"ups{m}b", [128, QB], F32) as u_ps1, \
                nc.psum_tensor(f"dps{m}a", [128, QB], F32) as d_ps0, \
                nc.psum_tensor(f"dps{m}b", [128, QB], F32) as d_ps1:
            s_ps = (sps0, sps1)
            u_pst = (u_ps0, u_ps1)
            d_pst = (d_ps0, d_ps1)

            for p in range(NPAIR):
                pslots = [sl for sl in SLOTS if sl[1] == m and sl[2] == p]
                pair_idx = m * NPAIR + p

                def _gkv(g, pslots=pslots, p=p,
                         dkv=None if not (debug and m == 0 and p == 0) else (sS.n + 16)):
                    for (s, _m, _p, t, T_, masked) in pslots:
                        if s >= 8:
                            g.wait_ge(AVD.h, s - 7)
                        if dkv is not None and s == 2:
                            g.wait_ge(sS.h, dkv)
                        g.indirect_dma_start(
                            kvbuf[s % 8][:, :], None,
                            bass.AP(kv_scr, 0, [[256, 128], [1, 256]]),
                            IndirectOffsetOnAxis(kvidx_sb[:, s:s + 1], 0)
                        ).then_inc(KVL.h, 16)
                        KVL.n += 16
                        if p == 0 and t == 1:
                            g.dma_start(out=bass.AP(flush_scr, 0, [[1536, 128], [1, 256]]),
                                        in_=kvbuf[(s - 1) % 8][:, :]).then_inc(FLS.h, 16)
                            g.dma_start(out=bass.AP(flush_scr, 256, [[1536, 128], [1, 256]]),
                                        in_=kvbuf[s % 8][:, :]).then_inc(FLS.h, 16)
                            FLS.n += 32
                blk.gpsimd(_gkv)

                def _gmsk(sy, pslots=pslots):
                    for (s, _m, _p, t, T_, masked) in pslots:
                        if masked:
                            c = MSK_CNT[s]
                            if c >= 9:
                                sy.wait_ge(MSK.h, c - 8)
                            sy.dma_start(out=mkbuf[c % 8][:, :],
                                         in_=maskseq_in.ap()[c - 1, :, :]
                                         ).then_inc(MKL.h, 16)
                            MKL.n += 16
                blk.sync(_gmsk)
                if p == 0:
                    fls_target[m] = FLS.n

                def _pe(te, pslots=pslots, p=p, m=m, pair_idx=pair_idx):
                    qtp = qt_sb[:, p * 2048 + m * QB: p * 2048 + m * QB + QB]

                    u_ps = u_pst[pair_idx % 2]
                    d_ps = d_pst[pair_idx % 2]

                    def emit_avd(sl):
                        (s, _m, _p, t, T_, masked) = sl
                        if t == 0 and pair_idx >= 2:
                            te.wait_ge(UEV.h, pair_idx - 1)
                        if masked:
                            te.wait_ge(MSK.h, MSK_CNT[s])
                        else:
                            te.wait_ge(EXD.h, 2 * s + 1)
                        kb = kvbuf[s % 8]
                        pt = pts[s % 8]
                        st_, sp_ = (t == 0), (t == T_ - 1)
                        te.matmul(u_ps[0:64, :], kb[:, 128:192], pt[:, 0:QB],
                                  start=st_, stop=sp_, tile_position=(0, 0))
                        if not masked:
                            te.wait_ge(EXD.h, 2 * s + 2)
                        te.matmul(u_ps[64:128, :], kb[:, 192:256], pt[:, QB:2 * QB],
                                  start=st_, stop=sp_, tile_position=(0, 64))
                        te.matmul(d_ps[0:64, :], ones64b[:, :], pt[:, 0:QB],
                                  start=st_, stop=sp_, tile_position=(0, 0))
                        AVD.inc(te.matmul(d_ps[64:128, :], ones64b[:, :], pt[:, QB:2 * QB],
                                          start=st_, stop=sp_, tile_position=(0, 64)))

                    for idx, sl in enumerate(pslots):
                        (s, _m, _p, t, T_, masked) = sl
                        te.wait_ge(KVL.h, 16 * (s + 1))
                        if idx <= 1 and p == 0:
                            te.wait_ge(FLS.h, fls_target[m])
                        if s >= 2:
                            te.wait_ge(EXD.h, 2 * s - 2)
                        kb = kvbuf[s % 8]
                        te.matmul(s_ps[s % 2][:, 0:QB], kb[0:64, 0:128], qtp[0:64, :],
                                  start=True, stop=True, tile_position=(0, 0))
                        SMM.inc(te.matmul(s_ps[s % 2][:, QB:2 * QB], kb[64:128, 0:128],
                                          qtp[64:128, :],
                                          start=True, stop=True, tile_position=(64, 0)))
                        if idx >= 1:
                            emit_avd(pslots[idx - 1])
                    emit_avd(pslots[-1])
                blk.tensor(_pe)

                dump0_val = (sS.n + 32) if (debug and m == 0 and p == 0) else None

                def _act(act, pslots=pslots, p=p, dump0_val=dump0_val):
                    for i_, (s, _m, _p, t, T_, masked) in enumerate(pslots):
                        act.wait_ge(SMM.h, s + 1 + (1 if (i_ == 0 and p == 0) else 0))
                        if dump0_val is not None and s == 2:
                            act.wait_ge(sS.h, dump0_val)
                        if s >= 8:
                            act.wait_ge(AVD.h, s - 7)
                        pt = pts[s % 8]
                        EXD.inc(act.activation(pt[:, 0:QB], s_ps[s % 2][:, 0:QB], AF.Exp,
                                               bias=biasseq_sb[:, 2 * s:2 * s + 1],
                                               scale=0.125))
                        EXD.inc(act.activation(pt[:, QB:2 * QB], s_ps[s % 2][:, QB:2 * QB],
                                               AF.Exp,
                                               bias=biasseq_sb[:, 2 * s + 1:2 * s + 2],
                                               scale=0.125))
                blk.scalar(_act)

                def _dve(v, pslots=pslots, p=p, pair_idx=pair_idx, m=m):
                    for (s, _m, _p, t, T_, masked) in pslots:
                        if masked:
                            c = MSK_CNT[s]
                            v.wait_ge(EXD.h, 2 * s + 2)
                            v.wait_ge(MKL.h, 16 * min(c + 1, MSK_MAX_OF_MP[(_m, _p)]))
                            pt = pts[s % 8]
                            v.tensor_mul(pt[:, 0:QB], pt[:, 0:QB], mkbuf[c % 8][:, :])
                            MSK.inc(v.tensor_mul(pt[:, QB:2 * QB], pt[:, QB:2 * QB],
                                                 mkbuf[c % 8][:, :]))
                    if debug and m == 0 and p == 1:
                        v.wait_ge(sS.h, sS.n)
                    v.wait_ge(AVD.h, pslots[-1][0] + 1)
                    mDVE.inc(v.tensor_copy(dbc[:, :], d_pst[pair_idx % 2][:, :]))
                    v.wait_ge(mDVE.h, mDVE.n)
                    if debug and m == 0 and p == 0:
                        mDVE.inc(v.tensor_copy(st_scr2[:, 0:QB], u_pst[pair_idx % 2][:, :]))
                        v.wait_ge(mDVE.h, mDVE.n)
                    mDVE.inc(v.reciprocal(dbc[:, :], dbc[:, :]))
                    v.wait_ge(mDVE.h, mDVE.n)
                    UEV.inc(v.tensor_mul(u_sb[:, p * QB:(p + 1) * QB],
                                         u_pst[pair_idx % 2][:, :], dbc[:, :]))
                blk.vector(_dve)

                if debug and m == 0 and p == 0:
                    def _dumpkv(sy):
                        sy.wait_ge(KVL.h, 16)
                        sy.dma_start(out=dbg["kv0"].ap(),
                                     in_=kvbuf[0][:, :]).then_inc(sS.h, 16)
                        sS.n += 16
                    blk.sync(_dumpkv)

                    def _dump0(sy):
                        sy.wait_ge(MSK.h, 1)
                        sy.dma_start(out=dbg["pt"].ap()[0, :, :],
                                     in_=pts[0][:, :]).then_inc(sS.h, 16)
                        sS.n += 16
                        assert sS.n == dump0_val
                    blk.sync(_dump0)

                if debug and m == 0 and p == 0 and False:
                    def _dumpp(sy):
                        for si in range(3):
                            sy.wait_ge(MSK.h, si + 1)
                            sy.dma_start(out=dbg["pt"].ap()[si, :, :],
                                         in_=pts[si % 2][:, :]).then_inc(sS.h, 16)
                            sy.dma_start(out=dbg["mk"].ap()[si, :, :],
                                         in_=mkbuf[(si + 1) % 2][:, :]).then_inc(sS.h, 16)
                            sS.n += 32
                    blk.sync(_dumpp)
                    ptdump_val = sS.n


                if debug and m == 0 and p == 0 and False:
                    uev_dump = UEV.n
                    dve_dump = mDVE.n

                    def _dump(sy):
                        sy.wait_ge(UEV.h, uev_dump)
                        sy.wait_ge(mDVE.h, dve_dump)
                        sy.dma_start(out=dbg["dbc"].ap(), in_=dbc[:, :]).then_inc(sS.h, 16)
                        sy.dma_start(out=dbg["upr"].ap(), in_=st_scr2[:, 0:QB]).then_inc(sS.h, 16)
                        sS.n += 32
                    blk.sync(_dump)

        with nc.Block(f"p2wo{m}") as blk, \
                nc.psum_tensor(f"atps0_{m}", [128, 1024], F32) as atps0, \
                nc.psum_tensor(f"atps1_{m}", [128, 1024], F32) as atps1:
            atps = (atps0, atps1)
            uev_need = UEV.n
            x2d_base = X2D.n
            wom_base = WOM.n
            sS_base = sS.n
            NTT = QB // 128

            def _syxo(sy, m=m):
                for tt in range(NTT):
                    if tt >= 2:
                        sy.wait_ge(X2D.h, x2d_base + tt - 1)
                    sy.dma_start(out=xo_buf[tt % 2][:, :],
                                 in_=bass.AP(x_own, (m * QB + tt * 128) * D,
                                             [[1024, 128], [1, 1024]])
                                 ).then_inc(sS.h, 16)
                    sS.n += 16
            blk.sync(_syxo)

            def _pewo(te, m=m):
                te.wait_ge(UEV.h, uev_need)
                for tt in range(NTT):
                    if tt >= 2:
                        te.wait_ge(X2D.h, x2d_base + tt - 1)
                    ps = atps[tt % 2]
                    last = None
                    for p in range(NPAIR):
                        lhsT = u_sb[:, p * QB + tt * 128: p * QB + tt * 128 + 128]
                        for on in range(2):
                            last = te.matmul(ps[:, on * 512:(on + 1) * 512], lhsT,
                                             woT_sb[:, p * 1024 + on * 512:
                                                    p * 1024 + (on + 1) * 512],
                                             start=(p == 0), stop=(p == NPAIR - 1))
                    WOM.inc(last)
            blk.tensor(_pewo)

            def _dvx2(v):
                for tt in range(NTT):
                    v.wait_ge(WOM.h, wom_base + tt + 1)
                    v.wait_ge(sS.h, sS_base + 16 * (tt + 1))
                    X2D.inc(v.tensor_add(x2m[:, tt * 1024:(tt + 1) * 1024],
                                         atps[tt % 2][:, :], xo_buf[tt % 2][:, :]))
            blk.vector(_dvx2)

            def _syx2(sy, m=m):
                for tt in range(NTT):
                    sy.wait_ge(X2D.h, x2d_base + tt + 1)
                    sy.dma_start(out=bass.AP(x2_scr, (m * QB + tt * 128) * D,
                                             [[1024, 128], [1, 1024]]),
                                 in_=x2m[:, tt * 1024:(tt + 1) * 1024]
                                 ).then_inc(sS.h, 16)
                    sS.n += 16
            blk.sync(_syx2)

            apply_at = emit_ln(blk, NTT, x2m, xn2_tokm, X2D, x2d_base + NTT)

            trw_base = mPE2.n
            trd_base = mDVE2.n

            def _trxpe(t, m=m, apply_at=apply_at):
                t.wait_ge(mACT.h, apply_at)
                for tt in range(NTT):
                    if tt >= 2:
                        t.wait_ge(mDVE2.h, trd_base + tt - 1)
                    ps = atps[tt % 2]
                    last = None
                    for fc in range(8):
                        last = t.matmul(ps[:, fc * 128:(fc + 1) * 128],
                                        xn2_tokm[:, tt * 1024 + fc * 128:
                                                 tt * 1024 + fc * 128 + 128],
                                        ident_b[:, :], start=True, stop=True)
                    mPE2.inc(last)
            blk.tensor(_trxpe)

            def _trxev(v, m=m):
                for tt in range(NTT):
                    v.wait_ge(mPE2.h, trw_base + tt + 1)
                    ps = atps[tt % 2]
                    last = None
                    for fc in range(8):
                        last = v.tensor_copy(
                            xn2T_sb[:, fc * 2048 + m * QB + tt * 128:
                                    fc * 2048 + m * QB + tt * 128 + 128],
                            ps[:, fc * 128:(fc + 1) * 128])
                    mDVE2.inc(last)
            blk.vector(_trxev)

            if debug:
                def _dbgm(sy, m=m):
                    sy.dma_start(out=dbg["u"].ap()[m, :, :], in_=u_sb[:, :]).then_inc(sS.h, 16)
                    sy.dma_start(out=dbg["x2"].ap()[m, :, :], in_=x2m[:, :]).then_inc(sS.h, 16)
                    sS.n += 32
                    sy.wait_ge(sS.h, sS.n)
                blk.sync(_dbgm)

    attn2.close()
    qts.close()


    ffnA = ExitStack()
    a_sb = ffnA.enter_context(nc.sbuf_tensor("a_sb", [128, 32 * 2048], BF16))

    with nc.Block("p3a") as blk, \
            nc.sbuf_tensor("w1b0", [128, 1024], BF16) as w1b0, \
            nc.sbuf_tensor("w1b1", [128, 1024], BF16) as w1b1, \
            nc.sbuf_tensor("w2b0", [128, 1024], BF16) as w2b0, \
            nc.sbuf_tensor("w2b1", [128, 1024], BF16) as w2b1, \
            nc.sbuf_tensor("t1_sb", [128, 2048], BF16) as t1_sb, \
            nc.psum_tensor("aps1", [128, 2048], F32) as aps1, \
            nc.psum_tensor("aps2", [128, 2048], F32) as aps2:
        w1b = (w1b0, w1b1)
        w2b = (w2b0, w2b1)
        g_base = sG.n
        fmm_base = FMM.n
        ts1_base = TS1.n
        adn_base = ADN.n

        def _g(g):
            for oc in range(32):
                if oc >= 2:
                    g.wait_ge(FMM.h, fmm_base + 2 * (oc - 2) + 2)
                for blk_in, dst in ((w1blk, w1b[oc % 2]), (w2blk, w2b[oc % 2])):
                    g.dma_start(
                        out=bass.AP(dst, 0, [[1024, 128], [128, 8], [1, 128]]),
                        in_=bass.AP(blk_in, oc * D * 128,
                                    [[128, 128], [128 * 128, 8], [1, 128]])
                    ).then_inc(sG.h, 16)
                    sG.n += 16
        blk.gpsimd(_g)

        def _pe(te):
            for oc in range(32):
                te.wait_ge(sG.h, g_base + 32 * (oc + 1))
                if oc >= 1:
                    te.wait_ge(TS1.h, ts1_base + oc)
                    te.wait_ge(ADN.h, adn_base + oc)
                last = None
                for ic in range(8):
                    lhsT = w1b[oc % 2][:, ic * 128:(ic + 1) * 128]
                    for tb in range(4):
                        last = te.matmul(aps1[:, tb * 512:(tb + 1) * 512], lhsT,
                                         xn2T_sb[:, ic * 2048 + tb * 512:
                                                 ic * 2048 + (tb + 1) * 512],
                                         start=(ic == 0), stop=(ic == 7))
                FMM.inc(last)
                last = None
                for ic in range(8):
                    lhsT = w2b[oc % 2][:, ic * 128:(ic + 1) * 128]
                    for tb in range(4):
                        last = te.matmul(aps2[:, tb * 512:(tb + 1) * 512], lhsT,
                                         xn2T_sb[:, ic * 2048 + tb * 512:
                                                 ic * 2048 + (tb + 1) * 512],
                                         start=(ic == 0), stop=(ic == 7))
                FMM.inc(last)
        blk.tensor(_pe)

        def _act(act):
            for oc in range(32):
                act.wait_ge(FMM.h, fmm_base + 2 * oc + 1)
                if oc >= 1:
                    act.wait_ge(ADN.h, adn_base + oc)
                TS1.inc(act.activation(t1_sb[:, :], aps1[:, :], AF.Silu,
                                       scale=s1_sb[:, oc:oc + 1]))
        blk.scalar(_act)

        def _dv(v):
            for oc in range(32):
                v.wait_ge(FMM.h, fmm_base + 2 * oc + 2)
                v.wait_ge(TS1.h, ts1_base + oc + 1)
                ADN.inc(v.scalar_tensor_tensor(a_sb[:, oc * 2048:(oc + 1) * 2048],
                                               aps2[:, :], s2_sb[:, oc:oc + 1],
                                               t1_sb[:, :], ALU.mult, ALU.mult))
        blk.vector(_dv)

        if debug:
            adn_need = ADN.n

            def _dbga(sy):
                sy.wait_ge(ADN.h, adn_need)
                sy.dma_start(out=dbg["a"].ap(), in_=a_sb[:, :]).then_inc(sS.h, 16)
                sS.n += 16
                sy.wait_ge(sS.h, sS.n)
            blk.sync(_dbga)

    with nc.Block("p3b") as blk, \
            nc.sbuf_tensor("h1b0", [128, 4096], BF16) as h1b0, \
            nc.sbuf_tensor("h1b1", [128, 4096], BF16) as h1b1, \
            nc.sbuf_tensor("bev0", [128, 2048], BF16) as bev0, \
            nc.sbuf_tensor("bev1", [128, 2048], BF16) as bev1, \
            nc.psum_tensor("bps0", [128, 2048], F32) as bps0, \
            nc.psum_tensor("bps1", [128, 2048], F32) as bps1:
        h1b = (h1b0, h1b1)
        bev = (bev0, bev1)
        bps = (bps0, bps1)
        g_base = sG.n
        fmm_base = FMM.n
        ts1_base = TS1.n
        s_base = sS.n

        def _g(g):
            for oc in range(32):
                if oc >= 2:
                    g.wait_ge(FMM.h, fmm_base + oc - 1)
                g.dma_start(
                    out=bass.AP(h1b[oc % 2], 0, [[4096, 128], [128, 32], [1, 128]]),
                    in_=bass.AP(h1blk, oc * HID * 128,
                                [[128, 128], [128 * 128, 32], [1, 128]])
                ).then_inc(sG.h, 16)
                sG.n += 16
        blk.gpsimd(_g)

        def _pe(te):
            for oc in range(32):
                te.wait_ge(sG.h, g_base + 16 * (oc + 1))
                if oc >= 2:
                    te.wait_ge(TS1.h, ts1_base + oc - 1)
                last = None
                for ic in range(32):
                    lhsT = h1b[oc % 2][:, ic * 128:(ic + 1) * 128]
                    for tb in range(4):
                        last = te.matmul(bps[oc % 2][:, tb * 512:(tb + 1) * 512], lhsT,
                                         a_sb[:, ic * 2048 + tb * 512:
                                              ic * 2048 + (tb + 1) * 512],
                                         start=(ic == 0), stop=(ic == 31))
                FMM.inc(last)
        blk.tensor(_pe)

        def _act(act):
            for oc in range(32):
                act.wait_ge(FMM.h, fmm_base + oc + 1)
                if oc >= 2:
                    act.wait_ge(sS.h, s_base + 16 * (oc - 1))
                TS1.inc(act.activation(bev[oc % 2][:, :], bps[oc % 2][:, :], AF.Gelu,
                                       scale=h1s_sb[:, oc:oc + 1]))
        blk.scalar(_act)

        def _sy(sy):
            for oc in range(32):
                sy.wait_ge(TS1.h, ts1_base + oc + 1)
                sy.dma_start(out=bass.AP(b_scr, oc * 128 * 2048, [[2048, 128], [1, 2048]]),
                             in_=bev[oc % 2][:, :]).then_inc(sS.h, 16)
                sS.n += 16
        blk.sync(_sy)
        bscr_q = [s_base + 16 * 8 * (q + 1) for q in range(4)]

    ffnA.close()


    p4 = ExitStack()
    c_sb = p4.enter_context(nc.sbuf_tensor("c_sb", [128, 32 * 1024], BF16))

    for th in range(2):
        with nc.Block(f"p4c{th}") as blk, \
                nc.sbuf_tensor(f"bb_all{th}", [128, 32 * 1024], BF16) as bb_all, \
                nc.sbuf_tensor(f"h2b0{th}", [128, 4096], BF16) as h2b0, \
                nc.sbuf_tensor(f"h2b1{th}", [128, 4096], BF16) as h2b1, \
                nc.psum_tensor(f"cps0{th}", [128, 1024], F32) as cps0, \
                nc.psum_tensor(f"cps1{th}", [128, 1024], F32) as cps1:
            h2b = (h2b0, h2b1)
            cps = (cps0, cps1)
            g_base = sG.n
            bld_base = BLD.n
            cmm_base = CMM.n
            cgl_base = CGL.n

            def _syb(sy, th=th):
                for q in range(4):
                    sy.wait_ge(sS.h, bscr_q[q])
                    sy.dma_start(
                        out=bass.AP(bb_all, q * 8 * 1024,
                                    [[32 * 1024, 128], [1024, 8], [1, 1024]]),
                        in_=bass.AP(b_scr, th * 1024 + q * 8 * 128 * 2048,
                                    [[2048, 128], [128 * 2048, 8], [1, 1024]])
                    ).then_inc(BLD.h, 16)
                    BLD.n += 16
            blk.sync(_syb)

            def _g(g, th=th):
                for oc in range(32):
                    if oc >= 2:
                        g.wait_ge(CMM.h, cmm_base + oc - 1)
                    g.dma_start(
                        out=bass.AP(h2b[oc % 2], 0, [[4096, 128], [128, 32], [1, 128]]),
                        in_=bass.AP(h2blk, oc * HID * 128,
                                    [[128, 128], [128 * 128, 32], [1, 128]])
                    ).then_inc(sG.h, 16)
                    sG.n += 16
            blk.gpsimd(_g)

            def _pe(te, th=th):
                for oc in range(32):
                    te.wait_ge(sG.h, g_base + 16 * (oc + 1))
                    if oc >= 2:
                        te.wait_ge(CGL.h, cgl_base + oc - 1)
                    if oc == 1:
                        te.wait_ge(BLD.h, bld_base + 64)
                    last = None
                    for ic in range(32):
                        if oc == 0 and ic % 8 == 0:
                            te.wait_ge(BLD.h, bld_base + 16 * (ic // 8 + 1))
                        lhsT = h2b[oc % 2][:, ic * 128:(ic + 1) * 128]
                        for tb in range(2):
                            last = te.matmul(cps[oc % 2][:, tb * 512:(tb + 1) * 512],
                                             lhsT,
                                             bb_all[:, ic * 1024 + tb * 512:
                                                    ic * 1024 + (tb + 1) * 512],
                                             start=(ic == 0), stop=(ic == 31))
                    CMM.inc(last)
            blk.tensor(_pe)

            def _act(act, th=th):
                for oc in range(32):
                    act.wait_ge(CMM.h, cmm_base + oc + 1)
                    CGL.inc(act.activation(c_sb[:, oc * 1024:(oc + 1) * 1024],
                                           cps[oc % 2][:, :], AF.Gelu,
                                           scale=h2s_sb[:, oc:oc + 1]))
            blk.scalar(_act)

        with nc.Block(f"p4d{th}") as blk, \
                nc.sbuf_tensor(f"w3b0{th}", [128, 4096], BF16) as w3b0, \
                nc.sbuf_tensor(f"w3b1{th}", [128, 4096], BF16) as w3b1, \
                nc.sbuf_tensor(f"yt{th}", [128, 1024], F32) as yt_sb, \
                nc.sbuf_tensor(f"x2th{th}", [128, 8 * 1024], F32) as x2th, \
                nc.sbuf_tensor(f"ot0{th}", [128, 128], F32) as ot0, \
                nc.sbuf_tensor(f"ot1{th}", [128, 128], F32) as ot1, \
                nc.psum_tensor(f"d4ps{th}", [128, 1024], F32) as dps0, \
                nc.psum_tensor(f"trp0{th}", [128, 128], F32) as trp0, \
                nc.psum_tensor(f"trp1{th}", [128, 128], F32) as trp1:
            w3b = (w3b0, w3b1)
            ot = (ot0, ot1)
            trp = (trp0, trp1)
            g_base = sG.n
            dm_base = DM2.n
            yev_base = YEV.n
            trd_base = TRD.n
            oad_base = OAD.n
            x2l_base = X2L.n
            odm_base = ODM.n

            def _g(g, th=th):
                for oc in range(8):
                    if oc >= 2:
                        g.wait_ge(DM2.h, dm_base + oc - 1)
                    g.dma_start(
                        out=bass.AP(w3b[oc % 2], 0, [[4096, 128], [128, 32], [1, 128]]),
                        in_=bass.AP(w3blk, oc * HID * 128,
                                    [[128, 128], [128 * 128, 32], [1, 128]])
                    ).then_inc(sG.h, 16)
                    sG.n += 16
            blk.gpsimd(_g)

            def _pe(te, th=th):
                for oc in range(8):
                    te.wait_ge(sG.h, g_base + 16 * (oc + 1))
                    if oc >= 1:
                        te.wait_ge(YEV.h, yev_base + oc)
                    last = None
                    for ic in range(32):
                        lhsT = w3b[oc % 2][:, ic * 128:(ic + 1) * 128]
                        for tb in range(2):
                            last = te.matmul(dps0[:, tb * 512:(tb + 1) * 512], lhsT,
                                             c_sb[:, ic * 1024 + tb * 512:
                                                  ic * 1024 + (tb + 1) * 512],
                                             start=(ic == 0), stop=(ic == 31))
                    DM2.inc(last)
                    te.wait_ge(YEV.h, yev_base + oc + 1)
                    for tt in range(8):
                        k = oc * 8 + tt
                        if k >= 2:
                            te.wait_ge(OAD.h, oad_base + k - 1)
                        TRD.inc(te.transpose(trp[k % 2][:, :],
                                             yt_sb[:, tt * 128:(tt + 1) * 128],
                                             ident_sb[:, :]))
            blk.tensor(_pe)

            def _dv(v, th=th):
                v.wait_ge(X2L.h, x2l_base + 16)
                for oc in range(8):
                    v.wait_ge(DM2.h, dm_base + oc + 1)
                    YEV.inc(v.tensor_scalar_mul(yt_sb[:, :], dps0[:, :],
                                                w3s_sb[:, oc:oc + 1]))
                    for tt in range(8):
                        k = oc * 8 + tt
                        v.wait_ge(TRD.h, trd_base + k + 1)
                        if k >= 2:
                            v.wait_ge(ODM.h, odm_base + 16 * (k - 1))
                        OAD.inc(v.tensor_add(ot[k % 2][:, :], trp[k % 2][:, :],
                                             x2th[:, tt * 1024 + oc * 128:
                                                  tt * 1024 + oc * 128 + 128]))
            blk.vector(_dv)

            def _syio(sy, th=th):
                sy.dma_start(
                    out=bass.AP(x2th, 0, [[8 * 1024, 128], [1024, 8], [1, 1024]]),
                    in_=bass.AP(x2_scr, th * 1024 * D,
                                [[1024, 128], [128 * 1024, 8], [1, 1024]])
                ).then_inc(X2L.h, 16)
                X2L.n += 16
                for k in range(64):
                    oc, tt = divmod(k, 8)
                    sy.wait_ge(OAD.h, oad_base + k + 1)
                    sy.dma_start(out=bass.AP(out, (th * 1024 + tt * 128) * D + oc * 128,
                                             [[1024, 128], [1, 128]]),
                                 in_=ot[k % 2][:, :]).then_inc(ODM.h, 16)
                    ODM.n += 16
            blk.sync(_syio)

    with nc.Block("pfin") as blk:
        blk.sync(lambda sy: sy.wait_ge(ODM.h, ODM.n))

    p4.close()
    es.close()
    return nc


_NC_CACHE = {}


def _get_nc(debug=False):
    key = bool(debug)
    if key not in _NC_CACHE:
        _NC_CACHE[key] = build_nc(debug=debug)
    return _NC_CACHE[key]


def _prep_inputs(inputs):
    x = np.asarray(inputs["L0_attn_x"], dtype=np.float32)
    wqkv = np.asarray(inputs["L0_attn_W_qkv"], dtype=np.float32)
    wo = np.asarray(inputs["L0_attn_W_o"], dtype=np.float32)
    w1 = np.asarray(inputs["L0_ffn_w1_w"], dtype=np.float32)
    s1 = np.asarray(inputs["L0_ffn_w1_s"], dtype=np.float32)
    w2 = np.asarray(inputs["L0_ffn_w2_w"], dtype=np.float32)
    s2 = np.asarray(inputs["L0_ffn_w2_s"], dtype=np.float32)
    h1 = np.asarray(inputs["L0_ffn_h1_w"], dtype=np.float32)
    h1s = np.asarray(inputs["L0_ffn_h1_s"], dtype=np.float32)
    h2 = np.asarray(inputs["L0_ffn_h2_w"], dtype=np.float32)
    h2s = np.asarray(inputs["L0_ffn_h2_s"], dtype=np.float32)
    w3 = np.asarray(inputs["L0_ffn_w3_w"], dtype=np.float32)
    w3s = np.asarray(inputs["L0_ffn_w3_s"], dtype=np.float32)

    def cont(a):
        return np.ascontiguousarray(a)

    shared = {
        "wqkvT": cont(wqkv.T),
        "woT": cont(wo.T),
        "w1blk": cont(w1.T.reshape(D, 32, 128).transpose(1, 0, 2)),
        "w2blk": cont(w2.T.reshape(D, 32, 128).transpose(1, 0, 2)),
        "h1blk": cont(h1.T.reshape(HID, 32, 128).transpose(1, 0, 2)),
        "h2blk": cont(h2.T.reshape(HID, 32, 128).transpose(1, 0, 2)),
        "w3blk": cont(w3.T.reshape(HID, 8, 128).transpose(1, 0, 2)),
        "s1m": cont(s1.reshape(32, 128).T),
        "s2m": cont(s2.reshape(32, 128).T),
        "h1sm": cont(h1s.reshape(32, 128).T),
        "h2sm": cont(h2s.reshape(32, 128).T),
        "w3sm": cont(w3s.reshape(8, 128).T),
        "ident": np.eye(128, dtype=np.float32),
    }
    in_maps = []
    for c in range(NCORES):
        b, j = c // 4, c % 4
        kvidx, midx, biasseq = _host_tables(j)
        m = dict(shared)
        m["x_full"] = cont(x[b])
        m["x_own"] = cont(np.concatenate(
            [x[b, QB * (4 * mm + j):QB * (4 * mm + j) + QB] for mm in range(NM)], axis=0))
        m["biasseq"] = cont(biasseq)
        m["kvidx"] = kvidx
        m["midx"] = midx
        m["maskseq"] = _host_maskseq(j)
        in_maps.append(m)
    return in_maps


def kernel(**inputs):
    nc = _get_nc(debug=False)
    in_maps = _prep_inputs(inputs)
    res = run_bass_kernel_spmd(nc, in_maps, core_ids=list(range(NCORES)))
    final = np.empty((B, S, D), dtype=np.float32)
    for c in range(NCORES):
        b, j = c // 4, c % 4
        o = res.results[c]["out"]
        for mm in range(NM):
            final[b, QB * (4 * mm + j):QB * (4 * mm + j) + QB] = o[QB * mm:QB * (mm + 1)]
    return final


# version 19
# speedup vs baseline: 1.0172x; 1.0001x over previous
import numpy as np
import ml_dtypes
import concourse.bass as bass
import concourse.mybir as mybir
from concourse.bass_utils import run_bass_kernel_spmd

F32 = mybir.dt.float32
BF16 = mybir.dt.bfloat16
I32 = mybir.dt.int32
AF = mybir.ActivationFunctionType
ALU = mybir.AluOpType

B, S, D, H, HD, HID = 2, 8192, 1024, 16, 64, 4096
NCORES = 8
NPAIR = 8
QB = 512
NM = 4
TRUNC = 160.0

SLOPES = [2.0 ** (-0.5 * (h + 1)) for h in range(H)]
KB = [min(S // 128, int((TRUNC / SLOPES[2 * p + 1]) // 128) + 1) for p in range(NPAIR)]
TMP = [[min(16 * m + 16, KB[p]) for p in range(NPAIR)] for m in range(NM)]
NSLOT = sum(sum(t) for t in TMP)
KV_TILE = 2 * 128 * 128


SLOTS = []
for _m in range(NM):
    for _p in range(NPAIR):
        for _t in range(TMP[_m][_p]):
            SLOTS.append((len(SLOTS), _m, _p, _t, TMP[_m][_p], _t >= 16 * _m))
_c = 0
MSK_CNT = []
for _s in SLOTS:
    _c += 1 if _s[5] else 0
    MSK_CNT.append(_c)
MSK_MAX_OF_MP = {}
for _s in SLOTS:
    if _s[5]:
        MSK_MAX_OF_MP[(_s[1], _s[2])] = MSK_CNT[_s[0]]


def _host_tables(j):
    kvidx = np.zeros((128, NSLOT), dtype=np.int32)
    midx = np.zeros((128, NSLOT), dtype=np.int32)
    biasseq = np.empty((128, 2 * NSLOT), dtype=np.float32)
    rows = np.arange(128, dtype=np.float64)
    ri = np.arange(128, dtype=np.int32)
    for (s, m, p, t, T, masked) in SLOTS:
        dstart = 16 * m + 4 * j
        live = min(KB[p], dstart + 4)
        if t < live:
            kt = t
            kvidx[:, s] = (p * 64 + kt) * 128 + ri
            for hs, h in enumerate((2 * p, 2 * p + 1)):
                biasseq[:, 2 * s + hs] = (-SLOPES[h] * (128 * kt + rows)).astype(np.float32)
            pat = (kt - dstart) if kt >= dstart else 4
        else:
            kvidx[:, s] = ri
            biasseq[:, 2 * s] = -1e30
            biasseq[:, 2 * s + 1] = -1e30
            pat = 4
        midx[:, s] = pat * 128 + ri
    return kvidx, midx, biasseq


NMASKED = MSK_CNT[-1]


def _host_maskseq(j):
    pats = _host_masks().astype(np.float32)
    seq = np.empty((NMASKED, 128, QB), dtype=np.float32)
    for (s, m, p, t, T, masked) in SLOTS:
        if not masked:
            continue
        dstart = 16 * m + 4 * j
        live = min(KB[p], dstart + 4)
        pat = ((t - dstart) if t >= dstart else 4) if t < live else 4
        seq[MSK_CNT[s] - 1] = pats[pat * 128:(pat + 1) * 128, :]
    return np.ascontiguousarray(seq.astype(ml_dtypes.bfloat16))


def _host_masks():
    pats = np.zeros((5 * 128, QB), dtype=np.float32)
    cols = np.arange(QB)[None, :]
    r = np.arange(128)[:, None]
    for d in range(4):
        pats[d * 128:(d + 1) * 128] = (cols >= 128 * d + r)
    pats[4 * 128:] = 1.0
    return np.ascontiguousarray(pats.astype(ml_dtypes.bfloat16))


from concourse.bass_core_rust import IndirectOffsetOnAxis


class SemC:
    REGISTRY = []

    def __init__(self, es, nc, name):
        self.h = es.enter_context(nc.semaphore(name))
        self.n = 0
        SemC.REGISTRY.append(self)

    def inc(self, inst, k=1):
        inst.then_inc(self.h, k)
        self.n += k
        return self.n


def build_nc(debug=False):
    from contextlib import ExitStack
    nc = bass.Bass("TRN2", target_bir_lowering=False)
    es = ExitStack()

    x_full = nc.dram_tensor("x_full", [S, D], F32, kind="ExternalInput")
    x_own = nc.dram_tensor("x_own", [NM * QB, D], F32, kind="ExternalInput")
    wqkvT = nc.dram_tensor("wqkvT", [D, 3 * D], F32, kind="ExternalInput")
    woT = nc.dram_tensor("woT", [D, D], F32, kind="ExternalInput")
    w1blk = nc.dram_tensor("w1blk", [32, D, 128], F32, kind="ExternalInput")
    w2blk = nc.dram_tensor("w2blk", [32, D, 128], F32, kind="ExternalInput")
    h1blk = nc.dram_tensor("h1blk", [32, HID, 128], F32, kind="ExternalInput")
    h2blk = nc.dram_tensor("h2blk", [32, HID, 128], F32, kind="ExternalInput")
    w3blk = nc.dram_tensor("w3blk", [8, HID, 128], F32, kind="ExternalInput")
    s1m = nc.dram_tensor("s1m", [128, 32], F32, kind="ExternalInput")
    s2m = nc.dram_tensor("s2m", [128, 32], F32, kind="ExternalInput")
    h1sm = nc.dram_tensor("h1sm", [128, 32], F32, kind="ExternalInput")
    h2sm = nc.dram_tensor("h2sm", [128, 32], F32, kind="ExternalInput")
    w3sm = nc.dram_tensor("w3sm", [128, 8], F32, kind="ExternalInput")
    biasseq_in = nc.dram_tensor("biasseq", [128, 2 * NSLOT], F32, kind="ExternalInput")
    kvidx_in = nc.dram_tensor("kvidx", [128, NSLOT], I32, kind="ExternalInput")
    midx_in = nc.dram_tensor("midx", [128, NSLOT], I32, kind="ExternalInput")
    maskseq_in = nc.dram_tensor("maskseq", [NMASKED, 128, QB], BF16, kind="ExternalInput")
    ident_in = nc.dram_tensor("ident", [128, 128], F32, kind="ExternalInput")
    out = nc.dram_tensor("out", [NM * QB, D], F32, kind="ExternalOutput")

    kv_scr = nc.dram_tensor("kv_scr", [NPAIR * 64 * 128, 256], BF16)
    flush_scr = nc.dram_tensor("flush_scr", [128, 2 * 256 + 2 * QB], BF16)
    x2_scr = nc.dram_tensor("x2_scr", [NM * QB, D], F32)
    b_scr = nc.dram_tensor("b_scr", [HID, NM * QB], BF16)

    dbg = {}
    if debug:
        dbg["qt"] = nc.dram_tensor("dbg_qt", [128, 8 * 2048], BF16, kind="ExternalOutput")
        dbg["xnt"] = nc.dram_tensor("dbg_xnt", [128, 8 * 2048], BF16, kind="ExternalOutput")
        dbg["kv0"] = nc.dram_tensor("dbg_kv0", [128, 256], BF16, kind="ExternalOutput")
        dbg["u"] = nc.dram_tensor("dbg_u", [NM, 128, 8 * QB], BF16, kind="ExternalOutput")
        dbg["x2"] = nc.dram_tensor("dbg_x2", [NM, 128, 4 * 1024], F32, kind="ExternalOutput")
        dbg["a"] = nc.dram_tensor("dbg_a", [128, 32 * 2048], BF16, kind="ExternalOutput")
        dbg["dbc"] = nc.dram_tensor("dbg_dbc", [128, QB], F32, kind="ExternalOutput")
        dbg["upr"] = nc.dram_tensor("dbg_upr", [128, QB], F32, kind="ExternalOutput")
        dbg["pt"] = nc.dram_tensor("dbg_pt", [3, 128, 2 * QB], BF16, kind="ExternalOutput")
        dbg["mk"] = nc.dram_tensor("dbg_mk", [3, 128, QB], BF16, kind="ExternalOutput")
        dbg["tbl"] = nc.dram_tensor("dbg_tbl", [128, 16], I32, kind="ExternalOutput")
        dbg["sps"] = nc.dram_tensor("dbg_sps", [128, 1024], F32, kind="ExternalOutput")
        dbg["qt2"] = nc.dram_tensor("dbg_qt2", [128, 8 * 2048], BF16, kind="ExternalOutput")
        dbg["xblk0"] = nc.dram_tensor("dbg_xblk0", [128, 4 * 1024], F32, kind="ExternalOutput")
        dbg["xntok0"] = nc.dram_tensor("dbg_xntok0", [128, 4 * 1024], BF16, kind="ExternalOutput")
        dbg["bsq"] = nc.dram_tensor("dbg_bsq", [128, 8], F32, kind="ExternalOutput")

    def sbp(name, shape, dt):
        return es.enter_context(nc.sbuf_tensor(name, shape, dt))

    ones64 = sbp("ones64", [128, 64], F32)
    ones64b = sbp("ones64b", [128, 64], BF16)
    ident_sb = sbp("ident_sb", [128, 128], F32)
    s1_sb = sbp("s1_sb", [128, 32], F32)
    s2_sb = sbp("s2_sb", [128, 32], F32)
    h1s_sb = sbp("h1s_sb", [128, 32], F32)
    h2s_sb = sbp("h2s_sb", [128, 32], F32)
    w3s_sb = sbp("w3s_sb", [128, 8], F32)
    st_scr = sbp("st_scr", [128, 1024], F32)
    st_scr2 = sbp("st_scr2", [128, 1024], F32)
    ident_b = sbp("ident_b", [128, 128], BF16)
    st_small = sbp("st_small", [128, 64], F32)
    xn2T_sb = sbp("xn2T_sb", [128, 8 * 2048], BF16)
    STATS = {
        "ssum": st_small[:, 0:8], "ssq": st_small[:, 8:16],
        "mean": st_small[:, 16:24], "var": st_small[:, 24:32],
        "m2": st_small[:, 32:40], "rstd": st_small[:, 40:48],
        "nb": st_small[:, 48:56], "lnv": st_small[:, 56:64],
    }

    SemC.REGISTRY = []
    sS = SemC(es, nc, "sS")
    sA = SemC(es, nc, "sA")
    sG = SemC(es, nc, "sG")
    mPE = SemC(es, nc, "mPE")
    mPE2 = SemC(es, nc, "mPE2")
    mACT = SemC(es, nc, "mACT")
    mDVE = SemC(es, nc, "mDVE")
    mDVE2 = SemC(es, nc, "mDVE2")
    KVL = SemC(es, nc, "KVL")
    MKL = SemC(es, nc, "MKL")
    SMM = SemC(es, nc, "SMM")
    EXD = SemC(es, nc, "EXD")
    MSK = SemC(es, nc, "MSK")
    AVD = SemC(es, nc, "AVD")
    UEV = SemC(es, nc, "UEV")
    WOM = SemC(es, nc, "WOM")
    X2D = SemC(es, nc, "X2D")
    FLS = SemC(es, nc, "FLS")
    FMM = SemC(es, nc, "FMM")
    TS1 = SemC(es, nc, "TS1")
    ADN = SemC(es, nc, "ADN")
    BLD = SemC(es, nc, "BLD")
    CMM = SemC(es, nc, "CMM")
    CGL = SemC(es, nc, "CGL")
    DM2 = SemC(es, nc, "DM2")
    YEV = SemC(es, nc, "YEV")
    TRD = SemC(es, nc, "TRD")
    OAD = SemC(es, nc, "OAD")
    X2L = SemC(es, nc, "X2L")
    ODM = SemC(es, nc, "ODM")

    def emit_ln(blk, nt, xblk, xn_tok, ready_sem, ready_val,
                apply_sem=None, apply_val=0, apply_sem2=None, apply_val2=0,
                warm=0, post=0):
        def _act1(act):
            act.wait_ge(ready_sem.h, ready_val)
            for _ in range(warm):
                act.activation(st_scr[:, :], st_scr[:, :], AF.Square)
            last = None
            for t in range(nt):
                last = act.activation(st_scr[:, :], xblk[:, t * 1024:(t + 1) * 1024],
                                      AF.Square, accum_out=STATS["ssq"][:, t:t + 1])
            mACT.inc(last)
        blk.scalar(_act1)
        act_sq = mACT.n

        def _dve1(vec):
            vec.wait_ge(ready_sem.h, ready_val)
            for _ in range(warm):
                vec.tensor_scalar_mul(st_scr2[:, :], st_scr2[:, :], 1.0)
            for t in range(nt):
                vec.tensor_scalar(st_scr2[:, :], xblk[:, t * 1024:(t + 1) * 1024],
                                  1.0, 0.0, ALU.mult, ALU.add,
                                  accum_out=STATS["ssum"][:, t:t + 1])
            vec.wait_ge(mACT.h, act_sq)
            mDVE.inc(vec.tensor_scalar_mul(STATS["mean"][:, 0:nt], STATS["ssum"][:, 0:nt], 1.0 / D))
            vec.wait_ge(mDVE.h, mDVE.n)
            vec.tensor_scalar_mul(STATS["var"][:, 0:nt], STATS["ssq"][:, 0:nt], 1.0 / D)
            mDVE.inc(vec.tensor_mul(STATS["m2"][:, 0:nt], STATS["mean"][:, 0:nt], STATS["mean"][:, 0:nt]))
            vec.wait_ge(mDVE.h, mDVE.n)
            mDVE.inc(vec.tensor_sub(STATS["var"][:, 0:nt], STATS["var"][:, 0:nt], STATS["m2"][:, 0:nt]))
            vec.wait_ge(mDVE.h, mDVE.n)
            mDVE.inc(vec.tensor_scalar_add(STATS["var"][:, 0:nt], STATS["var"][:, 0:nt], 1e-5))
        blk.vector(_dve1)
        dve_var = mDVE.n

        def _act2(act):
            act.wait_ge(mDVE.h, dve_var)
            mACT.inc(act.activation(STATS["lnv"][:, 0:nt], STATS["var"][:, 0:nt], AF.Ln))
            act.wait_ge(mACT.h, mACT.n)
            mACT.inc(act.activation(STATS["rstd"][:, 0:nt], STATS["lnv"][:, 0:nt],
                                    AF.Exp, scale=-0.5))
        blk.scalar(_act2)
        act_rstd = mACT.n

        def _dve2(vec):
            vec.wait_ge(mACT.h, act_rstd)
            mDVE.inc(vec.tensor_mul(STATS["nb"][:, 0:nt], STATS["mean"][:, 0:nt], STATS["rstd"][:, 0:nt]))
            vec.wait_ge(mDVE.h, mDVE.n)
            mDVE.inc(vec.tensor_scalar_mul(STATS["nb"][:, 0:nt], STATS["nb"][:, 0:nt], -1.0))
        blk.vector(_dve2)
        dve_nb = mDVE.n

        def _act3(act):
            act.wait_ge(mDVE.h, dve_nb)
            if apply_sem is not None:
                act.wait_ge(apply_sem.h, apply_val)
            if apply_sem2 is not None:
                act.wait_ge(apply_sem2.h, apply_val2)
            last = None
            for t in range(nt):
                last = act.activation(xn_tok[:, t * 1024:(t + 1) * 1024],
                                      xblk[:, t * 1024:(t + 1) * 1024], AF.Identity,
                                      bias=STATS["nb"][:, t:t + 1],
                                      scale=STATS["rstd"][:, t:t + 1])
            for t in range(post):
                last = act.activation(st_scr[:, 0:1024],
                                      xn_tok[:, t * 1024:(t + 1) * 1024], AF.Square)
            mACT.inc(last)
        blk.scalar(_act3)
        return mACT.n

    def emit_ln2(blk, xa, xb, xna, xnb, ra_sem, ra_val, rb_val,
                 apply_sem=None, apply_vala=0, apply_valb=0, warm=0,
                 rb_sem=None):
        rbs = rb_sem if rb_sem is not None else ra_sem
        """Paired LN of two 512-token blocks (4 tiles each) sharing one stats
        chain (halves the exposed ACT<->DVE hop latency of emit_ln).
        Returns (mACT after A applied, mACT after B applied)."""
        def _act1(act):
            act.wait_ge(ra_sem.h, ra_val)
            for _ in range(warm):
                act.activation(st_scr[:, :], st_scr[:, :], AF.Square)
            last = None
            for t in range(4):
                last = act.activation(st_scr[:, :], xa[:, t * 1024:(t + 1) * 1024],
                                      AF.Square, accum_out=STATS["ssq"][:, t:t + 1])
            act.wait_ge(rbs.h, rb_val)
            for t in range(4):
                last = act.activation(st_scr[:, :], xb[:, t * 1024:(t + 1) * 1024],
                                      AF.Square, accum_out=STATS["ssq"][:, 4 + t:5 + t])
            mACT.inc(last)
        blk.scalar(_act1)
        act_sq = mACT.n

        def _dve1(vec):
            vec.wait_ge(ra_sem.h, ra_val)
            for _ in range(warm):
                vec.tensor_scalar_mul(st_scr2[:, :], st_scr2[:, :], 1.0)
            for t in range(4):
                vec.tensor_scalar(st_scr2[:, :], xa[:, t * 1024:(t + 1) * 1024],
                                  1.0, 0.0, ALU.mult, ALU.add,
                                  accum_out=STATS["ssum"][:, t:t + 1])
            vec.wait_ge(rbs.h, rb_val)
            for t in range(4):
                vec.tensor_scalar(st_scr2[:, :], xb[:, t * 1024:(t + 1) * 1024],
                                  1.0, 0.0, ALU.mult, ALU.add,
                                  accum_out=STATS["ssum"][:, 4 + t:5 + t])
            vec.wait_ge(mACT.h, act_sq)
            mDVE.inc(vec.tensor_scalar_mul(STATS["mean"][:, 0:8], STATS["ssum"][:, 0:8], 1.0 / D))
            vec.wait_ge(mDVE.h, mDVE.n)
            vec.tensor_scalar_mul(STATS["var"][:, 0:8], STATS["ssq"][:, 0:8], 1.0 / D)
            mDVE.inc(vec.tensor_mul(STATS["m2"][:, 0:8], STATS["mean"][:, 0:8], STATS["mean"][:, 0:8]))
            vec.wait_ge(mDVE.h, mDVE.n)
            mDVE.inc(vec.tensor_sub(STATS["var"][:, 0:8], STATS["var"][:, 0:8], STATS["m2"][:, 0:8]))
            vec.wait_ge(mDVE.h, mDVE.n)
            mDVE.inc(vec.tensor_scalar_add(STATS["var"][:, 0:8], STATS["var"][:, 0:8], 1e-5))
        blk.vector(_dve1)
        dve_var = mDVE.n

        def _act2(act):
            act.wait_ge(mDVE.h, dve_var)
            mACT.inc(act.activation(STATS["lnv"][:, 0:8], STATS["var"][:, 0:8], AF.Ln))
            act.wait_ge(mACT.h, mACT.n)
            mACT.inc(act.activation(STATS["rstd"][:, 0:8], STATS["lnv"][:, 0:8],
                                    AF.Exp, scale=-0.5))
        blk.scalar(_act2)
        act_rstd = mACT.n

        def _dve2(vec):
            vec.wait_ge(mACT.h, act_rstd)
            mDVE.inc(vec.tensor_mul(STATS["nb"][:, 0:8], STATS["mean"][:, 0:8], STATS["rstd"][:, 0:8]))
            vec.wait_ge(mDVE.h, mDVE.n)
            mDVE.inc(vec.tensor_scalar_mul(STATS["nb"][:, 0:8], STATS["nb"][:, 0:8], -1.0))
        blk.vector(_dve2)
        dve_nb = mDVE.n

        def _act3(act):
            act.wait_ge(mDVE.h, dve_nb)
            if apply_sem is not None:
                act.wait_ge(apply_sem.h, apply_vala)
            last = None
            for t in range(4):
                last = act.activation(xna[:, t * 1024:(t + 1) * 1024],
                                      xa[:, t * 1024:(t + 1) * 1024], AF.Identity,
                                      bias=STATS["nb"][:, t:t + 1],
                                      scale=STATS["rstd"][:, t:t + 1])
            mACT.inc(last)
        blk.scalar(_act3)
        a_done = mACT.n

        def _act4(act):
            if apply_sem is not None:
                act.wait_ge(apply_sem.h, apply_valb)
            last = None
            for t in range(4):
                last = act.activation(xnb[:, t * 1024:(t + 1) * 1024],
                                      xb[:, t * 1024:(t + 1) * 1024], AF.Identity,
                                      bias=STATS["nb"][:, 4 + t:5 + t],
                                      scale=STATS["rstd"][:, 4 + t:5 + t])
            mACT.inc(last)
        blk.scalar(_act4)
        return a_done, mACT.n

    with nc.Block("p0") as blk:
        def _g(g):
            for sc in SemC.REGISTRY:
                g.sem_clear(sc.h)
            g.memset(ones64[:, :], 1.0)
            g.memset(ones64b[:, :], 1.0)
            for dst, src in ((ident_sb, ident_in), (s1_sb, s1m), (s2_sb, s2m),
                             (h1s_sb, h1sm), (h2s_sb, h2sm), (w3s_sb, w3sm)):
                g.dma_start(out=dst[:, :], in_=src.ap()).then_inc(sG.h, 16)
                sG.n += 16
            g.wait_ge(sG.h, sG.n)
        blk.gpsimd(_g)

        p0_g = sG.n

        def _cast(act):
            act.wait_ge(sG.h, p0_g)
            act.activation(ident_b[:, :], ident_sb[:, :], AF.Copy)
        blk.scalar(_cast)

    qts = ExitStack()
    qt_sb = qts.enter_context(nc.sbuf_tensor("qt_sb", [128, 8 * 2048], BF16))

    p1 = ExitStack()
    xblk = (p1.enter_context(nc.sbuf_tensor("xblkA", [128, 4 * 1024], F32)),
            p1.enter_context(nc.sbuf_tensor("xblkB", [128, 4 * 1024], F32)))
    xn_tok = (p1.enter_context(nc.sbuf_tensor("xn_tokA", [128, 4 * 1024], BF16)),
              p1.enter_context(nc.sbuf_tensor("xn_tokB", [128, 4 * 1024], BF16)))
    wqkvT_sb = p1.enter_context(nc.sbuf_tensor("wqkvT_sb", [128, 8 * 3072], BF16))

    sub1a = ExitStack()
    xn_ownT = sub1a.enter_context(nc.sbuf_tensor("xn_ownT", [128, 8 * 2048], BF16))

    with nc.Block("p1a") as blk, \
            nc.psum_tensor("tpA", [128, 1024], F32) as tpA, \
            nc.psum_tensor("tpB", [128, 1024], F32) as tpB:
        def _g(g):
            g.dma_start(
                out=bass.AP(wqkvT_sb, 0, [[8 * 3072, 128], [3072, 8], [1, 3072]]),
                in_=bass.AP(wqkvT, 0, [[3072, 128], [128 * 3072, 8], [1, 3072]])
            ).then_inc(sG.h, 16)
            sG.n += 16
        blk.gpsimd(_g)

        tps = (tpA, tpB)
        ld_hist = []
        apply_hist = []
        tpe_base = mPE.n
        tdve_base = mDVE2.n

        def _ld(sy, g_):
            if g_ >= 2:
                sy.wait_ge(mACT.h, apply_hist[g_ - 2])
            sy.dma_start(
                out=bass.AP(xblk[g_ % 2], 0, [[4 * 1024, 128], [1024, 4], [1, 1024]]),
                in_=bass.AP(x_own, g_ * 512 * D, [[1024, 128], [128 * 1024, 4], [1, 1024]])
            ).then_inc(sS.h, 16)
            sS.n += 16

        blk.sync(lambda sy: _ld(sy, 0))
        ld_hist.append(sS.n)
        blk.sync(lambda sy: _ld(sy, 1))
        ld_hist.append(sS.n)

        for pr in range(2):
            ga_, gb_ = 2 * pr, 2 * pr + 1
            ap_pair = emit_ln2(
                blk, xblk[ga_ % 2], xblk[gb_ % 2], xn_tok[ga_ % 2], xn_tok[gb_ % 2],
                sS, ld_hist[ga_], ld_hist[gb_],
                apply_sem=(mPE if ga_ >= 2 else None),
                apply_vala=(tpe_base + 4 * (ga_ - 2) + 4 if ga_ >= 2 else 0),
                apply_valb=(tpe_base + 4 * (gb_ - 2) + 4 if gb_ >= 2 else 0),
                warm=4)
            apply_hist.append(ap_pair[0])
            apply_hist.append(ap_pair[1])
            for g_ in (ga_, gb_):
                apply_at = ap_pair[g_ % 2]
                if debug and g_ == 0:
                    def _gdump(g, ld0=ld_hist[0], ap0=apply_at):
                        g.wait_ge(sS.h, ld0)
                        g.dma_start(out=dbg["xblk0"].ap(), in_=xblk[0][:, :]).then_inc(sG.h, 16)
                        g.wait_ge(mACT.h, ap0)
                        g.dma_start(out=dbg["xntok0"].ap(), in_=xn_tok[0][:, :]).then_inc(sG.h, 16)
                        sG.n += 32
                    blk.gpsimd(_gdump)

                def _tpe(t, g_=g_, apply_at=apply_at):
                    t.wait_ge(mACT.h, apply_at)
                    for tt in range(4):
                        idx = g_ * 4 + tt
                        if idx >= 2:
                            t.wait_ge(mDVE2.h, tdve_base + idx - 1)
                        ps = tps[idx % 2]
                        last = None
                        for fc in range(8):
                            last = t.matmul(ps[:, fc * 128:(fc + 1) * 128],
                                            xn_tok[g_ % 2][:, tt * 1024 + fc * 128:
                                                           tt * 1024 + fc * 128 + 128],
                                            ident_b[:, :], start=True, stop=True)
                        mPE.inc(last)
                blk.tensor(_tpe)
                assert mPE.n == tpe_base + 4 * g_ + 4

                def _tev(v, g_=g_):
                    for tt in range(4):
                        idx = g_ * 4 + tt
                        v.wait_ge(mPE.h, tpe_base + idx + 1)
                        ps = tps[idx % 2]
                        last = None
                        for fc in range(8):
                            last = v.tensor_copy(
                                xn_ownT[:, fc * 2048 + g_ * 512 + tt * 128:
                                        fc * 2048 + g_ * 512 + tt * 128 + 128],
                                ps[:, fc * 128:(fc + 1) * 128])
                        mDVE2.inc(last)
                blk.vector(_tev)
                assert mDVE2.n == tdve_base + 4 * g_ + 4
                if g_ + 2 < 4:
                    blk.sync(lambda sy, g_=g_: _ld(sy, g_ + 2))
                    ld_hist.append(sS.n)

    with nc.Block("p1q") as blk, \
            nc.psum_tensor("qps0", [128, 2048], F32) as qps0, \
            nc.psum_tensor("qps1", [128, 2048], F32) as qps1:
        qps = (qps0, qps1)
        pe_base = mPE.n
        dve_base = mDVE.n
        g_need = sG.n
        xnt_need = mDVE2.n

        def _pe(t):
            t.wait_ge(mDVE2.h, xnt_need)
            t.wait_ge(sG.h, g_need)
            for p in range(NPAIR):
                if p >= 2:
                    t.wait_ge(mDVE.h, dve_base + p - 1)
                ps = qps[p % 2]
                last = None
                for ic in range(8):
                    lhsT = wqkvT_sb[:, ic * 3072 + p * 128: ic * 3072 + p * 128 + 128]
                    for qn in range(4):
                        last = t.matmul(ps[:, qn * 512:(qn + 1) * 512], lhsT,
                                        xn_ownT[:, ic * 2048 + qn * 512:
                                                ic * 2048 + (qn + 1) * 512],
                                        start=(ic == 0), stop=(ic == 7))
                mPE.inc(last)
        blk.tensor(_pe)

        def _dv(v):
            for p in range(NPAIR):
                v.wait_ge(mPE.h, pe_base + p + 1)
                mDVE.inc(v.tensor_copy(qt_sb[:, p * 2048:(p + 1) * 2048], qps[p % 2][:, :]))
        blk.vector(_dv)

        if debug:
            dv_need = mDVE.n

            def _dbg(sy):
                sy.wait_ge(mDVE.h, dv_need)
                sy.dma_start(out=dbg["qt"].ap(), in_=qt_sb[:, :]).then_inc(sS.h, 16)
                sy.dma_start(out=dbg["xnt"].ap(), in_=xn_ownT[:, :]).then_inc(sS.h, 16)
                sS.n += 32
                sy.wait_ge(sS.h, sS.n)
            blk.sync(_dbg)

    sub1a.close()

    sub1b = ExitStack()
    xnT = (sub1b.enter_context(nc.sbuf_tensor("xnT_A", [128, 8 * 1024], BF16)),
           sub1b.enter_context(nc.sbuf_tensor("xnT_B", [128, 8 * 1024], BF16)))
    kv_ev = (sub1b.enter_context(nc.sbuf_tensor("kv_evA", [128, 1024], BF16)),
             sub1b.enter_context(nc.sbuf_tensor("kv_evB", [128, 1024], BF16)))
    KEPT = [[p for p in range(NPAIR) if KB[p] > 8 * bb] for bb in range(8)]

    with nc.Block("p1b") as blk, \
            nc.psum_tensor("ktps", [128, 1024], F32) as ktps, \
            nc.psum_tensor("vtps0", [128, 1024], F32) as vtps0, \
            nc.psum_tensor("vtps1", [128, 1024], F32) as vtps1, \
            nc.psum_tensor("trp0", [128, 512], F32) as trp0, \
            nc.psum_tensor("trp1", [128, 512], F32) as trp1:
        vtps = (vtps0, vtps1)
        trps = (trp0, trp1)
        ld_hist = []
        apply_hist = []
        kev_vals = []
        vev_vals = []
        trgrp_evac = []
        trgrp_pe = []
        trpe_done = []
        trev_done = []
        _n = mDVE2.n
        _p = mPE2.n
        for _bb in range(8):
            for _h in range(2):
                for _ in range(8):
                    _n += 1
                    trgrp_evac.append(_n)
                    _p += 1
                    trgrp_pe.append(_p)
                trev_done.append(_n)
                trpe_done.append(_p)
            for _ in KEPT[_bb]:
                _n += 1
                kev_vals.append(_n)
                _p += 1
            for _ in range(8):
                _n += 1
                vev_vals.append(_n)
                _p += 1
        evac_n = 0
        gk_count = [0]
        gv_count = [0]
        evac_wr = {}
        _sg = sG.n
        _ei = 0
        for _bb in range(8):
            for _ in KEPT[_bb]:
                _sg += 16
                evac_wr[_ei] = _sg
                _ei += 1
            for _ in range(8):
                _sg += 16 * len(KEPT[_bb])
                evac_wr[_ei] = _sg
                _ei += 1

        def _ld(eng, gg):
            if gg >= 2:
                eng.wait_ge(mACT.h, apply_hist[gg - 2])
            sem = sS if gg % 2 == 0 else sA
            eng.dma_start(
                out=bass.AP(xblk[gg % 2], 0, [[4 * 1024, 128], [1024, 4], [1, 1024]]),
                in_=bass.AP(x_full, gg * 512 * D, [[1024, 128], [128 * 1024, 4], [1, 1024]])
            ).then_inc(sem.h, 16)
            sem.n += 16

        blk.sync(lambda sy: _ld(sy, 0))
        ld_hist.append(sS.n)
        blk.scalar(lambda act: _ld(act, 1))
        ld_hist.append(sA.n)

        for bb in range(8):
            ga, gb = 2 * bb, 2 * bb + 1
            ap_pair = emit_ln2(
                blk, xblk[ga % 2], xblk[gb % 2], xn_tok[ga % 2], xn_tok[gb % 2],
                sS, ld_hist[ga], ld_hist[gb],
                apply_sem=(mPE2 if ga >= 2 else None),
                apply_vala=(trpe_done[ga - 2] if ga >= 2 else 0),
                apply_valb=(trpe_done[gb - 2] if gb >= 2 else 0),
                rb_sem=sA)
            apply_hist.append(ap_pair[0])
            apply_hist.append(ap_pair[1])
            for half in range(2):
                gg = 2 * bb + half
                apply_at = ap_pair[half]

                def _trpe(t, gg=gg, apply_at=apply_at):
                    t.wait_ge(mACT.h, apply_at)
                    for grp in range(8):
                        g = gg * 8 + grp
                        if g >= 2:
                            t.wait_ge(mDVE2.h, trgrp_evac[g - 2])
                        tt, fch = grp // 2, (grp % 2) * 4
                        ps = trps[g % 2]
                        last = None
                        for k in range(4):
                            fc = fch + k
                            last = t.matmul(ps[:, k * 128:(k + 1) * 128],
                                            xn_tok[gg % 2][:, tt * 1024 + fc * 128:
                                                           tt * 1024 + fc * 128 + 128],
                                            ident_b[:, :], start=True, stop=True)
                        mPE2.inc(last)
                        assert mPE2.n == trgrp_pe[g]
                blk.tensor(_trpe)

                def _trev(v, gg=gg, bb=bb, half=half):
                    for grp in range(8):
                        g = gg * 8 + grp
                        v.wait_ge(mPE2.h, trgrp_pe[g])
                        tt, fch = grp // 2, (grp % 2) * 4
                        ps = trps[g % 2]
                        last = None
                        for k in range(4):
                            fc = fch + k
                            last = v.tensor_copy(
                                xnT[bb % 2][:, fc * 1024 + half * 512 + tt * 128:
                                            fc * 1024 + half * 512 + tt * 128 + 128],
                                ps[:, k * 128:(k + 1) * 128])
                        mDVE2.inc(last)
                        assert mDVE2.n == trgrp_evac[g]
                blk.vector(_trev)

                if gg + 2 < 16:
                    if (gg + 2) % 2 == 0:
                        blk.sync(lambda sy, gg=gg: _ld(sy, gg + 2))
                        ld_hist.append(sS.n)
                    else:
                        blk.scalar(lambda act, gg=gg: _ld(act, gg + 2))
                        ld_hist.append(sA.n)

            kept = KEPT[bb]
            pmin, npk = kept[0], len(kept)
            tr_need = trev_done[2 * bb + 1]

            pe2_base = mPE2.n
            dve2k_base = mDVE2.n

            nk_prev = gk_count[0]

            def _pek(t, bb=bb, kept=kept, tr_need=tr_need, nk_prev=nk_prev):
                t.wait_ge(mDVE2.h, tr_need)
                for idx, p in enumerate(kept):
                    gk = nk_prev + idx
                    if gk >= 1:
                        t.wait_ge(mDVE2.h, kev_vals[gk - 1])
                    last = None
                    for ic in range(8):
                        lhsT = wqkvT_sb[:, ic * 3072 + 1024 + p * 128:
                                        ic * 3072 + 1024 + p * 128 + 128]
                        for qn in range(2):
                            last = t.matmul(ktps[:, qn * 512:(qn + 1) * 512], lhsT,
                                            xnT[bb % 2][:, ic * 1024 + qn * 512:
                                                        ic * 1024 + (qn + 1) * 512],
                                            start=(ic == 0), stop=(ic == 7))
                    mPE2.inc(last)
            blk.tensor(_pek)

            ev0 = evac_n

            def _dvk(v, kept=kept, pe2_base=pe2_base, ev0=ev0):
                for idx, p in enumerate(kept):
                    v.wait_ge(mPE2.h, pe2_base + idx + 1)
                    ei = ev0 + idx
                    if ei >= 2:
                        v.wait_ge(sG.h, evac_wr[ei - 2])
                    mDVE2.inc(v.tensor_copy(kv_ev[ei % 2][:, :], ktps[:, :]))
            blk.vector(_dvk)

            def _gwk(g, kept=kept, bb=bb, dve2k_base=dve2k_base, ev0=ev0):
                for idx, p in enumerate(kept):
                    g.wait_ge(mDVE2.h, dve2k_base + idx + 1)
                    ei = ev0 + idx
                    g.dma_start(
                        out=bass.AP(kv_scr, (p * 64 + 8 * bb) * 128 * 256,
                                    [[256, 128], [128 * 256, 8], [1, 128]]),
                        in_=bass.AP(kv_ev[ei % 2], 0, [[1024, 128], [128, 8], [1, 128]])
                    ).then_inc(sG.h, 16)
                    sG.n += 16
                    assert evac_wr[ei] == sG.n
            blk.gpsimd(_gwk)
            evac_n += npk
            gk_count[0] += npk

            pe2v_base = mPE2.n
            dve2v_base = mDVE2.n

            nv_prev = gv_count[0]

            def _pev(t, bb=bb, pmin=pmin, npk=npk, nv_prev=nv_prev):
                for tt in range(8):
                    gv = nv_prev + tt
                    if gv >= 2:
                        t.wait_ge(mDVE2.h, vev_vals[gv - 2])
                    ps = vtps[tt % 2]
                    last = None
                    for ic in range(8):
                        lhsT = xnT[bb % 2][:, ic * 1024 + tt * 128: ic * 1024 + tt * 128 + 128]
                        rem = npk * 128
                        o = 0
                        while rem > 0:
                            n = min(512, rem)
                            last = t.matmul(
                                ps[:, pmin * 128 + o: pmin * 128 + o + n], lhsT,
                                wqkvT_sb[:, ic * 3072 + 2048 + pmin * 128 + o:
                                         ic * 3072 + 2048 + pmin * 128 + o + n],
                                start=(ic == 0), stop=(ic == 7))
                            o += n
                            rem -= n
                    mPE2.inc(last)
            blk.tensor(_pev)

            ev0 = evac_n

            def _dvv(v, pe2v_base=pe2v_base, ev0=ev0, pmin=pmin, npk=npk):
                for tt in range(8):
                    v.wait_ge(mPE2.h, pe2v_base + tt + 1)
                    ei = ev0 + tt
                    if ei >= 2:
                        v.wait_ge(sG.h, evac_wr[ei - 2])
                    mDVE2.inc(v.tensor_copy(
                        kv_ev[ei % 2][:, pmin * 128:(pmin + npk) * 128],
                        vtps[tt % 2][:, pmin * 128:(pmin + npk) * 128]))
            blk.vector(_dvv)

            def _gwv(g, kept=kept, bb=bb, dve2v_base=dve2v_base, ev0=ev0):
                for tt in range(8):
                    g.wait_ge(mDVE2.h, dve2v_base + tt + 1)
                    kt = 8 * bb + tt
                    ei = ev0 + tt
                    for p in kept:
                        g.dma_start(
                            out=bass.AP(kv_scr, (p * 64 + kt) * 128 * 256 + 128,
                                        [[256, 128], [1, 128]]),
                            in_=bass.AP(kv_ev[ei % 2], p * 128, [[1024, 128], [1, 128]])
                        ).then_inc(sG.h, 16)
                        sG.n += 16
                    assert evac_wr[ei] == sG.n
            blk.gpsimd(_gwv)
            evac_n += 8
            gv_count[0] += 8

        blk.gpsimd(lambda g: g.wait_ge(sG.h, sG.n))
        if debug:
            def _dbgkv(sy):
                sy.wait_ge(sG.h, sG.n)
                sy.dma_start(out=kv_ev[0][:, 0:256],
                             in_=bass.AP(kv_scr, 0, [[256, 128], [1, 256]])
                             ).then_inc(sS.h, 16)
                sS.n += 16
                sy.wait_ge(sS.h, sS.n)
                sy.dma_start(out=dbg["kv0"].ap(), in_=kv_ev[0][:, 0:256]).then_inc(sS.h, 16)
                sS.n += 16
                sy.wait_ge(sS.h, sS.n)
            blk.sync(_dbgkv)

    sub1b.close()
    p1.close()

    attn2 = ExitStack()
    biasseq_sb = attn2.enter_context(nc.sbuf_tensor("biasseq_sb", [128, 2 * NSLOT], F32))
    kvidx_sb = attn2.enter_context(nc.sbuf_tensor("kvidx_sb", [128, NSLOT], I32))
    woT_sb = attn2.enter_context(nc.sbuf_tensor("woT_sb", [128, 8 * 1024], BF16))
    kvbuf = tuple(attn2.enter_context(nc.sbuf_tensor(f"kvbuf{i}", [128, 256], BF16))
                  for i in range(8))
    pts = tuple(attn2.enter_context(nc.sbuf_tensor(f"pt{i}", [128, 1024], BF16))
                for i in range(8))
    mkbuf = tuple(attn2.enter_context(nc.sbuf_tensor(f"mk{i}", [128, QB], BF16))
                  for i in range(8))
    dbc = attn2.enter_context(nc.sbuf_tensor("dbc", [128, QB], F32))
    u_sb = attn2.enter_context(nc.sbuf_tensor("u_sb", [128, 8 * QB], BF16))
    xo_buf = (attn2.enter_context(nc.sbuf_tensor("xo0", [128, 1024], F32)),
              attn2.enter_context(nc.sbuf_tensor("xo1", [128, 1024], F32)))
    x2m = attn2.enter_context(nc.sbuf_tensor("x2m", [128, 4 * 1024], F32))
    xn2_tokm = attn2.enter_context(nc.sbuf_tensor("xn2_tokm", [128, 4 * 1024], BF16))

    with nc.Block("p2pre") as blk:
        def _g(g):
            for dst, src_ in ((biasseq_sb, biasseq_in), (kvidx_sb, kvidx_in)):
                g.dma_start(out=dst[:, :], in_=src_.ap()).then_inc(sG.h, 16)
                sG.n += 16
            g.dma_start(
                out=bass.AP(woT_sb, 0, [[8 * 1024, 128], [1024, 8], [1, 1024]]),
                in_=bass.AP(woT, 0, [[1024, 128], [128 * 1024, 8], [1, 1024]])
            ).then_inc(sG.h, 16)
            sG.n += 16
            g.wait_ge(sG.h, sG.n)
        blk.gpsimd(_g)

    xn2_tr_done = (0, 0)
    fls_target = {}

    if debug:
        with nc.Block("p2dbgtbl") as blk:
            def _gd(g):
                g.dma_start(out=dbg["qt2"].ap(), in_=qt_sb[:, :]).then_inc(sG.h, 16)
                sG.n += 16
                g.dma_start(out=dbg["tbl"].ap()[:, 0:8], in_=kvidx_sb[:, 0:8]).then_inc(sG.h, 16)
                g.dma_start(out=dbg["tbl"].ap()[:, 8:16], in_=kvidx_sb[:, 0:8]).then_inc(sG.h, 16)
                g.dma_start(out=dbg["bsq"].ap(), in_=biasseq_sb[:, 0:8]).then_inc(sG.h, 16)
                g.dma_start(out=dbg["sps"].ap()[:, 0:8], in_=biasseq_sb[:, 0:8]).then_inc(sG.h, 16)
                sG.n += 64
                g.wait_ge(sG.h, sG.n)
            blk.gpsimd(_gd)

    for m in range(NM):
        with nc.Block(f"p2m{m}") as blk, \
                nc.psum_tensor(f"sps{m}a", [128, 1024], F32) as sps0, \
                nc.psum_tensor(f"sps{m}b", [128, 1024], F32) as sps1, \
                nc.psum_tensor(f"ups{m}a", [128, QB], F32) as u_ps0, \
                nc.psum_tensor(f"ups{m}b", [128, QB], F32) as u_ps1, \
                nc.psum_tensor(f"dps{m}a", [128, QB], F32) as d_ps0, \
                nc.psum_tensor(f"dps{m}b", [128, QB], F32) as d_ps1:
            s_ps = (sps0, sps1)
            u_pst = (u_ps0, u_ps1)
            d_pst = (d_ps0, d_ps1)

            for p in range(NPAIR):
                pslots = [sl for sl in SLOTS if sl[1] == m and sl[2] == p]
                pair_idx = m * NPAIR + p

                def _gkv(g, pslots=pslots, p=p,
                         dkv=None if not (debug and m == 0 and p == 0) else (sS.n + 16)):
                    for (s, _m, _p, t, T_, masked) in pslots:
                        if s >= 8:
                            g.wait_ge(AVD.h, s - 7)
                        if dkv is not None and s == 2:
                            g.wait_ge(sS.h, dkv)
                        g.indirect_dma_start(
                            kvbuf[s % 8][:, :], None,
                            bass.AP(kv_scr, 0, [[256, 128], [1, 256]]),
                            IndirectOffsetOnAxis(kvidx_sb[:, s:s + 1], 0)
                        ).then_inc(KVL.h, 16)
                        KVL.n += 16
                        if p == 0 and t == 1:
                            g.dma_start(out=bass.AP(flush_scr, 0, [[1536, 128], [1, 256]]),
                                        in_=kvbuf[(s - 1) % 8][:, :]).then_inc(FLS.h, 16)
                            g.dma_start(out=bass.AP(flush_scr, 256, [[1536, 128], [1, 256]]),
                                        in_=kvbuf[s % 8][:, :]).then_inc(FLS.h, 16)
                            FLS.n += 32
                blk.gpsimd(_gkv)

                def _gmsk(sy, pslots=pslots):
                    for (s, _m, _p, t, T_, masked) in pslots:
                        if masked:
                            c = MSK_CNT[s]
                            if c >= 9:
                                sy.wait_ge(MSK.h, c - 8)
                            sy.dma_start(out=mkbuf[c % 8][:, :],
                                         in_=maskseq_in.ap()[c - 1, :, :]
                                         ).then_inc(MKL.h, 16)
                            MKL.n += 16
                blk.sync(_gmsk)
                if p == 0:
                    fls_target[m] = FLS.n

                def _pe(te, pslots=pslots, p=p, m=m, pair_idx=pair_idx):
                    qtp = qt_sb[:, p * 2048 + m * QB: p * 2048 + m * QB + QB]

                    u_ps = u_pst[pair_idx % 2]
                    d_ps = d_pst[pair_idx % 2]

                    def emit_avd(sl):
                        (s, _m, _p, t, T_, masked) = sl
                        if t == 0 and pair_idx >= 2:
                            te.wait_ge(UEV.h, pair_idx - 1)
                        if masked:
                            te.wait_ge(MSK.h, MSK_CNT[s])
                        else:
                            te.wait_ge(EXD.h, s + 1)
                        kb = kvbuf[s % 8]
                        pt = pts[s % 8]
                        st_, sp_ = (t == 0), (t == T_ - 1)
                        te.matmul(u_ps[0:64, :], kb[:, 128:192], pt[:, 0:QB],
                                  start=st_, stop=sp_, tile_position=(0, 0))
                        te.matmul(u_ps[64:128, :], kb[:, 192:256], pt[:, QB:2 * QB],
                                  start=st_, stop=sp_, tile_position=(0, 64))
                        te.matmul(d_ps[0:64, :], ones64b[:, :], pt[:, 0:QB],
                                  start=st_, stop=sp_, tile_position=(0, 0))
                        AVD.inc(te.matmul(d_ps[64:128, :], ones64b[:, :], pt[:, QB:2 * QB],
                                          start=st_, stop=sp_, tile_position=(0, 64)))

                    for idx, sl in enumerate(pslots):
                        (s, _m, _p, t, T_, masked) = sl
                        te.wait_ge(KVL.h, 16 * (s + 1))
                        if idx <= 1 and p == 0:
                            te.wait_ge(FLS.h, fls_target[m])
                        if s >= 2:
                            te.wait_ge(EXD.h, s - 1)
                        kb = kvbuf[s % 8]
                        te.matmul(s_ps[s % 2][:, 0:QB], kb[0:64, 0:128], qtp[0:64, :],
                                  start=True, stop=True, tile_position=(0, 0))
                        SMM.inc(te.matmul(s_ps[s % 2][:, QB:2 * QB], kb[64:128, 0:128],
                                          qtp[64:128, :],
                                          start=True, stop=True, tile_position=(64, 0)))
                        if idx >= 1:
                            emit_avd(pslots[idx - 1])
                    emit_avd(pslots[-1])
                blk.tensor(_pe)

                dump0_val = (sS.n + 32) if (debug and m == 0 and p == 0) else None

                def _act(act, pslots=pslots, p=p, dump0_val=dump0_val):
                    for i_, (s, _m, _p, t, T_, masked) in enumerate(pslots):
                        act.wait_ge(SMM.h, s + 1 + (1 if (i_ == 0 and p == 0) else 0))
                        if dump0_val is not None and s == 2:
                            act.wait_ge(sS.h, dump0_val)
                        if s >= 8:
                            act.wait_ge(AVD.h, s - 7)
                        pt = pts[s % 8]
                        act.activation(pt[:, 0:QB], s_ps[s % 2][:, 0:QB], AF.Exp,
                                       bias=biasseq_sb[:, 2 * s:2 * s + 1], scale=0.125)
                        EXD.inc(act.activation(pt[:, QB:2 * QB], s_ps[s % 2][:, QB:2 * QB],
                                               AF.Exp,
                                               bias=biasseq_sb[:, 2 * s + 1:2 * s + 2],
                                               scale=0.125))
                blk.scalar(_act)

                def _dve(v, pslots=pslots, p=p, pair_idx=pair_idx, m=m):
                    for (s, _m, _p, t, T_, masked) in pslots:
                        if masked:
                            c = MSK_CNT[s]
                            v.wait_ge(EXD.h, s + 1)
                            v.wait_ge(MKL.h, 16 * min(c + 1, MSK_MAX_OF_MP[(_m, _p)]))
                            pt = pts[s % 8]
                            v.tensor_mul(pt[:, 0:QB], pt[:, 0:QB], mkbuf[c % 8][:, :])
                            MSK.inc(v.tensor_mul(pt[:, QB:2 * QB], pt[:, QB:2 * QB],
                                                 mkbuf[c % 8][:, :]))
                    if debug and m == 0 and p == 1:
                        v.wait_ge(sS.h, sS.n)
                    v.wait_ge(AVD.h, pslots[-1][0] + 1)
                    mDVE.inc(v.tensor_copy(dbc[:, :], d_pst[pair_idx % 2][:, :]))
                    v.wait_ge(mDVE.h, mDVE.n)
                    if debug and m == 0 and p == 0:
                        mDVE.inc(v.tensor_copy(st_scr2[:, 0:QB], u_pst[pair_idx % 2][:, :]))
                        v.wait_ge(mDVE.h, mDVE.n)
                    mDVE.inc(v.reciprocal(dbc[:, :], dbc[:, :]))
                    v.wait_ge(mDVE.h, mDVE.n)
                    UEV.inc(v.tensor_mul(u_sb[:, p * QB:(p + 1) * QB],
                                         u_pst[pair_idx % 2][:, :], dbc[:, :]))
                blk.vector(_dve)

                if debug and m == 0 and p == 0:
                    def _dumpkv(sy):
                        sy.wait_ge(KVL.h, 16)
                        sy.dma_start(out=dbg["kv0"].ap(),
                                     in_=kvbuf[0][:, :]).then_inc(sS.h, 16)
                        sS.n += 16
                    blk.sync(_dumpkv)

                    def _dump0(sy):
                        sy.wait_ge(MSK.h, 1)
                        sy.dma_start(out=dbg["pt"].ap()[0, :, :],
                                     in_=pts[0][:, :]).then_inc(sS.h, 16)
                        sS.n += 16
                        assert sS.n == dump0_val
                    blk.sync(_dump0)

                if debug and m == 0 and p == 0 and False:
                    def _dumpp(sy):
                        for si in range(3):
                            sy.wait_ge(MSK.h, si + 1)
                            sy.dma_start(out=dbg["pt"].ap()[si, :, :],
                                         in_=pts[si % 2][:, :]).then_inc(sS.h, 16)
                            sy.dma_start(out=dbg["mk"].ap()[si, :, :],
                                         in_=mkbuf[(si + 1) % 2][:, :]).then_inc(sS.h, 16)
                            sS.n += 32
                    blk.sync(_dumpp)
                    ptdump_val = sS.n


                if debug and m == 0 and p == 0 and False:
                    uev_dump = UEV.n
                    dve_dump = mDVE.n

                    def _dump(sy):
                        sy.wait_ge(UEV.h, uev_dump)
                        sy.wait_ge(mDVE.h, dve_dump)
                        sy.dma_start(out=dbg["dbc"].ap(), in_=dbc[:, :]).then_inc(sS.h, 16)
                        sy.dma_start(out=dbg["upr"].ap(), in_=st_scr2[:, 0:QB]).then_inc(sS.h, 16)
                        sS.n += 32
                    blk.sync(_dump)

        with nc.Block(f"p2wo{m}") as blk, \
                nc.psum_tensor(f"atps0_{m}", [128, 1024], F32) as atps0, \
                nc.psum_tensor(f"atps1_{m}", [128, 1024], F32) as atps1:
            atps = (atps0, atps1)
            uev_need = UEV.n
            x2d_base = X2D.n
            wom_base = WOM.n
            sS_base = sS.n
            NTT = QB // 128

            def _syxo(sy, m=m):
                for tt in range(NTT):
                    if tt >= 2:
                        sy.wait_ge(X2D.h, x2d_base + tt - 1)
                    sy.dma_start(out=xo_buf[tt % 2][:, :],
                                 in_=bass.AP(x_own, (m * QB + tt * 128) * D,
                                             [[1024, 128], [1, 1024]])
                                 ).then_inc(sS.h, 16)
                    sS.n += 16
            blk.sync(_syxo)

            def _pewo(te, m=m):
                te.wait_ge(UEV.h, uev_need)
                for tt in range(NTT):
                    if tt >= 2:
                        te.wait_ge(X2D.h, x2d_base + tt - 1)
                    ps = atps[tt % 2]
                    last = None
                    for p in range(NPAIR):
                        lhsT = u_sb[:, p * QB + tt * 128: p * QB + tt * 128 + 128]
                        for on in range(2):
                            last = te.matmul(ps[:, on * 512:(on + 1) * 512], lhsT,
                                             woT_sb[:, p * 1024 + on * 512:
                                                    p * 1024 + (on + 1) * 512],
                                             start=(p == 0), stop=(p == NPAIR - 1))
                    WOM.inc(last)
            blk.tensor(_pewo)

            def _dvx2(v):
                for tt in range(NTT):
                    v.wait_ge(WOM.h, wom_base + tt + 1)
                    v.wait_ge(sS.h, sS_base + 16 * (tt + 1))
                    X2D.inc(v.tensor_add(x2m[:, tt * 1024:(tt + 1) * 1024],
                                         atps[tt % 2][:, :], xo_buf[tt % 2][:, :]))
            blk.vector(_dvx2)

            def _syx2(sy, m=m):
                for tt in range(NTT):
                    sy.wait_ge(X2D.h, x2d_base + tt + 1)
                    sy.dma_start(out=bass.AP(x2_scr, (m * QB + tt * 128) * D,
                                             [[1024, 128], [1, 1024]]),
                                 in_=x2m[:, tt * 1024:(tt + 1) * 1024]
                                 ).then_inc(sS.h, 16)
                    sS.n += 16
            blk.sync(_syx2)

            apply_at = emit_ln(blk, NTT, x2m, xn2_tokm, X2D, x2d_base + NTT)

            trw_base = mPE2.n
            trd_base = mDVE2.n

            def _trxpe(t, m=m, apply_at=apply_at):
                t.wait_ge(mACT.h, apply_at)
                for tt in range(NTT):
                    if tt >= 2:
                        t.wait_ge(mDVE2.h, trd_base + tt - 1)
                    ps = atps[tt % 2]
                    last = None
                    for fc in range(8):
                        last = t.matmul(ps[:, fc * 128:(fc + 1) * 128],
                                        xn2_tokm[:, tt * 1024 + fc * 128:
                                                 tt * 1024 + fc * 128 + 128],
                                        ident_b[:, :], start=True, stop=True)
                    mPE2.inc(last)
            blk.tensor(_trxpe)

            def _trxev(v, m=m):
                for tt in range(NTT):
                    v.wait_ge(mPE2.h, trw_base + tt + 1)
                    ps = atps[tt % 2]
                    last = None
                    for fc in range(8):
                        last = v.tensor_copy(
                            xn2T_sb[:, fc * 2048 + m * QB + tt * 128:
                                    fc * 2048 + m * QB + tt * 128 + 128],
                            ps[:, fc * 128:(fc + 1) * 128])
                    mDVE2.inc(last)
            blk.vector(_trxev)

            if debug:
                def _dbgm(sy, m=m):
                    sy.dma_start(out=dbg["u"].ap()[m, :, :], in_=u_sb[:, :]).then_inc(sS.h, 16)
                    sy.dma_start(out=dbg["x2"].ap()[m, :, :], in_=x2m[:, :]).then_inc(sS.h, 16)
                    sS.n += 32
                    sy.wait_ge(sS.h, sS.n)
                blk.sync(_dbgm)

    attn2.close()
    qts.close()


    ffnA = ExitStack()
    a_sb = ffnA.enter_context(nc.sbuf_tensor("a_sb", [128, 32 * 2048], BF16))

    with nc.Block("p3a") as blk, \
            nc.sbuf_tensor("w1b0", [128, 1024], BF16) as w1b0, \
            nc.sbuf_tensor("w1b1", [128, 1024], BF16) as w1b1, \
            nc.sbuf_tensor("w2b0", [128, 1024], BF16) as w2b0, \
            nc.sbuf_tensor("w2b1", [128, 1024], BF16) as w2b1, \
            nc.sbuf_tensor("t1_sb", [128, 2048], BF16) as t1_sb, \
            nc.psum_tensor("aps1", [128, 2048], F32) as aps1, \
            nc.psum_tensor("aps2", [128, 2048], F32) as aps2:
        w1b = (w1b0, w1b1)
        w2b = (w2b0, w2b1)
        g_base = sG.n
        fmm_base = FMM.n
        ts1_base = TS1.n
        adn_base = ADN.n

        def _g(g):
            for oc in range(32):
                if oc >= 2:
                    g.wait_ge(FMM.h, fmm_base + 2 * (oc - 2) + 2)
                for blk_in, dst in ((w1blk, w1b[oc % 2]), (w2blk, w2b[oc % 2])):
                    g.dma_start(
                        out=bass.AP(dst, 0, [[1024, 128], [128, 8], [1, 128]]),
                        in_=bass.AP(blk_in, oc * D * 128,
                                    [[128, 128], [128 * 128, 8], [1, 128]])
                    ).then_inc(sG.h, 16)
                    sG.n += 16
        blk.gpsimd(_g)

        def _pe(te):
            for oc in range(32):
                te.wait_ge(sG.h, g_base + 32 * (oc + 1))
                if oc >= 1:
                    te.wait_ge(TS1.h, ts1_base + oc)
                    te.wait_ge(ADN.h, adn_base + oc)
                last = None
                for ic in range(8):
                    lhsT = w1b[oc % 2][:, ic * 128:(ic + 1) * 128]
                    for tb in range(4):
                        last = te.matmul(aps1[:, tb * 512:(tb + 1) * 512], lhsT,
                                         xn2T_sb[:, ic * 2048 + tb * 512:
                                                 ic * 2048 + (tb + 1) * 512],
                                         start=(ic == 0), stop=(ic == 7))
                FMM.inc(last)
                last = None
                for ic in range(8):
                    lhsT = w2b[oc % 2][:, ic * 128:(ic + 1) * 128]
                    for tb in range(4):
                        last = te.matmul(aps2[:, tb * 512:(tb + 1) * 512], lhsT,
                                         xn2T_sb[:, ic * 2048 + tb * 512:
                                                 ic * 2048 + (tb + 1) * 512],
                                         start=(ic == 0), stop=(ic == 7))
                FMM.inc(last)
        blk.tensor(_pe)

        def _act(act):
            for oc in range(32):
                act.wait_ge(FMM.h, fmm_base + 2 * oc + 1)
                if oc >= 1:
                    act.wait_ge(ADN.h, adn_base + oc)
                TS1.inc(act.activation(t1_sb[:, :], aps1[:, :], AF.Silu,
                                       scale=s1_sb[:, oc:oc + 1]))
        blk.scalar(_act)

        def _dv(v):
            for oc in range(32):
                v.wait_ge(FMM.h, fmm_base + 2 * oc + 2)
                v.wait_ge(TS1.h, ts1_base + oc + 1)
                ADN.inc(v.scalar_tensor_tensor(a_sb[:, oc * 2048:(oc + 1) * 2048],
                                               aps2[:, :], s2_sb[:, oc:oc + 1],
                                               t1_sb[:, :], ALU.mult, ALU.mult))
        blk.vector(_dv)

        if debug:
            adn_need = ADN.n

            def _dbga(sy):
                sy.wait_ge(ADN.h, adn_need)
                sy.dma_start(out=dbg["a"].ap(), in_=a_sb[:, :]).then_inc(sS.h, 16)
                sS.n += 16
                sy.wait_ge(sS.h, sS.n)
            blk.sync(_dbga)

    with nc.Block("p3b") as blk, \
            nc.sbuf_tensor("h1b0", [128, 4096], BF16) as h1b0, \
            nc.sbuf_tensor("h1b1", [128, 4096], BF16) as h1b1, \
            nc.sbuf_tensor("bev0", [128, 2048], BF16) as bev0, \
            nc.sbuf_tensor("bev1", [128, 2048], BF16) as bev1, \
            nc.psum_tensor("bps0", [128, 2048], F32) as bps0, \
            nc.psum_tensor("bps1", [128, 2048], F32) as bps1:
        h1b = (h1b0, h1b1)
        bev = (bev0, bev1)
        bps = (bps0, bps1)
        g_base = sG.n
        fmm_base = FMM.n
        ts1_base = TS1.n
        s_base = sS.n

        def _g(g):
            for oc in range(32):
                if oc >= 2:
                    g.wait_ge(FMM.h, fmm_base + oc - 1)
                g.dma_start(
                    out=bass.AP(h1b[oc % 2], 0, [[4096, 128], [128, 32], [1, 128]]),
                    in_=bass.AP(h1blk, oc * HID * 128,
                                [[128, 128], [128 * 128, 32], [1, 128]])
                ).then_inc(sG.h, 16)
                sG.n += 16
        blk.gpsimd(_g)

        def _pe(te):
            for oc in range(32):
                te.wait_ge(sG.h, g_base + 16 * (oc + 1))
                if oc >= 2:
                    te.wait_ge(TS1.h, ts1_base + oc - 1)
                last = None
                for ic in range(32):
                    lhsT = h1b[oc % 2][:, ic * 128:(ic + 1) * 128]
                    for tb in range(4):
                        last = te.matmul(bps[oc % 2][:, tb * 512:(tb + 1) * 512], lhsT,
                                         a_sb[:, ic * 2048 + tb * 512:
                                              ic * 2048 + (tb + 1) * 512],
                                         start=(ic == 0), stop=(ic == 31))
                FMM.inc(last)
        blk.tensor(_pe)

        def _act(act):
            for oc in range(32):
                act.wait_ge(FMM.h, fmm_base + oc + 1)
                if oc >= 2:
                    act.wait_ge(sS.h, s_base + 16 * (oc - 1))
                TS1.inc(act.activation(bev[oc % 2][:, :], bps[oc % 2][:, :], AF.Gelu,
                                       scale=h1s_sb[:, oc:oc + 1]))
        blk.scalar(_act)

        def _sy(sy):
            for oc in range(32):
                sy.wait_ge(TS1.h, ts1_base + oc + 1)
                sy.dma_start(out=bass.AP(b_scr, oc * 128 * 2048, [[2048, 128], [1, 2048]]),
                             in_=bev[oc % 2][:, :]).then_inc(sS.h, 16)
                sS.n += 16
        blk.sync(_sy)
        bscr_q = [s_base + 16 * 8 * (q + 1) for q in range(4)]

    ffnA.close()


    p4 = ExitStack()
    c_sb = p4.enter_context(nc.sbuf_tensor("c_sb", [128, 32 * 1024], BF16))

    for th in range(2):
        with nc.Block(f"p4c{th}") as blk, \
                nc.sbuf_tensor(f"bb_all{th}", [128, 32 * 1024], BF16) as bb_all, \
                nc.sbuf_tensor(f"h2b0{th}", [128, 4096], BF16) as h2b0, \
                nc.sbuf_tensor(f"h2b1{th}", [128, 4096], BF16) as h2b1, \
                nc.psum_tensor(f"cps0{th}", [128, 1024], F32) as cps0, \
                nc.psum_tensor(f"cps1{th}", [128, 1024], F32) as cps1:
            h2b = (h2b0, h2b1)
            cps = (cps0, cps1)
            g_base = sG.n
            bld_base = BLD.n
            cmm_base = CMM.n
            cgl_base = CGL.n

            def _syb(sy, th=th):
                for q in range(4):
                    sy.wait_ge(sS.h, bscr_q[q])
                    sy.dma_start(
                        out=bass.AP(bb_all, q * 8 * 1024,
                                    [[32 * 1024, 128], [1024, 8], [1, 1024]]),
                        in_=bass.AP(b_scr, th * 1024 + q * 8 * 128 * 2048,
                                    [[2048, 128], [128 * 2048, 8], [1, 1024]])
                    ).then_inc(BLD.h, 16)
                    BLD.n += 16
            blk.sync(_syb)

            def _g(g, th=th):
                for oc in range(32):
                    if oc >= 2:
                        g.wait_ge(CMM.h, cmm_base + oc - 1)
                    g.dma_start(
                        out=bass.AP(h2b[oc % 2], 0, [[4096, 128], [128, 32], [1, 128]]),
                        in_=bass.AP(h2blk, oc * HID * 128,
                                    [[128, 128], [128 * 128, 32], [1, 128]])
                    ).then_inc(sG.h, 16)
                    sG.n += 16
            blk.gpsimd(_g)

            def _pe(te, th=th):
                for oc in range(32):
                    te.wait_ge(sG.h, g_base + 16 * (oc + 1))
                    if oc >= 2:
                        te.wait_ge(CGL.h, cgl_base + oc - 1)
                    if oc == 1:
                        te.wait_ge(BLD.h, bld_base + 64)
                    last = None
                    for ic in range(32):
                        if oc == 0 and ic % 8 == 0:
                            te.wait_ge(BLD.h, bld_base + 16 * (ic // 8 + 1))
                        lhsT = h2b[oc % 2][:, ic * 128:(ic + 1) * 128]
                        for tb in range(2):
                            last = te.matmul(cps[oc % 2][:, tb * 512:(tb + 1) * 512],
                                             lhsT,
                                             bb_all[:, ic * 1024 + tb * 512:
                                                    ic * 1024 + (tb + 1) * 512],
                                             start=(ic == 0), stop=(ic == 31))
                    CMM.inc(last)
            blk.tensor(_pe)

            def _act(act, th=th):
                for oc in range(32):
                    act.wait_ge(CMM.h, cmm_base + oc + 1)
                    CGL.inc(act.activation(c_sb[:, oc * 1024:(oc + 1) * 1024],
                                           cps[oc % 2][:, :], AF.Gelu,
                                           scale=h2s_sb[:, oc:oc + 1]))
            blk.scalar(_act)

        with nc.Block(f"p4d{th}") as blk, \
                nc.sbuf_tensor(f"w3b0{th}", [128, 4096], BF16) as w3b0, \
                nc.sbuf_tensor(f"w3b1{th}", [128, 4096], BF16) as w3b1, \
                nc.sbuf_tensor(f"yt{th}", [128, 1024], F32) as yt_sb, \
                nc.sbuf_tensor(f"x2th{th}", [128, 8 * 1024], F32) as x2th, \
                nc.sbuf_tensor(f"ot0{th}", [128, 128], F32) as ot0, \
                nc.sbuf_tensor(f"ot1{th}", [128, 128], F32) as ot1, \
                nc.psum_tensor(f"d4ps{th}", [128, 1024], F32) as dps0, \
                nc.psum_tensor(f"trp0{th}", [128, 128], F32) as trp0, \
                nc.psum_tensor(f"trp1{th}", [128, 128], F32) as trp1:
            w3b = (w3b0, w3b1)
            ot = (ot0, ot1)
            trp = (trp0, trp1)
            g_base = sG.n
            dm_base = DM2.n
            yev_base = YEV.n
            trd_base = TRD.n
            oad_base = OAD.n
            x2l_base = X2L.n
            odm_base = ODM.n

            def _g(g, th=th):
                for oc in range(8):
                    if oc >= 2:
                        g.wait_ge(DM2.h, dm_base + oc - 1)
                    g.dma_start(
                        out=bass.AP(w3b[oc % 2], 0, [[4096, 128], [128, 32], [1, 128]]),
                        in_=bass.AP(w3blk, oc * HID * 128,
                                    [[128, 128], [128 * 128, 32], [1, 128]])
                    ).then_inc(sG.h, 16)
                    sG.n += 16
            blk.gpsimd(_g)

            def _pe(te, th=th):
                for oc in range(8):
                    te.wait_ge(sG.h, g_base + 16 * (oc + 1))
                    if oc >= 1:
                        te.wait_ge(YEV.h, yev_base + oc)
                    last = None
                    for ic in range(32):
                        lhsT = w3b[oc % 2][:, ic * 128:(ic + 1) * 128]
                        for tb in range(2):
                            last = te.matmul(dps0[:, tb * 512:(tb + 1) * 512], lhsT,
                                             c_sb[:, ic * 1024 + tb * 512:
                                                  ic * 1024 + (tb + 1) * 512],
                                             start=(ic == 0), stop=(ic == 31))
                    DM2.inc(last)
                    te.wait_ge(YEV.h, yev_base + oc + 1)
                    for tt in range(8):
                        k = oc * 8 + tt
                        if k >= 2:
                            te.wait_ge(OAD.h, oad_base + k - 1)
                        TRD.inc(te.transpose(trp[k % 2][:, :],
                                             yt_sb[:, tt * 128:(tt + 1) * 128],
                                             ident_sb[:, :]))
            blk.tensor(_pe)

            def _dv(v, th=th):
                v.wait_ge(X2L.h, x2l_base + 16)
                for oc in range(8):
                    v.wait_ge(DM2.h, dm_base + oc + 1)
                    YEV.inc(v.tensor_scalar_mul(yt_sb[:, :], dps0[:, :],
                                                w3s_sb[:, oc:oc + 1]))
                    for tt in range(8):
                        k = oc * 8 + tt
                        v.wait_ge(TRD.h, trd_base + k + 1)
                        if k >= 2:
                            v.wait_ge(ODM.h, odm_base + 16 * (k - 1))
                        OAD.inc(v.tensor_add(ot[k % 2][:, :], trp[k % 2][:, :],
                                             x2th[:, tt * 1024 + oc * 128:
                                                  tt * 1024 + oc * 128 + 128]))
            blk.vector(_dv)

            def _syio(sy, th=th):
                sy.dma_start(
                    out=bass.AP(x2th, 0, [[8 * 1024, 128], [1024, 8], [1, 1024]]),
                    in_=bass.AP(x2_scr, th * 1024 * D,
                                [[1024, 128], [128 * 1024, 8], [1, 1024]])
                ).then_inc(X2L.h, 16)
                X2L.n += 16
                for k in range(64):
                    oc, tt = divmod(k, 8)
                    sy.wait_ge(OAD.h, oad_base + k + 1)
                    sy.dma_start(out=bass.AP(out, (th * 1024 + tt * 128) * D + oc * 128,
                                             [[1024, 128], [1, 128]]),
                                 in_=ot[k % 2][:, :]).then_inc(ODM.h, 16)
                    ODM.n += 16
            blk.sync(_syio)

    with nc.Block("pfin") as blk:
        blk.sync(lambda sy: sy.wait_ge(ODM.h, ODM.n))

    p4.close()
    es.close()
    return nc


_NC_CACHE = {}


def _get_nc(debug=False):
    key = bool(debug)
    if key not in _NC_CACHE:
        _NC_CACHE[key] = build_nc(debug=debug)
    return _NC_CACHE[key]


def _prep_inputs(inputs):
    x = np.asarray(inputs["L0_attn_x"], dtype=np.float32)
    wqkv = np.asarray(inputs["L0_attn_W_qkv"], dtype=np.float32)
    wo = np.asarray(inputs["L0_attn_W_o"], dtype=np.float32)
    w1 = np.asarray(inputs["L0_ffn_w1_w"], dtype=np.float32)
    s1 = np.asarray(inputs["L0_ffn_w1_s"], dtype=np.float32)
    w2 = np.asarray(inputs["L0_ffn_w2_w"], dtype=np.float32)
    s2 = np.asarray(inputs["L0_ffn_w2_s"], dtype=np.float32)
    h1 = np.asarray(inputs["L0_ffn_h1_w"], dtype=np.float32)
    h1s = np.asarray(inputs["L0_ffn_h1_s"], dtype=np.float32)
    h2 = np.asarray(inputs["L0_ffn_h2_w"], dtype=np.float32)
    h2s = np.asarray(inputs["L0_ffn_h2_s"], dtype=np.float32)
    w3 = np.asarray(inputs["L0_ffn_w3_w"], dtype=np.float32)
    w3s = np.asarray(inputs["L0_ffn_w3_s"], dtype=np.float32)

    def cont(a):
        return np.ascontiguousarray(a)

    shared = {
        "wqkvT": cont(wqkv.T),
        "woT": cont(wo.T),
        "w1blk": cont(w1.T.reshape(D, 32, 128).transpose(1, 0, 2)),
        "w2blk": cont(w2.T.reshape(D, 32, 128).transpose(1, 0, 2)),
        "h1blk": cont(h1.T.reshape(HID, 32, 128).transpose(1, 0, 2)),
        "h2blk": cont(h2.T.reshape(HID, 32, 128).transpose(1, 0, 2)),
        "w3blk": cont(w3.T.reshape(HID, 8, 128).transpose(1, 0, 2)),
        "s1m": cont(s1.reshape(32, 128).T),
        "s2m": cont(s2.reshape(32, 128).T),
        "h1sm": cont(h1s.reshape(32, 128).T),
        "h2sm": cont(h2s.reshape(32, 128).T),
        "w3sm": cont(w3s.reshape(8, 128).T),
        "ident": np.eye(128, dtype=np.float32),
    }
    in_maps = []
    for c in range(NCORES):
        b, j = c // 4, c % 4
        kvidx, midx, biasseq = _host_tables(j)
        m = dict(shared)
        m["x_full"] = cont(x[b])
        m["x_own"] = cont(np.concatenate(
            [x[b, QB * (4 * mm + j):QB * (4 * mm + j) + QB] for mm in range(NM)], axis=0))
        m["biasseq"] = cont(biasseq)
        m["kvidx"] = kvidx
        m["midx"] = midx
        m["maskseq"] = _host_maskseq(j)
        in_maps.append(m)
    return in_maps


def kernel(**inputs):
    nc = _get_nc(debug=False)
    in_maps = _prep_inputs(inputs)
    res = run_bass_kernel_spmd(nc, in_maps, core_ids=list(range(NCORES)))
    final = np.empty((B, S, D), dtype=np.float32)
    for c in range(NCORES):
        b, j = c // 4, c % 4
        o = res.results[c]["out"]
        for mm in range(NM):
            final[b, QB * (4 * mm + j):QB * (4 * mm + j) + QB] = o[QB * mm:QB * (mm + 1)]
    return final
